# version 09
# speedup vs baseline: 1.4665x; 1.0958x over previous
from contextlib import ExitStack
import numpy as np
import concourse.bass as bass
import concourse.mybir as mybir
from concourse.bass_utils import run_bass_kernel_spmd

F32 = mybir.dt.float32
BF16 = mybir.dt.bfloat16
I32 = mybir.dt.int32
AF = mybir.ActivationFunctionType
ALU = mybir.AluOpType
AX = mybir.AxisListType

NCORES = 8
B_PER = 2
S = 2048
T = B_PER * S
D = 1024
NC_D = D // 128
TT = 128
NTT = T // TT

C32 = 12582912.0
C64 = 6755399441055744.0
LN_EPS = 1e-5


class Sem:
    def __init__(self, h):
        self.h = h
        self.n = 0

    def inc(self, instr, k=1):
        instr.then_inc(self.h, k)
        self.n += k
        return self.n


class K:

    def __init__(self, name="k"):
        self.nc = bass.Bass("TRN2", target_bir_lowering=False)
        self.es = ExitStack()
        self.name = name
        self._nbank = 0
        self.V = self.sem("semV")
        self.A = self.sem("semA")
        self.P = self.sem("semP")
        self.G = self.sem("semG")
        self.SY = self.sem("semSY")
        self.GQ = self.sem("semGQ")

    def sb(self, name, shape, dt=F32):
        return self.es.enter_context(self.nc.sbuf_tensor(name, shape, dt))

    def bank(self, name):
        self._nbank += 1
        assert self._nbank <= 8, "out of PSUM banks"
        return self.es.enter_context(self.nc.psum_tensor(name, [128, 512], F32))

    def bankb(self, name):
        self._nbank += 1
        assert self._nbank <= 8, "out of PSUM banks"
        return self.es.enter_context(self.nc.psum_tensor(name, [128, 1024], BF16))

    def sem(self, name):
        return Sem(self.es.enter_context(self.nc.semaphore(name)))

    def din(self, name, shape, dt=F32):
        return self.nc.dram_tensor(name, shape, dt, kind="ExternalInput").ap()

    def dout(self, name, shape, dt=F32):
        return self.nc.dram_tensor(name, shape, dt, kind="ExternalOutput").ap()

    def close(self):
        self.es.close()


def wV(e, k, n=None):
    e.wait_ge(k.V.h, k.V.n if n is None else n)


def wA(e, k, n=None):
    e.wait_ge(k.A.h, k.A.n if n is None else n)


def wP(e, k, n=None):
    e.wait_ge(k.P.h, k.P.n if n is None else n)


def wG(e, k, n=None):
    e.wait_ge(k.G.h, k.G.n if n is None else n)


def wSY(e, k, n=None):
    e.wait_ge(k.SY.h, k.SY.n if n is None else n)


def wGQ(e, k, n=None):
    e.wait_ge(k.GQ.h, k.GQ.n if n is None else n)


def wT(e, k, t):
    if t:
        kind, n = t
        e.wait_ge(getattr(k, kind).h, n)



def setup_consts(k, b, need_identf=False):
    ident = k.sb("c_ident", [128, 128], BF16)
    identf = k.sb("c_identf", [128, 128], F32) if need_identf else None
    iot = k.sb("c_iota", [128, 128], I32)
    ceff = k.sb("c_ceff", [128, 1], F32)
    epsc = k.sb("c_eps", [128, 1], F32)
    ones_rf = k.sb("c_ones_rf", [1, 128], F32)
    onescol = k.sb("c_onescol", [128, 1], BF16)
    onescolf = k.sb("c_onescolf", [128, 1], F32)

    @b.gpsimd
    def _(e):
        k.G.inc(e.iota(iot[:], [[1, 128]], base=0, channel_multiplier=-1))
    g_iota = k.G.n

    @b.vector
    def _(e):
        wG(e, k, g_iota)
        k.V.inc(e.tensor_scalar(ident[:], iot[:], 0, None, ALU.is_equal))
        if need_identf:
            k.V.inc(e.tensor_scalar(identf[:], iot[:], 0, None, ALU.is_equal))
        k.V.inc(e.memset(epsc[:], LN_EPS))
        k.V.inc(e.memset(ones_rf[:], 1.0))
        k.V.inc(e.memset(onescol[:], 1.0))
        k.V.inc(e.memset(onescolf[:], 1.0))
        k.V.inc(e.memset(ceff[:], 1.5))
        wV(e, k)
        k.V.inc(e.tensor_scalar(ceff[:], ceff[:], C32, C32, ALU.add,
                                ALU.subtract))
        wV(e, k)
        k.V.inc(e.tensor_scalar(ceff[:], ceff[:], -1.0, 2.0, ALU.mult,
                                ALU.add))
        wV(e, k)
        k.V.inc(e.tensor_scalar(ceff[:], ceff[:], 2.0 * (C64 - C32), C32,
                                ALU.mult, ALU.add))

    return dict(ident=ident, identf=identf, iot=iot, ceff=ceff, epsc=epsc,
                ones_rf=ones_rf, onescol=onescol, onescolf=onescolf,
                v_done=k.V.n)


def emit_ln(k, b, i, xt, st, mv, rstd, cst, dep_wait):

    @b.vector
    def _(e):
        dep_wait(e)
        k.V.inc(e.bn_stats(st[:, 0:6], xt[:, 0:512]))
        k.V.inc(e.bn_stats(st[:, 6:12], xt[:, 512:1024]))
        wV(e, k)
        k.V.inc(e.bn_aggr(mv, st))

    @b.scalar
    def _(e):
        wV(e, k)
        k.A.inc(e.activation(rstd, mv[:, 1:2], AF.Ln, bias=cst["epsc"][:]))
        wA(e, k)
        k.A.inc(e.activation(rstd, rstd, AF.Exp, scale=-0.5))

    @b.vector
    def _(e):
        wA(e, k)
        k.V.inc(e.tensor_scalar(xt, xt, mv[:, 0:1], rstd, ALU.subtract,
                                ALU.mult))
    return k.V.n


class Prologue:

    def __init__(self, k, x_in, xlnT, tr_banks, col_of=None):
        self.k = k
        self.x_in = x_in
        self.xlnT = xlnT
        self.col_of = col_of or (lambda c, i: c * T + i * TT)
        self.trv = [tb[:, 0:128] for tb in tr_banks]
        self.xa = k.sb("p_x", [128, 2 * D])
        self.st = k.sb("p_st", [128, 24])
        self.mv = k.sb("p_mv", [128, 4])
        self.rstd = k.sb("p_rstd", [128, 2])
        self.lnb = k.sb("p_lnb", [128, 2 * D], BF16)
        self.s_ld = [k.sem("p_ld0"), k.sem("p_ld1")]
        self.slot_free_v = [0, 0]
        self.lnb_free_p = [0, 0]
        self.tr_free_a = [0, 0]
        self.ld_t = {}
        self.ev_a = {}

    def tile(self, b, cst, i, post_ln=None, evac_wait=None):
        k = self.k
        sl = i % 2
        xt = self.xa[:, sl * D:(sl + 1) * D]
        lt = self.lnb[:, sl * D:(sl + 1) * D]
        st = self.st[:, sl * 12:(sl + 1) * 12]
        mv = self.mv[:, sl * 2:(sl + 1) * 2]
        rstd = self.rstd[:, sl:sl + 1]
        free_v = self.slot_free_v[sl]
        lnb_free = self.lnb_free_p[sl]

        s_ld = self.s_ld[sl]

        @b.sync
        def _(e):
            if free_v:
                wV(e, k, free_v)
            s_ld.inc(e.dma_start(out=xt, in_=self.x_in[i * TT:(i + 1) * TT, :]),
                     16)
        ld = s_ld.n
        self.ld_t[i] = (s_ld, ld)

        emit_ln(k, b, i, xt, st, mv, rstd, cst,
                lambda e, s_ld=s_ld, ld=ld: e.wait_ge(s_ld.h, ld))

        if post_ln is not None and getattr(post_ln, "multi", False):
            post_ln(b, i, sl, xt, lt, lnb_free)
        else:
            @b.vector
            def _(e):
                wV(e, k)
                if lnb_free:
                    wP(e, k, lnb_free)
                if post_ln is not None:
                    last = post_ln(e, xt, lt, i)
                    if last is not None:
                        k.V.inc(last)
                else:
                    k.V.inc(e.tensor_copy(lt, xt))
        self.slot_free_v[sl] = k.V.n
        v_lt = k.V.n

        for c in range(NC_D):
            @b.tensor
            def _(e, c=c):
                if c == 0:
                    wV(e, k, max(v_lt, cst["v_done"]))
                if self.tr_free_a[c % 2]:
                    wV(e, k, self.tr_free_a[c % 2])
                k.P.inc(e.transpose(
                    self.trv[c % 2],
                    lt[:, c * 128:(c + 1) * 128],
                    cst["ident"][:]))
            p_tr = k.P.n

            @b.vector
            def _(e, c=c, p_tr=p_tr):
                wP(e, k, p_tr)
                if c == 0 and evac_wait is not None:
                    evac_wait(e)
                col = self.col_of(c, i)
                k.V.inc(e.tensor_copy(
                    self.xlnT[:, col:col + TT],
                    self.trv[c % 2]))
            self.tr_free_a[c % 2] = k.V.n
        self.lnb_free_p[sl] = k.P.n
        self.ev_a[i] = k.V.n


class Epilogue:

    def __init__(self, k, x_in, x_out, init_wait_v=0):
        self.k = k
        self.init_wait_v = init_wait_v
        self.x_in, self.x_out = x_in, x_out
        self.xo = k.sb("e_xo", [128, 2 * D])
        self.st = k.sb("e_st", [128, 24])
        self.mv = k.sb("e_mv", [128, 4])
        self.rstd = k.sb("e_rstd", [128, 2])
        self.s_ld = [k.sem("e_ld0"), k.sem("e_ld1")]
        self.s_st = [k.sem("e_st0"), k.sem("e_st1")]

    def tile(self, b, cst, i, fout_ap, ready_wait):
        k = self.k
        sl = i % 2
        xo = self.xo[:, sl * D:(sl + 1) * D]
        st = self.st[:, sl * 12:(sl + 1) * 12]
        mv = self.mv[:, sl * 2:(sl + 1) * 2]
        rstd = self.rstd[:, sl:sl + 1]
        s_ld, s_st = self.s_ld[sl], self.s_st[sl]

        @b.sync
        def _(e):
            if s_st.n:
                e.wait_ge(s_st.h, s_st.n)
            elif self.init_wait_v:
                wV(e, k, self.init_wait_v)
            s_ld.inc(e.dma_start(out=xo, in_=self.x_in[i * TT:(i + 1) * TT, :]),
                     16)
        ld = s_ld.n

        emit_ln(k, b, i, fout_ap, st, mv, rstd, cst, ready_wait)

        @b.vector
        def _(e):
            wV(e, k)
            e.wait_ge(s_ld.h, ld)
            k.V.inc(e.tensor_add(xo, xo, fout_ap))
        v_add = k.V.n

        @b.sync
        def _(e):
            wV(e, k, v_add)
            s_st.inc(e.dma_start(out=self.x_out[i * TT:(i + 1) * TT, :],
                                 in_=xo), 16)

    def finish(self, b):
        k = self.k
        s_st = self.s_st

        @b.sync
        def _(e):
            for s in s_st:
                e.wait_ge(s.h, s.n)



def build_L3f():
    k = K("l3f")
    nc = k.nc
    x_in = k.din("x", [T, D])
    twT = k.din("twT", [D, D], I32)
    bias = k.din("bias", [1, D])
    ascale = k.din("ascale", [1, 1])
    x_out = k.dout("x_new", [T, D])

    xlnT = k.sb("xlnT", [128, NC_D * T], BF16)
    wb = k.sb("wb", [128, NC_D * D], BF16)
    wi = k.sb("wi", [128, NC_D * D], I32)
    srow = k.sb("srow", [1, 2])
    scol = k.sb("scol", [128, 2])
    bsc = k.sb("bsc", [1, D])
    fout = k.sb("fout", [128, 2 * D])

    bk_mm = [k.bank("mm0"), k.bank("mm1"), k.bank("mm2"), k.bank("mm3")]
    bk_tr = [k.bankb("tr0"), k.bankb("tr1")]
    bk_bc = k.bank("bc")

    with nc.Block() as b:
        cst = setup_consts(k, b)

        @b.gpsimd
        def _(e):
            k.GQ.inc(e.dma_start(
                out=wi[:].rearrange("p (c n) -> p c n", c=NC_D),
                in_=twT.rearrange("(c p) n -> p c n", p=128)), 16)
            k.GQ.inc(e.dma_start(out=srow[:, 0:1], in_=ascale), 16)
            k.GQ.inc(e.dma_start(out=bsc[:], in_=bias), 16)
        gq_w = k.GQ.n

        @b.vector
        def _(e):
            wGQ(e, k, gq_w)
            k.V.inc(e.tensor_scalar(wb[:], wi[:], 1.0, None, ALU.subtract))
            k.V.inc(e.tensor_scalar(srow[:, 0:1], srow[:, 0:1], 1e-5, None,
                                    ALU.max))
            wV(e, k)
            k.V.inc(e.reciprocal(srow[:, 1:2], srow[:, 0:1]))
            wV(e, k)
            k.V.inc(e.tensor_scalar(bsc[:], bsc[:], srow[:, 1:2], None,
                                    ALU.mult))
        v_scale = k.V.n

        @b.tensor
        def _(e):
            wV(e, k, max(v_scale, cst["v_done"]))
            k.P.inc(e.matmul(bk_bc[0:128, 0:2], cst["ones_rf"][0:1, 0:128],
                             srow[:, 0:2], start=True, stop=True))
        p_bc = k.P.n

        @b.vector
        def _(e):
            wP(e, k, p_bc)
            k.V.inc(e.tensor_copy(scol[:], bk_bc[0:128, 0:2]))
        v_scol = k.V.n

        def post_ln(e, xt, lt, i):
            if i == 0:
                wV(e, k, v_scol)
            k.V.inc(e.tensor_scalar(xt, xt, scol[:, 1:2], 127.0, ALU.mult,
                                    ALU.min))
            wV(e, k)
            k.V.inc(e.tensor_scalar(xt, xt, -127.0, cst["ceff"][:], ALU.max,
                                    ALU.add))
            wV(e, k)
            k.V.inc(e.tensor_scalar(lt, xt, cst["ceff"][:], None,
                                    ALU.subtract))

        pro = Prologue(k, x_in, xlnT, bk_tr)
        epi = Epilogue(k, x_in, x_out)
        mm_free_v = [0, 0]
        fout_free_v = [0, 0]

        for i in range(NTT):
            pro.tile(b, cst, i, post_ln=post_ln)
            bs = (i % 2) * 2
            pA, pB = bk_mm[bs], bk_mm[bs + 1]
            ev = pro.ev_a[i]
            freev = mm_free_v[i % 2]

            @b.tensor
            def _(e, i=i, pA=pA, pB=pB, ev=ev, freev=freev):
                wV(e, k, ev)
                if i == 0:
                    wV(e, k, v_scol)
                if freev:
                    wV(e, k, freev)
                for c in range(NC_D):
                    lhs = xlnT[:, c * T + i * TT:c * T + (i + 1) * TT]
                    e.matmul(pA[:, 0:512], lhs, wb[:, c * D:c * D + 512],
                             start=(c == 0), stop=False)
                    e.matmul(pB[:, 0:512], lhs, wb[:, c * D + 512:(c + 1) * D],
                             start=(c == 0), stop=False)
                e.matmul(pA[0:128, 0:512], cst["ones_rf"][0:1, 0:128],
                         bsc[:, 0:512], start=False, stop=True)
                k.P.inc(e.matmul(pB[0:128, 0:512], cst["ones_rf"][0:1, 0:128],
                                 bsc[:, 512:1024], start=False, stop=True))
            p_mm = k.P.n

            ft = fout[:, (i % 2) * D:(i % 2 + 1) * D]
            ffv = fout_free_v[i % 2]

            @b.vector
            def _(e, p_mm=p_mm, ft=ft, ffv=ffv):
                wP(e, k, p_mm)
                k.V.inc(e.tensor_scalar(ft[:, 0:512], pA[:, 0:512],
                                        scol[:, 0:1], None, ALU.mult))
                k.V.inc(e.tensor_scalar(ft[:, 512:1024], pB[:, 0:512],
                                        scol[:, 0:1], None, ALU.mult))
            v_f = k.V.n
            mm_free_v[i % 2] = v_f

            epi.tile(b, cst, i, ft, lambda e, v_f=v_f: wV(e, k, v_f))
            fout_free_v[i % 2] = k.V.n

        epi.finish(b)

    k.close()
    return nc



def emit_colsum_to_row(k, b, cst, col_ap, row_ap, bank, wait_v):
    @b.tensor
    def _(e):
        wV(e, k, max(wait_v, cst["v_done"]))
        k.P.inc(e.matmul(bank[0:1, 0:1], col_ap, cst["onescolf"][:],
                         start=True, stop=True))
    p = k.P.n

    @b.vector
    def _(e):
        wP(e, k, p)
        k.V.inc(e.tensor_copy(row_ap, bank[0:1, 0:1]))
    return k.V.n


def emit_colmax_to_row(k, b, cst, col_ap, row_ap, bank, wait_v):
    @b.tensor
    def _(e):
        wV(e, k, max(wait_v, cst["v_done"]))
        k.P.inc(e.transpose(bank[0:1, 0:128], col_ap, cst["identf"][:]))
    p = k.P.n

    @b.vector
    def _(e):
        wP(e, k, p)
        k.V.inc(e.reduce_max(row_ap, bank[0:1, 0:128], axis=AX.X))
    return k.V.n


def emit_bcast_col(k, b, cst, row_ap, col_ap, bank, wait_v, n=1):
    @b.tensor
    def _(e):
        wV(e, k, max(wait_v, cst["v_done"]))
        k.P.inc(e.matmul(bank[0:128, 0:n], cst["ones_rf"][0:1, 0:128],
                         row_ap, start=True, stop=True))
    p = k.P.n

    @b.vector
    def _(e):
        wP(e, k, p)
        k.V.inc(e.tensor_copy(col_ap, bank[0:128, 0:n]))
    return k.V.n



def build_L2f():
    k = K("l2f")
    nc = k.nc
    F = 4096
    NC_F = F // 128
    x_in = k.din("x", [T, D])
    wupT = k.din("wupT", [D, F])
    wdnT = k.din("wdnT", [F, D])
    x_out = k.dout("x_new", [T, D])

    xlnT = k.sb("xlnT", [128, NC_D * 512], BF16)
    wub = k.sb("wub", [128, NC_D * F], BF16)
    wdb = k.sb("wdb", [128, NC_F * D], BF16)
    acol = k.sb("acol", [128, 3])
    arow = k.sb("arow", [1, 2])
    tcol = k.sb("tcol", [128, 2])
    s2r = k.sb("s2r", [1, 256])
    s2c = k.sb("s2c", [128, 2])
    rmsc = k.sb("rmsc", [128, 2])

    bk_tr = [k.bankb("tr0"), k.bankb("tr1")]
    bk_m1 = [k.bank("m1a"), k.bank("m1b")]
    bk_ss = k.bank("ss")
    bk_m2 = [k.bank("m2a"), k.bank("m2b")]
    bk_bc = k.bank("bc")

    with nc.Block() as b:
        cst = setup_consts(k, b)
        wes = ExitStack()
        wf = wes.enter_context(nc.sbuf_tensor("wf", [128, F], F32))
        tscr = wes.enter_context(nc.sbuf_tensor("tscr", [128, F], BF16))

        s_wld = [k.sem("wld0"), k.sem("wld1")]

        @b.vector
        def _(e):
            k.V.inc(e.memset(acol[:, 0:2], 0.0))
        v_ac0 = k.V.n

        def w_dma(e, ci, wslot):
            if ci < 8:
                return e.dma_start(out=wslot,
                                   in_=wupT[ci * 128:(ci + 1) * 128, :])
            cj = ci - 8
            return e.dma_start(
                out=wslot.rearrange("p (g n) -> p g n", g=4),
                in_=wdnT[cj * 512:(cj + 1) * 512, :]
                .rearrange("(g p) n -> p g n", p=128))

        slot_rd_v = 0
        for ci in range(16):
            wslot = wf[:, 0:F]

            @b.sync
            def _(e, ci=ci, wslot=wslot):
                if slot_rd_v:
                    wV(e, k, slot_rd_v)
                s_wld[0].inc(w_dma(e, ci, wslot), 16)
            ldn = s_wld[0].n

            @b.vector
            def _(e, ci=ci, wslot=wslot, ldn=ldn):
                e.wait_ge(s_wld[0].h, ldn)
                wV(e, k)
                if ci < 2:
                    wV(e, k, v_ac0)
                ac = acol[:, (0 if ci < 8 else 1):(1 if ci < 8 else 2)]
                k.V.inc(e.reduce_sum(acol[:, 2:3], wslot, axis=AX.X,
                                     apply_absolute_value=True))
                wV(e, k)
                k.V.inc(e.tensor_add(ac, ac, acol[:, 2:3]))
            slot_rd_v = k.V.n
        v_wsum = k.V.n

        v_a1 = emit_colsum_to_row(k, b, cst, acol[:, 0:1], arow[:, 0:1],
                                  bk_bc, v_wsum)
        v_a2 = emit_colsum_to_row(k, b, cst, acol[:, 1:2], arow[:, 1:2],
                                  bk_bc, v_a1)

        @b.vector
        def _(e):
            wV(e, k, v_a2)
            k.V.inc(e.tensor_scalar(arow[:, 0:2], arow[:, 0:2],
                                    0.5 / (4096.0 * 1024.0), None, ALU.mult))
        v_tau = emit_bcast_col(k, b, cst, arow[:, 0:2], tcol[:, 0:2], bk_bc,
                               k.V.n, n=2)

        slot_rd_v = 0
        for ci in range(16):
            wslot = wf[:, 0:F]

            @b.sync
            def _(e, ci=ci, wslot=wslot):
                wV(e, k, max(slot_rd_v, v_tau))
                s_wld[0].inc(w_dma(e, ci, wslot), 16)
            ldn = s_wld[0].n

            up = ci < 8
            wt = (wub[:, ci * F:(ci + 1) * F] if up else
                  wdb[:, (ci - 8) * F:(ci - 7) * F])
            tc = tcol[:, 0:1] if up else tcol[:, 1:2]

            @b.vector
            def _(e, wslot=wslot, ldn=ldn, wt=wt, tc=tc):
                e.wait_ge(s_wld[0].h, ldn)
                wV(e, k)
                k.V.inc(e.tensor_scalar(tscr[:], wslot, -1.0, tc, ALU.mult,
                                        ALU.is_gt))
                k.V.inc(e.tensor_scalar(wt, wslot, tc, None, ALU.is_gt))
                wV(e, k)
                k.V.inc(e.tensor_sub(wt, wt, tscr[:]))
            slot_rd_v = k.V.n
        v_wready = k.V.n
        wes.close()
        hbuf = k.sb("hbuf", [128, NC_F * 256], BF16)
        sq = k.sb("sq", [128, 256], BF16)
        fout = k.sb("fout", [128, D])

        def col_of(c, i):
            return c * 512 + ((i // 2) % 2) * 256 + (i % 2) * TT
        pro = Prologue(k, x_in, xlnT, bk_tr, col_of=col_of)
        pro.slot_free_v = [v_wready, v_wready]
        epi = Epilogue(k, x_in, x_out, init_wait_v=v_wready)

        def post_ln_multi(b2, i, sl, xt, lt, lnb_free):
            mvv = pro.mv[:, sl * 2 + 1:sl * 2 + 2]
            rst = pro.rstd[:, sl:sl + 1]
            rc = rmsc[:, sl:sl + 1]

            @b2.scalar
            def _(e):
                wV(e, k)
                k.A.inc(e.activation(rc, mvv, AF.Sqrt))

            @b2.vector
            def _(e):
                wA(e, k)
                if lnb_free:
                    wP(e, k, lnb_free)
                k.V.inc(e.tensor_scalar(rc, rc, rst, 1e-8, ALU.mult, ALU.add))
                wV(e, k)
                k.V.inc(e.reciprocal(rc, rc))
                wV(e, k)
                k.V.inc(e.tensor_scalar(lt, xt, rc, None, ALU.mult))
        post_ln_multi.multi = True

        m1_free_a = [0, 0]
        m2_free_v = 0
        fout_free = 0
        m1_last_p = [0, 0]

        for tci in range(NTT // 2):
            tiles = list(range(2 * tci, 2 * tci + 2))
            slw = m1_last_p[tci % 2]
            for i in tiles:
                ew = (lambda e, slw=slw: wP(e, k, slw)) if slw else None
                pro.tile(b, cst, i, post_ln=post_ln_multi, evac_wait=ew)
            ev = pro.ev_a[tiles[-1]]
            xsl = (tci % 2) * 256

            a_sq_prev = 0
            for fc in range(NC_F):
                bk = bk_m1[fc % 2]
                freea = m1_free_a[fc % 2]

                @b.tensor
                def _(e, fc=fc, bk=bk, freea=freea, ev=ev, tci=tci):
                    if fc == 0:
                        wV(e, k, ev)
                        if tci == 0:
                            wV(e, k, v_wready)
                    if freea:
                        wA(e, k, freea)
                    for c in range(NC_D):
                        k.P.inc(e.matmul(
                            bk[:, 0:256],
                            wub[:, c * F + fc * 128:c * F + (fc + 1) * 128],
                            xlnT[:, c * 512 + xsl:c * 512 + xsl + 256],
                            start=(c == 0), stop=(c == NC_D - 1)))
                p_m1 = k.P.n
                m1_last_p[tci % 2] = p_m1

                @b.scalar
                def _(e, fc=fc, bk=bk, p_m1=p_m1):
                    wP(e, k, p_m1)
                    k.A.inc(e.activation(hbuf[:, fc * 256:(fc + 1) * 256],
                                         bk[:, 0:256], AF.Gelu))
                    wA(e, k)
                    k.A.inc(e.activation(sq[:],
                                         hbuf[:, fc * 256:(fc + 1) * 256],
                                         AF.Square))
                a_sq = k.A.n
                m1_free_a[fc % 2] = a_sq

                @b.tensor
                def _(e, fc=fc, a_sq=a_sq):
                    wA(e, k, a_sq)
                    k.P.inc(e.matmul(bk_ss[0:1, 0:256], cst["onescol"][:],
                                     sq[:], start=(fc == 0),
                                     stop=(fc == NC_F - 1)))
            p_ss = k.P.n

            @b.scalar
            def _(e, p_ss=p_ss):
                wP(e, k, p_ss)
                k.A.inc(e.activation(s2r[:], bk_ss[0:1, 0:256], AF.Sqrt,
                                     scale=1.0 / 4096.0))
            a_s2 = k.A.n

            @b.vector
            def _(e, a_s2=a_s2):
                wA(e, k, a_s2)
                k.V.inc(e.tensor_scalar(s2r[:], s2r[:], 1e-8, None, ALU.add))
                wV(e, k)
                k.V.inc(e.reciprocal(s2r[:], s2r[:]))
            v_s2 = k.V.n

            @b.tensor
            def _(e, v_s2=v_s2):
                wV(e, k, v_s2)
                for j in range(2):
                    k.P.inc(e.matmul(bk_bc[0:128, j:j + 1],
                                     s2r[0:1, j * 128:(j + 1) * 128],
                                     cst["ones_rf"][0:1, 0:1],
                                     start=True, stop=True))
            p_s2c = k.P.n

            @b.vector
            def _(e, p_s2c=p_s2c):
                wP(e, k, p_s2c)
                k.V.inc(e.tensor_copy(s2c[:, 0:2], bk_bc[0:128, 0:2]))
            v_s2c = k.V.n

            for j, i in enumerate(tiles):
                @b.tensor
                def _(e, j=j, i=i, v_s2c=v_s2c):
                    if j == 0:
                        wV(e, k, v_s2c)
                    if m2_free_v:
                        wV(e, k, m2_free_v)
                    for half in range(2):
                        bk = bk_m2[half]
                        for fc in range(NC_F):
                            k.P.inc(e.matmul(
                                bk[:, 0:512],
                                hbuf[:, fc * 256 + j * 128:
                                     fc * 256 + (j + 1) * 128],
                                wdb[:, fc * D + half * 512:
                                    fc * D + (half + 1) * 512],
                                start=(fc == 0), stop=(fc == NC_F - 1)))
                p_m2 = k.P.n

                ft = fout[:, 0:D]
                ffv = fout_free

                @b.vector
                def _(e, p_m2=p_m2, ft=ft, ffv=ffv, j=j):
                    wP(e, k, p_m2)
                    if ffv:
                        wV(e, k, ffv)
                    k.V.inc(e.tensor_scalar(ft[:, 0:512], bk_m2[0][:, 0:512],
                                            s2c[:, j:j + 1], None, ALU.mult))
                    k.V.inc(e.tensor_scalar(ft[:, 512:1024],
                                            bk_m2[1][:, 0:512],
                                            s2c[:, j:j + 1], None, ALU.mult))
                v_f = k.V.n
                m2_free_v = v_f

                epi.tile(b, cst, i, ft, lambda e, v_f=v_f: wV(e, k, v_f))
                fout_free = k.V.n

        epi.finish(b)

    k.close()
    return nc


def run_L2f(x_full, w_up, w_down):
    nc = build_L2f()
    shards = np_shard_x(x_full)
    wupT = np.ascontiguousarray(np.asarray(w_up).T)
    wdnT = np.ascontiguousarray(np.asarray(w_down).T)
    in_maps = [{"x": s, "wupT": wupT, "wdnT": wdnT} for s in shards]
    res = run_bass_kernel_spmd(nc, in_maps, core_ids=list(range(NCORES)))
    return np.concatenate([r["x_new"].reshape(B_PER, S, D)
                           for r in res.results], axis=0)



def build_L1f():
    k = K("l1f")
    nc = k.nc
    F = 4096
    NC_F = F // 128
    x_in = k.din("x", [T, D])
    wfcT = k.din("wfcT", [D, F])
    wprT = k.din("wprT", [F, D])
    afcT = k.din("afcT", [D, 16])
    bfcT = k.din("bfcT", [16, F])
    aprT = k.din("aprT", [F, 16])
    bprT = k.din("bprT", [16, D])
    bfc_pc = k.din("bfc_pc", [128, 32])
    bpr = k.din("bpr", [1, D])
    x_out = k.dout("x_new", [T, D])

    wqf = k.sb("wqf", [128, NC_D * F], BF16)
    wqp = k.sb("wqp", [128, NC_F * D], BF16)
    afb = k.sb("afb", [128, NC_D * 16], BF16)
    bfb = k.sb("bfb", [16, F], BF16)
    apb = k.sb("apb", [128, NC_F * 16], BF16)
    bpb = k.sb("bpb", [16, D], BF16)
    bfcc = k.sb("bfcc", [128, 32])
    bprr = k.sb("bprr", [1, D])
    xaT = k.sb("xaT", [16, 256], BF16)
    haT = k.sb("haT", [16, 256], BF16)
    mcol = k.sb("mcol", [128, 3])
    mrow = k.sb("mrow", [1, 4])
    scl = k.sb("scl", [128, 4])
    xlnT = k.sb("xlnT", [128, NC_D * 512], BF16)

    bk_tr = [k.bankb("tr0"), k.bankb("tr1")]
    bk_m1 = [k.bank("m1a"), k.bank("m1b")]
    bk_lo = k.bank("lo")
    bk_m2 = [k.bank("m2a"), k.bank("m2b")]
    bk_bc = k.bank("bc")

    with nc.Block() as b:
        cst = setup_consts(k, b, need_identf=True)
        wes = ExitStack()
        wf = wes.enter_context(nc.sbuf_tensor("wf", [128, F], F32))

        s_wld = [k.sem("wld0")]

        @b.gpsimd
        def _(e):
            k.GQ.inc(e.dma_start(out=afb[:].rearrange("p (c n) -> p c n", c=NC_D),
                                 in_=afcT.rearrange("(c p) n -> p c n", p=128)), 16)
            k.GQ.inc(e.dma_start(out=bfb[:], in_=bfcT), 16)
            k.GQ.inc(e.dma_start(out=apb[:].rearrange("p (c n) -> p c n", c=NC_F),
                                 in_=aprT.rearrange("(c p) n -> p c n", p=128)), 16)
            k.GQ.inc(e.dma_start(out=bpb[:], in_=bprT), 16)
        gq_small = k.GQ.n

        @b.sync
        def _(e):
            k.SY.inc(e.dma_start(out=bfcc[:], in_=bfc_pc), 16)
            k.SY.inc(e.dma_start(out=bprr[:], in_=bpr), 16)
        sy_b = k.SY.n

        @b.vector
        def _(e):
            k.V.inc(e.memset(mcol[:, 0:2], 0.0))
        v_m0 = k.V.n

        def w_dma(e, ci, dst):
            if ci < 8:
                return e.dma_start(out=dst, in_=wfcT[ci * 128:(ci + 1) * 128, :])
            cj = ci - 8
            return e.dma_start(
                out=dst.rearrange("p (g n) -> p g n", g=4),
                in_=wprT[cj * 512:(cj + 1) * 512, :]
                .rearrange("(g p) n -> p g n", p=128))

        slot_rd = 0
        for ci in range(16):
            @b.sync
            def _(e, ci=ci):
                if slot_rd:
                    wV(e, k, slot_rd)
                s_wld[0].inc(w_dma(e, ci, wf[:]), 16)
            ldn = s_wld[0].n

            @b.vector
            def _(e, ci=ci, ldn=ldn):
                e.wait_ge(s_wld[0].h, ldn)
                wV(e, k)
                if ci < 2:
                    wV(e, k, v_m0)
                mc = mcol[:, (0 if ci < 8 else 1):(1 if ci < 8 else 2)]
                k.V.inc(e.reduce_max(mcol[:, 2:3], wf[:], axis=AX.X,
                                     apply_absolute_value=True))
                wV(e, k)
                k.V.inc(e.tensor_max(mc, mc, mcol[:, 2:3]))
            slot_rd = k.V.n

        v_x1 = emit_colmax_to_row(k, b, cst, mcol[:, 0:1], mrow[:, 0:1],
                                  bk_bc, slot_rd)
        v_x2 = emit_colmax_to_row(k, b, cst, mcol[:, 1:2], mrow[:, 2:3],
                                  bk_bc, v_x1)

        @b.vector
        def _(e):
            wV(e, k, v_x2)
            k.V.inc(e.tensor_scalar(mrow[:, 0:1], mrow[:, 0:1], 1.0 / 127.0,
                                    None, ALU.mult))
            k.V.inc(e.tensor_scalar(mrow[:, 2:3], mrow[:, 2:3], 1.0 / 127.0,
                                    None, ALU.mult))
            wV(e, k)
            k.V.inc(e.reciprocal(mrow[:, 1:2], mrow[:, 0:1]))
            k.V.inc(e.reciprocal(mrow[:, 3:4], mrow[:, 2:3]))
        v_s = emit_bcast_col(k, b, cst, mrow[:, 0:4], scl[:, 0:4], bk_bc,
                             k.V.n, n=4)

        slot_rd = 0
        for ci in range(16):
            @b.sync
            def _(e, ci=ci):
                wV(e, k, max(slot_rd, v_s))
                s_wld[0].inc(w_dma(e, ci, wf[:]), 16)
            ldn = s_wld[0].n

            up = ci < 8
            wt = (wqf[:, ci * F:(ci + 1) * F] if up else
                  wqp[:, (ci - 8) * F:(ci - 7) * F])
            rsc = scl[:, 1:2] if up else scl[:, 3:4]
            sc = scl[:, 0:1] if up else scl[:, 2:3]

            @b.vector
            def _(e, ldn=ldn, wt=wt, rsc=rsc, sc=sc):
                e.wait_ge(s_wld[0].h, ldn)
                wV(e, k)
                k.V.inc(e.tensor_scalar(wf[:], wf[:], rsc, 127.0, ALU.mult,
                                        ALU.min))
                wV(e, k)
                k.V.inc(e.tensor_scalar(wf[:], wf[:], -128.0, cst["ceff"][:],
                                        ALU.max, ALU.add))
                wV(e, k)
                k.V.inc(e.tensor_scalar(wf[:], wf[:], cst["ceff"][:], None,
                                        ALU.subtract))
                wV(e, k)
                k.V.inc(e.tensor_scalar(wt, wf[:], sc, None, ALU.mult))
            slot_rd = k.V.n
        v_wready = k.V.n
        wes.close()
        hbuf = k.sb("hbuf", [128, NC_F * 256], BF16)
        fout = k.sb("fout", [128, D])

        def col_of(c, i):
            return c * 512 + ((i // 2) % 2) * 256 + (i % 2) * TT
        pro = Prologue(k, x_in, xlnT, bk_tr, col_of=col_of)
        pro.slot_free_v = [v_wready, v_wready]
        epi = Epilogue(k, x_in, x_out, init_wait_v=v_wready)

        m1_free_a = [0, 0]
        m2_free_v = 0
        fout_free = 0
        m1_last_p = [0, 0]
        xa_free_p = 0
        ha_free_p = 0

        for tci in range(NTT // 2):
            tiles = list(range(2 * tci, 2 * tci + 2))
            slw = m1_last_p[tci % 2]
            for i in tiles:
                ew = (lambda e, slw=slw: wP(e, k, slw)) if slw else None
                pro.tile(b, cst, i, evac_wait=ew)
            ev = pro.ev_a[tiles[-1]]
            xsl = (tci % 2) * 256

            @b.tensor
            def _(e, ev=ev, xsl=xsl, tci=tci, xa_free_p=xa_free_p):
                wV(e, k, ev)
                if tci == 0:
                    wV(e, k, v_wready)
                    wGQ(e, k, gq_small)
                    wSY(e, k, sy_b)
                for c in range(NC_D):
                    k.P.inc(e.matmul(bk_lo[0:16, 0:256],
                                     afb[:, c * 16:(c + 1) * 16],
                                     xlnT[:, c * 512 + xsl:c * 512 + xsl + 256],
                                     start=(c == 0), stop=(c == NC_D - 1)))
            p_xa = k.P.n

            @b.vector
            def _(e, p_xa=p_xa):
                wP(e, k, p_xa)
                k.V.inc(e.tensor_scalar(xaT[:, 0:256], bk_lo[0:16, 0:256],
                                        2.0, None, ALU.mult))
            v_xa = k.V.n

            for fc in range(NC_F):
                bk = bk_m1[fc % 2]
                freea = m1_free_a[fc % 2]

                @b.tensor
                def _(e, fc=fc, bk=bk, freea=freea, v_xa=v_xa, xsl=xsl):
                    if fc == 0:
                        wV(e, k, v_xa)
                    if freea:
                        wA(e, k, freea)
                    for c in range(NC_D):
                        e.matmul(bk[:, 0:256],
                                 wqf[:, c * F + fc * 128:c * F + (fc + 1) * 128],
                                 xlnT[:, c * 512 + xsl:c * 512 + xsl + 256],
                                 start=(c == 0), stop=False)
                    k.P.inc(e.matmul(bk[:, 0:256],
                                     bfb[:, fc * 128:(fc + 1) * 128],
                                     xaT[:, 0:256], start=False, stop=True))
                p_m1 = k.P.n
                m1_last_p[tci % 2] = p_m1

                @b.scalar
                def _(e, fc=fc, bk=bk, p_m1=p_m1):
                    wP(e, k, p_m1)
                    k.A.inc(e.activation(hbuf[:, fc * 256:(fc + 1) * 256],
                                         bk[:, 0:256], AF.Gelu,
                                         bias=bfcc[:, fc:fc + 1]))
                m1_free_a[fc % 2] = k.A.n
            a_h = k.A.n

            @b.tensor
            def _(e, a_h=a_h):
                wA(e, k, a_h)
                for fc in range(NC_F):
                    k.P.inc(e.matmul(bk_lo[0:16, 0:256],
                                     apb[:, fc * 16:(fc + 1) * 16],
                                     hbuf[:, fc * 256:(fc + 1) * 256],
                                     start=(fc == 0), stop=(fc == NC_F - 1)))
            p_ha = k.P.n

            @b.vector
            def _(e, p_ha=p_ha):
                wP(e, k, p_ha)
                k.V.inc(e.tensor_scalar(haT[:, 0:256], bk_lo[0:16, 0:256],
                                        2.0, None, ALU.mult))
            v_ha = k.V.n

            for j, i in enumerate(tiles):
                @b.tensor
                def _(e, j=j, v_ha=v_ha, m2f=m2_free_v):
                    if j == 0:
                        wV(e, k, v_ha)
                    if m2f:
                        wV(e, k, m2f)
                    for half in range(2):
                        bk = bk_m2[half]
                        for fc in range(NC_F):
                            e.matmul(bk[:, 0:512],
                                     hbuf[:, fc * 256 + j * 128:
                                          fc * 256 + (j + 1) * 128],
                                     wqp[:, fc * D + half * 512:
                                         fc * D + (half + 1) * 512],
                                     start=(fc == 0), stop=False)
                        e.matmul(bk[:, 0:512], haT[:, j * 128:(j + 1) * 128],
                                 bpb[:, half * 512:(half + 1) * 512],
                                 start=False, stop=False)
                        k.P.inc(e.matmul(bk[0:128, 0:512],
                                         cst["ones_rf"][0:1, 0:128],
                                         bprr[:, half * 512:(half + 1) * 512],
                                         start=False, stop=True))
                p_m2 = k.P.n

                ft = fout[:, 0:D]
                ffv = fout_free

                @b.vector
                def _(e, p_m2=p_m2, ft=ft, ffv=ffv):
                    wP(e, k, p_m2)
                    if ffv:
                        wV(e, k, ffv)
                    k.V.inc(e.tensor_copy(ft[:, 0:512], bk_m2[0][:, 0:512]))
                    k.V.inc(e.tensor_copy(ft[:, 512:1024], bk_m2[1][:, 0:512]))
                v_f = k.V.n
                m2_free_v = v_f

                epi.tile(b, cst, i, ft, lambda e, v_f=v_f: wV(e, k, v_f))
                fout_free = k.V.n

        epi.finish(b)

    k.close()
    return nc


def run_L1f(x_full, W_fc, b_fc, A_fc, B_fc, W_proj, b_proj, A_proj, B_proj):
    nc = build_L1f()
    shards = np_shard_x(x_full)
    cc = np.ascontiguousarray
    in_common = {
        "wfcT": cc(np.asarray(W_fc).T), "wprT": cc(np.asarray(W_proj).T),
        "afcT": cc(np.asarray(A_fc).T), "bfcT": cc(np.asarray(B_fc).T),
        "aprT": cc(np.asarray(A_proj).T), "bprT": cc(np.asarray(B_proj).T),
        "bfc_pc": cc(np.asarray(b_fc).reshape(32, 128).T),
        "bpr": cc(np.asarray(b_proj).reshape(1, D)),
    }
    in_maps = [dict(in_common, x=s) for s in shards]
    res = run_bass_kernel_spmd(nc, in_maps, core_ids=list(range(NCORES)))
    return np.concatenate([r["x_new"].reshape(B_PER, S, D)
                           for r in res.results], axis=0)



def build_maxln():
    k = K("maxln")
    nc = k.nc
    x_in = k.din("x", [T, D])
    pmax = k.dout("pmax", [1, 1])

    xa = k.sb("xa", [128, 2 * D])
    st = k.sb("st", [128, 24])
    mv = k.sb("mv", [128, 4])
    rstd = k.sb("rstd", [128, 2])
    mcol = k.sb("mcol", [128, 2])
    mrow = k.sb("mrow", [1, 1])
    bk = k.bank("b0")
    s_ld = [k.sem("ld0"), k.sem("ld1")]

    with nc.Block() as b:
        cst = setup_consts(k, b, need_identf=True)

        @b.vector
        def _(e):
            k.V.inc(e.memset(mcol[:, 0:1], -1e30))
        free_v = [k.V.n, k.V.n]

        for i in range(NTT):
            sl = i % 2
            xt = xa[:, sl * D:(sl + 1) * D]

            @b.sync
            def _(e, i=i, sl=sl, xt=xt):
                wV(e, k, free_v[sl])
                s_ld[sl].inc(e.dma_start(
                    out=xt, in_=x_in[i * TT:(i + 1) * TT, :]), 16)
            ldn = s_ld[sl].n

            emit_ln(k, b, i, xt, st[:, sl * 12:(sl + 1) * 12],
                    mv[:, sl * 2:(sl + 1) * 2], rstd[:, sl:sl + 1], cst,
                    lambda e, sl=sl, ldn=ldn: e.wait_ge(s_ld[sl].h, ldn))

            @b.vector
            def _(e, xt=xt):
                wV(e, k)
                k.V.inc(e.reduce_max(mcol[:, 1:2], xt, axis=AX.X))
                wV(e, k)
                k.V.inc(e.tensor_max(mcol[:, 0:1], mcol[:, 0:1], mcol[:, 1:2]))
            free_v[sl] = k.V.n

        v_m = emit_colmax_to_row(k, b, cst, mcol[:, 0:1], mrow[:, 0:1],
                                 bk, k.V.n)

        @b.sync
        def _(e):
            wV(e, k, v_m)
            k.SY.inc(e.dma_start(out=pmax, in_=mrow[:, 0:1]), 16)
            wSY(e, k)

    k.close()
    return nc



def build_L0f1():
    k = K("l0f1")
    nc = k.nc
    F = 4096
    NC_F = F // 128
    x_in = k.din("x", [T, D])
    pmax8 = k.din("pmax8", [1, 8])
    w1 = k.din("w1", [F, D])
    b1c = k.din("b1c", [128, 32])
    h_out = k.dout("h_out", [32, 128, T], BF16)
    pmaxh = k.dout("pmaxh", [1, 1])

    wq1T = k.sb("wq1T", [128, NC_D * F], BF16)
    xlnT = k.sb("xlnT", [128, NC_D * 512], BF16)
    sw = k.sb("sw", [128, 32])
    bq = k.sb("bq", [128, 32])
    srow = k.sb("srow", [1, 4])
    scol = k.sb("scol", [128, 2])
    hstage = k.sb("hstage", [128, 2 * 512], BF16)
    hmaxc = k.sb("hmaxc", [128, 3])
    hmaxr = k.sb("hmaxr", [1, 1])
    pm8 = k.sb("pm8", [1, 8])

    bk_tr = [k.bankb("tr0"), k.bankb("tr1")]
    bk_m1 = [k.bank("m1a"), k.bank("m1b")]
    bk_bc = k.bank("bc")

    with nc.Block() as b:
        cst = setup_consts(k, b, need_identf=True)
        wes = ExitStack()
        wstg = wes.enter_context(nc.sbuf_tensor("wstg", [128, D], F32))
        wqs = wes.enter_context(nc.sbuf_tensor("wqs", [128, D], BF16))

        @b.sync
        def _(e):
            k.SY.inc(e.dma_start(out=pm8[:], in_=pmax8), 16)
            k.SY.inc(e.dma_start(out=bq[:, 0:32], in_=b1c), 16)
        sy_p2 = k.SY.n

        @b.vector
        def _(e):
            e.wait_ge(k.SY.h, sy_p2)
            k.V.inc(e.reduce_max(srow[:, 0:1], pm8[:], axis=AX.X))
            wV(e, k)
            k.V.inc(e.tensor_scalar(srow[:, 1:2], srow[:, 0:1], 1.0 / 255.0,
                                    None, ALU.mult))
            wV(e, k)
            k.V.inc(e.reciprocal(srow[:, 2:3], srow[:, 1:2]))
        v_sin = emit_bcast_col(k, b, cst, srow[:, 1:3], scol[:, 0:2], bk_bc,
                               k.V.n, n=2)

        s_wld = [k.sem("wld0")]
        slot_rd = 0
        tr_free_a = [0, 0]
        wqs_free_p = 0
        trv = [tb[:, 0:128] for tb in bk_tr]
        for fc in range(32):
            @b.sync
            def _(e, fc=fc):
                if slot_rd:
                    wV(e, k, slot_rd)
                s_wld[0].inc(e.dma_start(
                    out=wstg[:], in_=w1[fc * 128:(fc + 1) * 128, :]), 16)
            ldn = s_wld[0].n

            @b.vector
            def _(e, fc=fc, ldn=ldn, wqs_free_p=wqs_free_p):
                e.wait_ge(s_wld[0].h, ldn)
                wV(e, k)
                if wqs_free_p:
                    wP(e, k, wqs_free_p)
                k.V.inc(e.reduce_max(sw[:, fc:fc + 1], wstg[:], axis=AX.X,
                                     apply_absolute_value=True))
                wV(e, k)
                k.V.inc(e.tensor_scalar(sw[:, fc:fc + 1], sw[:, fc:fc + 1],
                                        1e-8, 1.0 / 127.0, ALU.max, ALU.mult))
                wV(e, k)
                k.V.inc(e.reciprocal(hmaxc[:, 2:3], sw[:, fc:fc + 1]))
                wV(e, k)
                k.V.inc(e.tensor_scalar(wstg[:], wstg[:], hmaxc[:, 2:3],
                                        127.0, ALU.mult, ALU.min))
                wV(e, k)
                k.V.inc(e.tensor_scalar(wstg[:], wstg[:], -128.0,
                                        cst["ceff"][:], ALU.max, ALU.add))
                wV(e, k)
                k.V.inc(e.tensor_scalar(wstg[:], wstg[:], cst["ceff"][:],
                                        None, ALU.subtract))
                wV(e, k)
                k.V.inc(e.tensor_scalar(wqs[:], wstg[:], sw[:, fc:fc + 1],
                                        None, ALU.mult))
            slot_rd = k.V.n
            v_wq = k.V.n

            for c in range(NC_D):
                @b.tensor
                def _(e, fc=fc, c=c, v_wq=v_wq):
                    if c == 0:
                        wV(e, k, max(v_wq, cst["v_done"]))
                    if tr_free_a[c % 2]:
                        wV(e, k, tr_free_a[c % 2])
                    k.P.inc(e.transpose(trv[c % 2],
                                        wqs[:, c * 128:(c + 1) * 128],
                                        cst["ident"][:]))
                p_tr = k.P.n

                @b.vector
                def _(e, fc=fc, c=c, p_tr=p_tr):
                    wP(e, k, p_tr)
                    k.V.inc(e.tensor_copy(
                        wq1T[:, c * F + fc * 128:c * F + (fc + 1) * 128],
                        trv[c % 2]))
                tr_free_a[c % 2] = k.V.n
            wqs_free_p = k.P.n
        a_wq = k.V.n

        @b.vector
        def _(e):
            wV(e, k, a_wq)
            wV(e, k, v_sin)
            for fc in range(32):
                k.V.inc(e.reciprocal(hmaxc[:, 2:3], sw[:, fc:fc + 1]))
                wV(e, k)
                k.V.inc(e.tensor_scalar(bq[:, fc:fc + 1], bq[:, fc:fc + 1],
                                        scol[:, 1:2], hmaxc[:, 2:3],
                                        ALU.mult, ALU.mult))
                wV(e, k)
                k.V.inc(e.tensor_scalar(bq[:, fc:fc + 1], bq[:, fc:fc + 1],
                                        cst["ceff"][:], cst["ceff"][:],
                                        ALU.add, ALU.subtract))
                wV(e, k)
                k.V.inc(e.tensor_scalar(bq[:, fc:fc + 1], bq[:, fc:fc + 1],
                                        scol[:, 0:1], sw[:, fc:fc + 1],
                                        ALU.mult, ALU.mult))
                wV(e, k)
            k.V.inc(e.memset(hmaxc[:, 0:1], 0.0))
        v_bq = k.V.n
        wes.close()

        def post_ln(e, xt, lt, i):
            if i == 0:
                wV(e, k, v_bq)
            k.V.inc(e.tensor_scalar(xt, xt, scol[:, 1:2], 255.0, ALU.mult,
                                    ALU.min))
            wV(e, k)
            k.V.inc(e.tensor_scalar(xt, xt, 0.0, cst["ceff"][:], ALU.max,
                                    ALU.add))
            wV(e, k)
            return e.tensor_scalar(lt, xt, cst["ceff"][:], None, ALU.subtract)

        def col_of(c, i):
            return c * 512 + ((i // 2) % 2) * 256 + (i % 2) * TT
        pro = Prologue(k, x_in, xlnT, bk_tr, col_of=col_of)
        pro.slot_free_v = [v_bq, v_bq]

        s_hst = [k.sem("hst0"), k.sem("hst1")]
        m1_last_p = [0, 0]
        m1_free_a = [0, 0]

        for tci in range(NTT // 2):
            tiles = list(range(2 * tci, 2 * tci + 2))
            slw = m1_last_p[tci % 2]
            for i in tiles:
                ew = (lambda e, slw=slw: wP(e, k, slw)) if slw else None
                pro.tile(b, cst, i, post_ln=post_ln, evac_wait=ew)
            ev = pro.ev_a[tiles[-1]]
            xsl = (tci % 2) * 256
            t0 = tci * 256

            for fc in range(NC_F):
                bk = bk_m1[fc % 2]

                @b.tensor
                def _(e, fc=fc, bk=bk, ev=ev, xsl=xsl, tci=tci,
                      freea=m1_free_a[fc % 2]):
                    if fc == 0:
                        wV(e, k, ev)
                    if freea:
                        wA(e, k, freea)
                    for c in range(NC_D):
                        k.P.inc(e.matmul(
                            bk[:, 0:256],
                            wq1T[:, c * F + fc * 128:c * F + (fc + 1) * 128],
                            xlnT[:, c * 512 + xsl:c * 512 + xsl + 256],
                            start=(c == 0), stop=(c == NC_D - 1)))
                p_m1 = k.P.n
                m1_last_p[tci % 2] = p_m1

                hs = hstage[:, (fc % 2) * 512:(fc % 2) * 512 + 256]

                @b.scalar
                def _(e, fc=fc, bk=bk, p_m1=p_m1, hs=hs):
                    wP(e, k, p_m1)
                    e.wait_ge(s_hst[fc % 2].h, s_hst[fc % 2].n)
                    k.A.inc(e.activation(hs, bk[:, 0:256], AF.Relu,
                                         bias=bq[:, fc:fc + 1],
                                         scale=scol[:, 0:1]))
                a_h = k.A.n
                m1_free_a[fc % 2] = a_h

                @b.vector
                def _(e, hs=hs, a_h=a_h):
                    wA(e, k, a_h)
                    wV(e, k)
                    k.V.inc(e.reduce_max(hmaxc[:, 1:2], hs, axis=AX.X))
                    wV(e, k)
                    k.V.inc(e.tensor_max(hmaxc[:, 0:1], hmaxc[:, 0:1],
                                         hmaxc[:, 1:2]))
                v_hm = k.V.n

                @b.sync
                def _(e, fc=fc, hs=hs, v_hm=v_hm, t0=t0):
                    wV(e, k, v_hm)
                    s_hst[fc % 2].inc(e.dma_start(
                        out=h_out[fc, :, t0:t0 + 256], in_=hs), 16)

        v_hmax = emit_colmax_to_row(k, b, cst, hmaxc[:, 0:1], hmaxr[:, 0:1],
                                    bk_bc, k.V.n)

        @b.sync
        def _(e):
            wV(e, k, v_hmax)
            k.SY.inc(e.dma_start(out=pmaxh, in_=hmaxr[:, 0:1]), 16)
            for s in s_hst:
                e.wait_ge(s.h, s.n)
            wSY(e, k)

    k.close()
    return nc



def build_L0f2():
    k = K("l0f2")
    nc = k.nc
    F = 4096
    NC_F = F // 128
    x_in = k.din("x", [T, D])
    h_in = k.din("h_in", [32, 128, T], BF16)
    pmaxh8 = k.din("pmaxh8", [1, 8])
    w2 = k.din("w2", [D, F])
    b2c = k.din("b2c", [128, 8])
    x_out = k.dout("x_new", [T, D])

    wq2T = k.sb("wq2T", [128, NC_F * D], BF16)
    sw = k.sb("sw", [128, 8])
    bq = k.sb("bq", [128, 8])
    pm8 = k.sb("pm8", [1, 8])
    srow = k.sb("srow", [1, 4])
    scol = k.sb("scol", [128, 2])
    foutT = k.sb("foutT", [128, NC_D * 512], BF16)
    fout = k.sb("fout", [128, D])
    rcol = k.sb("rcol", [128, 1])

    bk_tr = [k.bankb("tr0"), k.bankb("tr1")]
    bk_m2 = [k.bank("m2a"), k.bank("m2b")]
    bk_bc = k.bank("bc")

    with nc.Block() as b:
        cst = setup_consts(k, b, need_identf=True)
        wes = ExitStack()
        wstg = wes.enter_context(nc.sbuf_tensor("wstg", [128, F], F32))
        wqs = wes.enter_context(nc.sbuf_tensor("wqs", [128, F], BF16))

        @b.sync
        def _(e):
            k.SY.inc(e.dma_start(out=pm8[:], in_=pmaxh8), 16)
            k.SY.inc(e.dma_start(out=bq[:, 0:8], in_=b2c), 16)
        sy_p = k.SY.n

        @b.vector
        def _(e):
            e.wait_ge(k.SY.h, sy_p)
            k.V.inc(e.reduce_max(srow[:, 0:1], pm8[:], axis=AX.X))
            wV(e, k)
            k.V.inc(e.tensor_scalar(srow[:, 1:2], srow[:, 0:1], 1.0 / 255.0,
                                    None, ALU.mult))
            wV(e, k)
            k.V.inc(e.reciprocal(srow[:, 2:3], srow[:, 1:2]))
        v_sh = emit_bcast_col(k, b, cst, srow[:, 1:3], scol[:, 0:2], bk_bc,
                              k.V.n, n=2)

        s_wld = [k.sem("wld0")]
        slot_rd = 0
        tr_free_a = [0, 0]
        wqs_free_p = 0
        trv = [tb[:, 0:128] for tb in bk_tr]
        for oc in range(NC_D):
            @b.sync
            def _(e, oc=oc):
                if slot_rd:
                    wV(e, k, slot_rd)
                s_wld[0].inc(e.dma_start(
                    out=wstg[:], in_=w2[oc * 128:(oc + 1) * 128, :]), 16)
            ldn = s_wld[0].n

            @b.vector
            def _(e, oc=oc, ldn=ldn, wqs_free_p=wqs_free_p):
                e.wait_ge(s_wld[0].h, ldn)
                wV(e, k)
                if wqs_free_p:
                    wP(e, k, wqs_free_p)
                k.V.inc(e.reduce_max(sw[:, oc:oc + 1], wstg[:], axis=AX.X,
                                     apply_absolute_value=True))
                wV(e, k)
                k.V.inc(e.tensor_scalar(sw[:, oc:oc + 1], sw[:, oc:oc + 1],
                                        1e-8, 1.0 / 127.0, ALU.max, ALU.mult))
                wV(e, k)
                k.V.inc(e.reciprocal(rcol[:], sw[:, oc:oc + 1]))
                wV(e, k)
                k.V.inc(e.tensor_scalar(wstg[:], wstg[:], rcol[:], 127.0,
                                        ALU.mult, ALU.min))
                wV(e, k)
                k.V.inc(e.tensor_scalar(wstg[:], wstg[:], -128.0,
                                        cst["ceff"][:], ALU.max, ALU.add))
                wV(e, k)
                k.V.inc(e.tensor_scalar(wstg[:], wstg[:], cst["ceff"][:],
                                        None, ALU.subtract))
                wV(e, k)
                k.V.inc(e.tensor_scalar(wqs[:], wstg[:], sw[:, oc:oc + 1],
                                        None, ALU.mult))
            slot_rd = k.V.n
            v_wq = k.V.n

            for fc in range(NC_F):
                @b.tensor
                def _(e, oc=oc, fc=fc, v_wq=v_wq):
                    if fc == 0:
                        wV(e, k, max(v_wq, cst["v_done"]))
                    if tr_free_a[fc % 2]:
                        wV(e, k, tr_free_a[fc % 2])
                    k.P.inc(e.transpose(trv[fc % 2],
                                        wqs[:, fc * 128:(fc + 1) * 128],
                                        cst["ident"][:]))
                p_tr = k.P.n

                @b.vector
                def _(e, oc=oc, fc=fc, p_tr=p_tr):
                    wP(e, k, p_tr)
                    k.V.inc(e.tensor_copy(
                        wq2T[:, fc * D + oc * 128:fc * D + (oc + 1) * 128],
                        trv[fc % 2]))
                tr_free_a[fc % 2] = k.V.n
            wqs_free_p = k.P.n
        a_wq = k.V.n

        @b.vector
        def _(e):
            wV(e, k, a_wq)
            wV(e, k, v_sh)
            for oc in range(NC_D):
                k.V.inc(e.reciprocal(rcol[:], sw[:, oc:oc + 1]))
                wV(e, k)
                k.V.inc(e.tensor_scalar(bq[:, oc:oc + 1], bq[:, oc:oc + 1],
                                        scol[:, 1:2], rcol[:], ALU.mult,
                                        ALU.mult))
                wV(e, k)
                k.V.inc(e.tensor_scalar(bq[:, oc:oc + 1], bq[:, oc:oc + 1],
                                        cst["ceff"][:], cst["ceff"][:],
                                        ALU.add, ALU.subtract))
                wV(e, k)
                k.V.inc(e.tensor_scalar(bq[:, oc:oc + 1], bq[:, oc:oc + 1],
                                        scol[:, 0:1], sw[:, oc:oc + 1],
                                        ALU.mult, ALU.mult))
                wV(e, k)
        v_bq = k.V.n
        wes.close()
        hq = k.sb("hq", [128, NC_F * 512], BF16)
        qscr = k.sb("qscr", [128, 512])

        epi = Epilogue(k, x_in, x_out, init_wait_v=v_bq)
        s_hld = [k.sem("hld0"), k.sem("hld1")]
        hq_rd_p = [0, 0]
        m2_free_v = 0
        foutT_free_p = 0
        fout_free_v = 0

        for tci in range(NTT // 4):
            t0 = tci * 512
            hsl = tci % 2

            @b.sync
            def _(e, t0=t0, hsl=hsl, tci=tci):
                if hq_rd_p[hsl]:
                    wP(e, k, hq_rd_p[hsl])
                if tci == 0:
                    wV(e, k, v_bq)
                for fc in range(NC_F):
                    s_hld[hsl].inc(e.dma_start(
                        out=hq[:, fc * 512:(fc + 1) * 512],
                        in_=h_in[fc, :, t0:t0 + 512]), 16)
            ldn = s_hld[hsl].n

            for fc in range(NC_F):
                @b.vector
                def _(e, fc=fc, hsl=hsl, ldn=ldn):
                    if fc == 0:
                        e.wait_ge(s_hld[hsl].h, ldn)
                    wV(e, k)
                    hqs = hq[:, fc * 512:(fc + 1) * 512]
                    k.V.inc(e.tensor_scalar(qscr[:], hqs, scol[:, 1:2], 255.0,
                                            ALU.mult, ALU.min))
                    wV(e, k)
                    k.V.inc(e.tensor_scalar(qscr[:], qscr[:], 0.0,
                                            cst["ceff"][:], ALU.max, ALU.add))
                    wV(e, k)
                    k.V.inc(e.tensor_scalar(hqs, qscr[:], cst["ceff"][:], None,
                                            ALU.subtract))
            v_q_all = k.V.n

            for oc in range(NC_D):
                bk = bk_m2[oc % 2]

                @b.tensor
                def _(e, oc=oc, bk=bk, v_q_all=v_q_all):
                    if oc == 0:
                        wV(e, k, v_q_all)
                    if m2_free_v:
                        wV(e, k, m2_free_v)
                    for fc in range(NC_F):
                        k.P.inc(e.matmul(
                            bk[:, 0:512],
                            wq2T[:, fc * D + oc * 128:fc * D + (oc + 1) * 128],
                            hq[:, fc * 512:(fc + 1) * 512],
                            start=(fc == 0), stop=(fc == NC_F - 1)))
                p_m2 = k.P.n
                if oc == NC_D - 1:
                    hq_rd_p[hsl] = p_m2
                    hq_rd_p[1 - hsl] = p_m2

                @b.vector
                def _(e, oc=oc, bk=bk, p_m2=p_m2):
                    wP(e, k, p_m2)
                    if foutT_free_p:
                        wP(e, k, foutT_free_p)
                    k.V.inc(e.tensor_scalar(
                        foutT[:, oc * 512:(oc + 1) * 512], bk[:, 0:512],
                        scol[:, 0:1], bq[:, oc:oc + 1], ALU.mult, ALU.add))
                m2_free_v = k.V.n
            v_foutT = k.V.n

            trc_free_v = [0, 0]
            for j in range(4):
                i = 4 * tci + j

                for c in range(NC_D):
                    @b.tensor
                    def _(e, j=j, c=c, v_foutT=v_foutT, tfv=trc_free_v[c % 2]):
                        if c == 0 and j == 0:
                            wV(e, k, v_foutT)
                        if c == 0 and fout_free_v:
                            wV(e, k, fout_free_v)
                        if tfv:
                            wV(e, k, tfv)
                        k.P.inc(e.transpose(
                            bk_tr[c % 2][:, 0:128],
                            foutT[:, c * 512 + j * 128:c * 512 + (j + 1) * 128],
                            cst["ident"][:]))
                    p_tr = k.P.n

                    @b.vector
                    def _(e, c=c, p_tr=p_tr):
                        wP(e, k, p_tr)
                        wV(e, k)
                        k.V.inc(e.tensor_copy(fout[:, c * 128:(c + 1) * 128],
                                              bk_tr[c % 2][:, 0:128]))
                    trc_free_v[c % 2] = k.V.n
                v_f = k.V.n
                if j == 3:
                    foutT_free_p = k.P.n

                epi.tile(b, cst, i, fout[:, 0:D],
                         lambda e, v_f=v_f: wV(e, k, v_f))
                fout_free_v = k.V.n

        epi.finish(b)

    k.close()
    return nc



def build_L1a():
    k = K("l1a")
    nc = k.nc
    SB = S
    x_in = k.din("x", [T, D])
    wqT = k.din("wqT", [D, D])
    wkT = k.din("wkT", [D, D])
    wvT = k.din("wvT", [D, D])
    woT = k.din("woT", [D, D])
    bqc = k.din("bqc", [128, 8])
    bkc = k.din("bkc", [128, 8])
    bvr = k.din("bvr", [1, D])
    bor = k.din("bor", [1, D])
    x_out = k.dout("x_new", [T, D])

    xlnT = k.sb("xlnT", [128, NC_D * SB], BF16)
    qT = k.sb("qT", [128, NC_D * SB], BF16)
    kTb = k.sb("kTb", [128, NC_D * SB], BF16)
    vp = k.sb("vp", [128, 16 * 16 * 65], BF16)
    wb = k.sb("wb", [128, NC_D * D], BF16)
    attnT = k.sb("attnT", [128, NC_D * 512], BF16)
    ptile = k.sb("ptile", [128, 4 * 512], BF16)
    rlb = k.sb("rlb", [64, 2 * 512])
    lr = k.sb("lr", [1, 2 * 512])
    sqn = k.sb("sqn", [128, 512], BF16)
    bcin = k.sb("bcin", [128, 8])
    bvrow = k.sb("bvrow", [1, D])
    borow = k.sb("borow", [1, D])
    bd2 = k.sb("bd2", [128, 2], BF16)
    nn2 = k.sb("nn2", [2, 3])
    nr = k.sb("nr", [1, 4])
    negm = k.sb("negm", [128, 1])
    fout = k.sb("fout", [128, D])

    bk_tr = [k.bankb("tr0"), k.bankb("tr1")]
    bk_sc = [k.bank("sc0"), k.bank("sc1"), k.bank("sc2")]
    bk_pv = [k.bank("pv0"), k.bank("pv1")]
    bk_bc = k.bank("bc")
    bk_op = [bk_sc[0], bk_sc[1]]

    with nc.Block() as b:
        cst = setup_consts(k, b, need_identf=True)

        s_rows = k.sem("s_rows")
        s_bcin = k.sem("s_bcin")

        @b.sync
        def _(e):
            s_rows.inc(e.dma_start(out=bvrow[:], in_=bvr), 16)
            s_rows.inc(e.dma_start(out=borow[:], in_=bor), 16)
        sy_rows = s_rows.n

        @b.vector
        def _(e):
            wV(e, k, cst["v_done"])
            k.V.inc(e.tensor_scalar(bd2[:, 0:1], cst["iot"][:, 64:65], 0,
                                    None, ALU.is_gt))
            wV(e, k)
            k.V.inc(e.tensor_scalar(bd2[:, 1:2], bd2[:, 0:1], -1.0, 1.0,
                                    ALU.mult, ALU.add))
        v_bd2 = k.V.n

        pro = Prologue(k, x_in, xlnT, bk_tr,
                       col_of=lambda c, i: c * SB + (i % 16) * TT)
        epi = Epilogue(k, x_in, x_out)

        s_wldq = k.sem("wldq")
        xln_rd_p = 0
        attn_rd_p = 0
        fout_free_v = 0
        proj_ev_v = 0
        sc_free = [None, None, None]
        pt_free_p = [0, 0, 0, 0]
        pv_free_v = [0, 0]
        bc_free_a = 0
        bc_free = 0
        rlb_free_v = 0
        lr_free_p = 0
        attnT_free_p = 0

        for bi in range(B_PER):
            tb0 = bi * SB
            for i in range(bi * 16, bi * 16 + 16):
                ew = (lambda e, n=xln_rd_p: wP(e, k, n)) if xln_rd_p else None
                pro.tile(b, cst, i, evac_wait=ew)
            ev_all = pro.ev_a[bi * 16 + 15]

            def load_w(wsrc):
                @b.gpsimd
                def _(e, wsrc=wsrc):
                    wP(e, k)
                    s_wldq.inc(e.dma_start(
                        out=wb[:].rearrange("p (c n) -> p c n", c=NC_D),
                        in_=wsrc.rearrange("(c p) n -> p c n", p=128)), 16)
                return s_wldq.n

            for pi, (dst, wsrc, bsrc) in enumerate(
                    ((qT, wqT, bqc), (kTb, wkT, bkc))):
                ldw = load_w(wsrc)

                @b.sync
                def _(e, bsrc=bsrc):
                    wV(e, k)
                    s_bcin.inc(e.dma_start(out=bcin[:], in_=bsrc), 16)
                sy_bc = s_bcin.n

                for ec in range(NC_D):
                    for tc in range(4):
                        sl = (ec * 4 + tc) % 2
                        bk2 = bk_sc[sl]

                        @b.tensor
                        def _(e, ec=ec, tc=tc, bk2=bk2, ldw=ldw,
                              first=(ec == 0 and tc == 0),
                              arp=attn_rd_p, scf=sc_free[sl]):
                            if first:
                                wV(e, k, ev_all)
                                e.wait_ge(s_wldq.h, ldw)
                                if arp:
                                    wP(e, k, arp)
                            wT(e, k, scf)
                            for c in range(NC_D):
                                k.P.inc(e.matmul(
                                    bk2[:, 0:512],
                                    wb[:, c * D + ec * 128:c * D + (ec + 1) * 128],
                                    xlnT[:, c * SB + tc * 512:
                                         c * SB + (tc + 1) * 512],
                                    start=(c == 0), stop=(c == NC_D - 1)))
                        p_pj = k.P.n

                        @b.vector
                        def _(e, ec=ec, tc=tc, bk2=bk2, p_pj=p_pj, dst=dst,
                              sy_bc=sy_bc):
                            wP(e, k, p_pj)
                            e.wait_ge(s_bcin.h, sy_bc)
                            k.V.inc(e.tensor_scalar(
                                dst[:, ec * SB + tc * 512:
                                    ec * SB + (tc + 1) * 512],
                                bk2[:, 0:512], bcin[:, ec:ec + 1], None,
                                ALU.add))
                        sc_free[sl] = ("V", k.V.n)
                proj_ev_v = k.V.n

            ldw = load_w(wvT)

            @b.vector
            def _(e):
                wP(e, k)
                wV(e, k)
                k.V.inc(e.memset(vp[:], 1.0))
            v_vp0 = k.V.n

            for tau in range(16):
                for half in range(2):
                    sl = (tau * 2 + half) % 2
                    bk2 = bk_sc[sl]

                    @b.tensor
                    def _(e, tau=tau, half=half, bk2=bk2, ldw=ldw,
                          v_vp0=v_vp0, scf=sc_free[sl]):
                        if tau == 0 and half == 0:
                            e.wait_ge(s_wldq.h, ldw)
                            wV(e, k, v_vp0)
                            e.wait_ge(s_rows.h, sy_rows)
                        wT(e, k, scf)
                        for c in range(NC_D):
                            e.matmul(
                                bk2[:, 0:512],
                                xlnT[:, c * SB + tau * 128:
                                     c * SB + (tau + 1) * 128],
                                wb[:, c * D + half * 512:c * D + (half + 1) * 512],
                                start=(c == 0), stop=False)
                        k.P.inc(e.matmul(
                            bk2[0:128, 0:512], cst["ones_rf"][0:1, 0:128],
                            bvrow[:, half * 512:(half + 1) * 512],
                            start=False, stop=True))
                    p_v = k.P.n

                    @b.vector
                    def _(e, tau=tau, half=half, bk2=bk2, p_v=p_v):
                        wP(e, k, p_v)
                        k.V.inc(e.tensor_copy(
                            vp[:, (tau * 16 + half * 8) * 65:
                               (tau * 16 + half * 8 + 8) * 65]
                            .rearrange("p (h j) -> p h j", h=8)[:, :, 0:64],
                            bk2[:, 0:512]
                            .rearrange("p (h j) -> p h j", h=8)))
                    sc_free[sl] = ("V", k.V.n)
            v_vp = k.V.n
            xln_rd_p = k.P.n

            @b.vector
            def _(e):
                wV(e, k, v_bd2)
                k.V.inc(e.memset(nn2[:, 0:2], 0.0))
            for (src_t, col) in ((qT, 0), (kTb, 1)):
                for ch in range(NC_D * 4):
                    @b.scalar
                    def _(e, src_t=src_t, ch=ch):
                        wV(e, k)
                        k.A.inc(e.activation(
                            sqn[:], src_t[:, ch * 512:(ch + 1) * 512],
                            AF.Square))
                    a_sq = k.A.n

                    @b.tensor
                    def _(e, a_sq=a_sq):
                        wA(e, k, a_sq)
                        wV(e, k)
                        k.P.inc(e.matmul(bk_bc[0:2, 0:512], bd2[:, 0:2],
                                         sqn[:], start=True, stop=True))
                    p_nn = k.P.n

                    @b.vector
                    def _(e, col=col, p_nn=p_nn):
                        wP(e, k, p_nn)
                        k.V.inc(e.reduce_max(nn2[0:2, 2:3], bk_bc[0:2, 0:512],
                                             axis=AX.X))
                        wV(e, k)
                        k.V.inc(e.tensor_max(nn2[0:2, col:col + 1],
                                             nn2[0:2, col:col + 1],
                                             nn2[0:2, 2:3]))
            v_nn = k.V.n

            @b.tensor
            def _(e):
                wV(e, k, v_nn)
                k.P.inc(e.transpose(bk_bc[0:2, 0:2], nn2[0:2, 0:2],
                                    cst["identf"][0:2, 0:2]))
            p_tr2 = k.P.n

            @b.vector
            def _(e, p_tr2=p_tr2):
                wP(e, k, p_tr2)
                k.V.inc(e.reduce_max(nr[:, 0:1], bk_bc[0:1, 0:2], axis=AX.X))
            @b.tensor
            def _(e):
                wV(e, k)
                k.P.inc(e.transpose(bk_bc[0:2, 4:6], nn2[0:2, 1:3],
                                    cst["identf"][0:2, 0:2]))
            p_tr3 = k.P.n

            @b.vector
            def _(e, p_tr3=p_tr3):
                wP(e, k, p_tr3)
                k.V.inc(e.reduce_max(nr[:, 1:2], bk_bc[0:1, 4:6], axis=AX.X))
                wV(e, k)
                k.V.inc(e.tensor_mul(nr[:, 2:3], nr[:, 0:1], nr[:, 1:2]))

            @b.scalar
            def _(e):
                wV(e, k)
                k.A.inc(e.activation(nr[:, 3:4], nr[:, 2:3], AF.Sqrt))

            @b.vector
            def _(e):
                wA(e, k)
                k.V.inc(e.tensor_scalar(nr[:, 3:4], nr[:, 3:4],
                                        -0.125 * 1.0005, None, ALU.mult))
            v_negm = emit_bcast_col(k, b, cst, nr[:, 3:4], negm[:], bk_bc,
                                    k.V.n, n=1)

            ldwo = load_w(woT)

            for sg in range(4):
                for h in range(16):
                    hc, hp = h // 2, (h % 2) * 64
                    hs = h % 2
                    pvb = bk_pv[hs]
                    ntau = 4 * sg + 4
                    p0, a0, g0 = k.P.n, k.A.n, k.G.n
                    recs = []
                    scf_l = list(sc_free)
                    ptw_l = list(pt_free_p)
                    gi = 0
                    for tau in range(ntau):
                        diag = tau >= 4 * sg
                        sl = tau % 3
                        psl = 3 if diag else sl
                        r = dict(tau=tau, diag=diag, sl=sl, psl=psl,
                                 scf=scf_l[sl], ptw=ptw_l[psl],
                                 p_qk=p0 + tau + 1, a_exp=a0 + tau + 1,
                                 p_pv=p0 + ntau + tau + 1, g_msk=0)
                        if diag:
                            gi += 1
                            r["g_msk"] = g0 + gi
                        recs.append(r)
                        scf_l[sl] = ("A", r["a_exp"])
                        ptw_l[psl] = r["p_pv"]
                    first_attn = (h == 0 and sg == 0)
                    pvf = pv_free_v[hs]
                    order = []
                    for j in range(ntau + 2):
                        if j < ntau:
                            order.append(("qk", j))
                        if j >= 2:
                            order.append(("pv", j - 2))
                    for idx, (kind, j) in enumerate(order):
                        recs[j]["p_" + kind] = p0 + idx + 1
                    for r in recs:
                        r["p_pv"] = r["p_pv"]
                    ptw_l2 = list(pt_free_p)
                    for r in recs:
                        r["ptw"] = ptw_l2[r["psl"]]
                        ptw_l2[r["psl"]] = r["p_pv"]
                    ptw_fin = ptw_l2

                    @b.tensor
                    def _(e, recs=recs, order=order, hc=hc, hp=hp, sg=sg,
                          h=h, pvb=pvb, pvf=pvf, first=first_attn):
                        if first:
                            wV(e, k, v_negm)
                        for kind, j in order:
                            r = recs[j]
                            tau = r["tau"]
                            if kind == "qk":
                                wT(e, k, r["scf"])
                                e.matmul(
                                    bk_sc[r["sl"]][0:128, 0:512],
                                    kTb[hp:hp + 64,
                                        hc * SB + tau * 128:
                                        hc * SB + (tau + 1) * 128],
                                    qT[hp:hp + 64,
                                       hc * SB + sg * 512:
                                       hc * SB + (sg + 1) * 512],
                                    start=True, stop=True).then_inc(k.P.h, 1)
                            else:
                                if r["diag"]:
                                    wG(e, k, r["g_msk"])
                                else:
                                    wA(e, k, r["a_exp"])
                                if tau == 0 and pvf:
                                    wV(e, k, pvf)
                                e.matmul(
                                    pvb[0:65, 0:512],
                                    vp[:, (tau * 16 + h) * 65:
                                       (tau * 16 + h + 1) * 65],
                                    ptile[:, r["psl"] * 512:
                                          (r["psl"] + 1) * 512],
                                    start=(tau == 0),
                                    stop=(tau == len(recs) - 1)) \
                                    .then_inc(k.P.h, 1)

                    @b.scalar
                    def _(e, recs=recs):
                        for r in recs:
                            e.wait_ge(k.P.h, r["p_qk"])
                            if r["ptw"]:
                                wP(e, k, r["ptw"])
                            e.activation(
                                ptile[:, r["psl"] * 512:(r["psl"] + 1) * 512],
                                bk_sc[r["sl"]][0:128, 0:512], AF.Exp,
                                bias=negm[:],
                                scale=0.125).then_inc(k.A.h, 1)

                    @b.gpsimd
                    def _(e, recs=recs, sg=sg):
                        for r in recs:
                            if r["diag"]:
                                wA(e, k, r["a_exp"])
                                e.affine_select(
                                    ptile[:, 3 * 512:4 * 512],
                                    ptile[:, 3 * 512:4 * 512],
                                    [[1, 512]], ALU.is_ge, 0.0,
                                    base=sg * 512 - r["tau"] * 128,
                                    channel_multiplier=-1).then_inc(k.G.h, 1)

                    k.P.n = p0 + 2 * ntau
                    k.A.n = a0 + ntau
                    k.G.n = g0 + gi
                    sc_free[:] = scf_l
                    pt_free_p[:] = ptw_fin
                    p_pv = k.P.n

                    lrs = lr[:, hs * 512:(hs + 1) * 512]
                    rlbs = rlb[:, hs * 512:(hs + 1) * 512]

                    @b.vector
                    def _(e, p_pv=p_pv, lfp=lr_free_p, pvb=pvb, lrs=lrs):
                        wP(e, k, p_pv)
                        if lfp:
                            wP(e, k, lfp)
                        k.V.inc(e.tensor_copy(lrs, pvb[64:65, 0:512]))
                    v_lrc = k.V.n

                    @b.scalar
                    def _(e, v_lrc=v_lrc, lrs=lrs):
                        wV(e, k, v_lrc)
                        k.A.inc(e.activation(lrs, lrs, AF.Ln))
                        wA(e, k)
                        k.A.inc(e.activation(lrs, lrs, AF.Exp, scale=-1.0))
                    a_rl = k.A.n

                    @b.tensor
                    def _(e, a_rl=a_rl, lrs=lrs, bca=bc_free_a):
                        wA(e, k, a_rl)
                        if bca:
                            wV(e, k, bca)
                        k.P.inc(e.matmul(bk_bc[0:64, 0:512],
                                         cst["ones_rf"][0:1, 0:64], lrs,
                                         start=True, stop=True))
                    lr_free_p = k.P.n
                    p_bc = k.P.n

                    @b.vector
                    def _(e, p_bc=p_bc, rfv=rlb_free_v, rlbs=rlbs):
                        wP(e, k, p_bc)
                        if rfv:
                            wV(e, k, rfv)
                        k.V.inc(e.tensor_copy(rlbs, bk_bc[0:64, 0:512]))
                    v_rlb = k.V.n
                    bc_free_a = v_rlb

                    @b.vector
                    def _(e, h=h, hc=hc, hp=hp, pvb=pvb,
                          rlbs=rlbs, atf=attnT_free_p):
                        wV(e, k)
                        if atf and h == 0:
                            wP(e, k, atf)
                        k.V.inc(e.tensor_mul(
                            attnT[hp:hp + 64, hc * 512:(hc + 1) * 512],
                            pvb[0:64, 0:512], rlbs))
                    rlb_free_v = k.V.n
                    pv_free_v[hs] = k.V.n
                v_at = k.V.n

                for j in range(4):
                    @b.tensor
                    def _(e, j=j, sg=sg, v_at=v_at, ofv=fout_free_v,
                          s0=sc_free[0], s1=sc_free[1], ldwo=ldwo):
                        if j == 0:
                            wV(e, k, v_at)
                            wT(e, k, s0)
                            wT(e, k, s1)
                            if sg == 0:
                                e.wait_ge(s_wldq.h, ldwo)
                        if ofv:
                            wV(e, k, ofv)
                        for half in range(2):
                            bko = bk_op[half]
                            for c in range(NC_D):
                                e.matmul(
                                    bko[:, 0:512],
                                    attnT[:, c * 512 + j * 128:
                                          c * 512 + (j + 1) * 128],
                                    wb[:, c * D + half * 512:
                                       c * D + (half + 1) * 512],
                                    start=(c == 0), stop=False)
                            k.P.inc(e.matmul(
                                bko[0:128, 0:512], cst["ones_rf"][0:1, 0:128],
                                borow[:, half * 512:(half + 1) * 512],
                                start=False, stop=True))
                    p_op = k.P.n

                    @b.vector
                    def _(e, p_op=p_op, ffv=fout_free_v):
                        wP(e, k, p_op)
                        if ffv:
                            wV(e, k, ffv)
                        k.V.inc(e.tensor_copy(fout[:, 0:512],
                                              bk_op[0][:, 0:512]))
                        k.V.inc(e.tensor_copy(fout[:, 512:1024],
                                              bk_op[1][:, 0:512]))
                    v_f = k.V.n
                    sc_free[0] = ("V", v_f)
                    sc_free[1] = ("V", v_f)

                    i_tile = bi * 16 + sg * 4 + j
                    epi.tile(b, cst, i_tile, fout[:, 0:D],
                             lambda e, v_f=v_f: wV(e, k, v_f))
                    fout_free_v = k.V.n
                attnT_free_p = k.P.n
            attn_rd_p = k.P.n

        epi.finish(b)

    k.close()
    return nc


def run_L1a(x_full, Wq, bq, Wk, bk_, Wv, bv, Wo, bo):
    nc = build_L1a()
    shards = np_shard_x(x_full)
    cc = np.ascontiguousarray
    com = {
        "wqT": cc(np.asarray(Wq).T), "wkT": cc(np.asarray(Wk).T),
        "wvT": cc(np.asarray(Wv).T), "woT": cc(np.asarray(Wo).T),
        "bqc": cc(np.asarray(bq).reshape(8, 128).T),
        "bkc": cc(np.asarray(bk_).reshape(8, 128).T),
        "bvr": cc(np.asarray(bv).reshape(1, D)),
        "bor": cc(np.asarray(bo).reshape(1, D)),
    }
    in_maps = [dict(com, x=s) for s in shards]
    res = run_bass_kernel_spmd(nc, in_maps, core_ids=list(range(NCORES)))
    return np.concatenate([r["x_new"].reshape(B_PER, S, D)
                           for r in res.results], axis=0)



def build_L0a():
    k = K("l0a")
    nc = k.nc
    SB = S
    x_in = k.din("x", [T, D])
    wq = k.din("wq", [D, D])
    wk = k.din("wk", [D, D])
    wv = k.din("wv", [D, D])
    wo = k.din("wo", [D, D])
    bqc = k.din("bqc", [128, 8])
    bkc = k.din("bkc", [128, 8])
    bvc = k.din("bvc", [128, 8])
    bor = k.din("bor", [1, D])
    x_out = k.dout("x_new", [T, D])

    xlnT = k.sb("xlnT", [128, NC_D * SB], BF16)
    qT = k.sb("qT", [128, NC_D * SB], BF16)
    kT = k.sb("kT", [128, NC_D * SB], BF16)
    vT = k.sb("vT", [128, NC_D * SB], BF16)
    wb = k.sb("wb", [128, NC_D * D], BF16)
    attnT = k.sb("attnT", [128, NC_D * 512], BF16)
    scr1 = k.sb("scr1", [128, 512])
    scr2 = k.sb("scr2", [128, 512])
    scr3 = k.sb("scr3", [128, 512])
    kvc = k.sb("kvc", [128, 8])
    ksc = k.sb("ksc", [128, 8])
    rc = k.sb("rc", [128, 1])
    bcin = k.sb("bcin", [128, 24])
    borow = k.sb("borow", [1, D])
    fout = k.sb("fout", [128, D])

    bk_tr = [k.bankb("tr0"), k.bankb("tr1")]
    bk_pj = [k.bank("pj0"), k.bank("pj1")]
    bk_op = [k.bank("op0"), k.bank("op1")]

    with nc.Block() as b:
        cst = setup_consts(k, b)

        s_rows = k.sem("s_rows")
        s_wldq = k.sem("wldq")

        @b.sync
        def _(e):
            s_rows.inc(e.dma_start(out=borow[:], in_=bor), 16)
            s_rows.inc(e.dma_start(out=bcin[:, 0:8], in_=bqc), 16)
            s_rows.inc(e.dma_start(out=bcin[:, 8:16], in_=bkc), 16)
            s_rows.inc(e.dma_start(out=bcin[:, 16:24], in_=bvc), 16)
        sy_rows = s_rows.n

        pro = Prologue(k, x_in, xlnT, bk_tr,
                       col_of=lambda c, i: c * SB + (i % 16) * TT)
        epi = Epilogue(k, x_in, x_out)

        xln_rd_p = 0
        fout_free_v = 0
        pj_free = [None, None]
        scr_busy_p = 0

        for bi in range(B_PER):
            for i in range(bi * 16, bi * 16 + 16):
                ew = (lambda e, n=xln_rd_p: wP(e, k, n)) if xln_rd_p else None
                pro.tile(b, cst, i, evac_wait=ew)
            ev_all = pro.ev_a[bi * 16 + 15]

            def load_w(wsrc):
                @b.gpsimd
                def _(e, wsrc=wsrc):
                    wP(e, k)
                    s_wldq.inc(e.dma_start(
                        out=wb[:].rearrange("p (c n) -> p c n", c=NC_D),
                        in_=wsrc.rearrange("(c p) n -> p c n", p=128)), 16)
                return s_wldq.n

            for pi, (dst, wsrc, bco, fmap) in enumerate(
                    ((qT, wq, 0, True), (kT, wk, 8, True), (vT, wv, 16, False))):
                ldw = load_w(wsrc)
                for ec in range(NC_D):
                    for tc in range(4):
                        sl = (ec * 4 + tc) % 2
                        bk2 = bk_pj[sl]

                        @b.tensor
                        def _(e, ec=ec, tc=tc, bk2=bk2, ldw=ldw,
                              first=(ec == 0 and tc == 0), arp=0,
                              pjf=pj_free[sl]):
                            if first:
                                wV(e, k, ev_all)
                                e.wait_ge(s_wldq.h, ldw)
                                e.wait_ge(s_rows.h, sy_rows)
                            wT(e, k, pjf)
                            for c in range(NC_D):
                                k.P.inc(e.matmul(
                                    bk2[:, 0:512],
                                    wb[:, c * D + ec * 128:c * D + (ec + 1) * 128],
                                    xlnT[:, c * SB + tc * 512:
                                         c * SB + (tc + 1) * 512],
                                    start=(c == 0), stop=(c == NC_D - 1)))
                        p_pj = k.P.n

                        if fmap:
                            @b.vector
                            def _(e, ec=ec, bk2=bk2, p_pj=p_pj, bco=bco):
                                wP(e, k, p_pj)
                                wV(e, k)
                                k.V.inc(e.tensor_scalar(
                                    scr1[:], bk2[:, 0:512],
                                    bcin[:, bco + ec:bco + ec + 1], None,
                                    ALU.add))
                                wV(e, k)
                                k.V.inc(e.tensor_scalar(scr2[:], scr1[:], 0.0,
                                                        None, ALU.min))

                            @b.scalar
                            def _(e):
                                wV(e, k)
                                k.A.inc(e.activation(scr2[:], scr2[:], AF.Exp))

                            @b.vector
                            def _(e, dst=dst, ec=ec, tc=tc):
                                wA(e, k)
                                k.V.inc(e.tensor_scalar(scr1[:], scr1[:], 0.0,
                                                        None, ALU.max))
                                wV(e, k)
                                k.V.inc(e.tensor_add(
                                    dst[:, ec * SB + tc * 512:
                                        ec * SB + (tc + 1) * 512],
                                    scr1[:], scr2[:]))
                        else:
                            @b.vector
                            def _(e, dst=dst, ec=ec, tc=tc, bk2=bk2,
                                  p_pj=p_pj, bco=bco):
                                wP(e, k, p_pj)
                                wV(e, k)
                                k.V.inc(e.tensor_scalar(
                                    dst[:, ec * SB + tc * 512:
                                        ec * SB + (tc + 1) * 512],
                                    bk2[:, 0:512],
                                    bcin[:, bco + ec:bco + ec + 1], None,
                                    ALU.add))
                        pj_free[sl] = ("V", k.V.n)
            xln_rd_p = k.P.n

            @b.vector
            def _(e):
                wV(e, k)
                k.V.inc(e.memset(kvc[:], 0.0))
                k.V.inc(e.memset(ksc[:], 0.0))
            for c in range(NC_D):
                for tc in range(4):
                    @b.vector
                    def _(e, c=c, tc=tc):
                        wV(e, k)
                        ks_ = kT[:, c * SB + tc * 512:c * SB + (tc + 1) * 512]
                        vs_ = vT[:, c * SB + tc * 512:c * SB + (tc + 1) * 512]
                        k.V.inc(e.tensor_mul(scr1[:], ks_, vs_))
                        wV(e, k)
                        k.V.inc(e.reduce_sum(rc[:], scr1[:], axis=AX.X))
                        wV(e, k)
                        k.V.inc(e.tensor_add(kvc[:, c:c + 1], kvc[:, c:c + 1],
                                             rc[:]))
                        wV(e, k)
                        k.V.inc(e.reduce_sum(rc[:], ks_, axis=AX.X))
                        wV(e, k)
                        k.V.inc(e.tensor_add(ksc[:, c:c + 1], ksc[:, c:c + 1],
                                             rc[:]))
            v_kv = k.V.n

            ldwo = load_w(wo)
            for tc in range(4):
                for c in range(NC_D):
                    @b.vector
                    def _(e, c=c, tc=tc, sbp=scr_busy_p):
                        wV(e, k, v_kv)
                        wV(e, k)
                        if sbp and c == 0:
                            wP(e, k, sbp)
                        qs = qT[:, c * SB + tc * 512:c * SB + (tc + 1) * 512]
                        k.V.inc(e.tensor_scalar(scr1[:], qs, ksc[:, c:c + 1],
                                                1e-6, ALU.mult, ALU.add))

                    @b.scalar
                    def _(e):
                        wV(e, k)
                        k.A.inc(e.activation(scr1[:], scr1[:], AF.Ln))
                        wA(e, k)
                        k.A.inc(e.activation(scr1[:], scr1[:], AF.Exp,
                                             scale=-1.0))

                    @b.vector
                    def _(e, c=c, tc=tc):
                        wA(e, k)
                        qs = qT[:, c * SB + tc * 512:c * SB + (tc + 1) * 512]
                        k.V.inc(e.tensor_scalar(scr2[:], qs, kvc[:, c:c + 1],
                                                None, ALU.mult))
                        wV(e, k)
                        k.V.inc(e.tensor_mul(attnT[:, c * 512:(c + 1) * 512],
                                             scr2[:], scr1[:]))
                v_at = k.V.n

                for j in range(4):
                    @b.tensor
                    def _(e, j=j, tc=tc, v_at=v_at, ofv=fout_free_v,
                          ldwo=ldwo):
                        if j == 0:
                            wV(e, k, v_at)
                            if tc == 0:
                                e.wait_ge(s_wldq.h, ldwo)
                        if ofv:
                            wV(e, k, ofv)
                        for half in range(2):
                            bko = bk_op[half]
                            for c in range(NC_D):
                                e.matmul(
                                    bko[:, 0:512],
                                    attnT[:, c * 512 + j * 128:
                                          c * 512 + (j + 1) * 128],
                                    wb[:, c * D + half * 512:
                                       c * D + (half + 1) * 512],
                                    start=(c == 0), stop=False)
                            k.P.inc(e.matmul(
                                bko[0:128, 0:512], cst["ones_rf"][0:1, 0:128],
                                borow[:, half * 512:(half + 1) * 512],
                                start=False, stop=True))
                    p_op = k.P.n
                    if j == 3:
                        scr_busy_p = p_op

                    for half in range(2):
                        bko = bk_op[half]

                        @b.scalar
                        def _(e, bko=bko, p_op=p_op):
                            wP(e, k, p_op)
                            wV(e, k)
                            k.A.inc(e.activation(scr1[:], bko[:, 0:512],
                                                 AF.Square))
                        a_sq = k.A.n

                        @b.vector
                        def _(e, bko=bko, a_sq=a_sq):
                            wA(e, k, a_sq)
                            k.V.inc(e.tensor_scalar(scr1[:], scr1[:],
                                                    0.044715, 1.0, ALU.mult,
                                                    ALU.add))
                            wV(e, k)
                            k.V.inc(e.tensor_mul(scr2[:], scr1[:],
                                                 bko[:, 0:512]))

                        @b.scalar
                        def _(e):
                            wV(e, k)
                            k.A.inc(e.activation(scr2[:], scr2[:], AF.Tanh,
                                                 scale=0.7978845608028654))

                        @b.vector
                        def _(e, bko=bko, half=half):
                            wA(e, k)
                            k.V.inc(e.tensor_scalar(scr2[:], scr2[:], 1.0,
                                                    0.5, ALU.add, ALU.mult))
                            wV(e, k)
                            k.V.inc(e.tensor_mul(
                                fout[:, half * 512:(half + 1) * 512],
                                scr2[:], bko[:, 0:512]))
                    v_f = k.V.n

                    i_tile = bi * 16 + tc * 4 + j
                    epi.tile(b, cst, i_tile, fout[:, 0:D],
                             lambda e, v_f=v_f: wV(e, k, v_f))
                    fout_free_v = k.V.n

        epi.finish(b)

    k.close()
    return nc


def run_L0a(x_full, wq4, bq4, wk4, bk4, wv4, bv4, wo4, bo4):
    nc = build_L0a()
    shards = np_shard_x(x_full)
    cc = np.ascontiguousarray
    com = {
        "wq": cc(np.asarray(wq4).reshape(D, D)),
        "wk": cc(np.asarray(wk4).reshape(D, D)),
        "wv": cc(np.asarray(wv4).reshape(D, D)),
        "wo": cc(np.asarray(wo4).reshape(D, D)),
        "bqc": cc(np.asarray(bq4).reshape(8, 128).T),
        "bkc": cc(np.asarray(bk4).reshape(8, 128).T),
        "bvc": cc(np.asarray(bv4).reshape(8, 128).T),
        "bor": cc(np.asarray(bo4).reshape(1, D)),
    }
    in_maps = [dict(com, x=s) for s in shards]
    res = run_bass_kernel_spmd(nc, in_maps, core_ids=list(range(NCORES)))
    return np.concatenate([r["x_new"].reshape(B_PER, S, D)
                           for r in res.results], axis=0)



def build_L2a():
    k = K("l2a")
    nc = k.nc
    SB = S
    KV = 512
    x_in = k.din("x", [T, D])
    wq2 = k.din("wq2", [D, D])
    wkr = k.din("wkr", [D, D])
    wvr = k.din("wvr", [D, D])
    memi = k.din("memi", [KV, KV])
    memn = k.din("memn", [128, 4])
    memw = k.din("memw", [1, 1])
    x_out = k.dout("x_new", [T, D])

    xlnT = k.sb("xlnT", [128, NC_D * SB], BF16)
    qT = k.sb("qT", [128, NC_D * SB], BF16)
    kTr = k.sb("kTr", [128, NC_D * SB], BF16)
    vp = k.sb("vp", [128, 16 * 16 * 65], BF16)
    wb = k.sb("wb", [128, NC_D * D], BF16)
    attnT = k.sb("attnT", [128, NC_D * 512], BF16)
    ptile = k.sb("ptile", [128, 3 * 512], BF16)
    rlb = k.sb("rlb", [64, 2 * 512])
    rdb = k.sb("rdb", [128, 512])
    lr = k.sb("lr", [1, 2 * 512])
    dr = k.sb("dr", [1, 512])
    sigc = k.sb("sigc", [128, 4 * 512], BF16)
    scrA = k.sb("scrA", [128, 512])
    scrB = k.sb("scrB", [128, 512])
    memb = k.sb("memb", [128, 4 * KV], BF16)
    memnb = k.sb("memnb", [128, 4], BF16)
    sqn = k.sb("sqn", [128, 512], BF16)
    bd2 = k.sb("bd2", [128, 2], BF16)
    nn2 = k.sb("nn2", [2, 3])
    nr = k.sb("nr", [1, 4])
    negm = k.sb("negm", [128, 1])
    wrow = k.sb("wrow", [1, 2])
    fout = k.sb("fout", [128, D])

    bk_tr = [k.bankb("tr0"), k.bankb("tr1")]
    bk_sc = [k.bank("sc0"), k.bank("sc1"), k.bank("sc2")]
    bk_pv = [k.bank("pv0"), k.bank("pv1")]
    bk_bc = k.bank("bc")
    bk_nm = [bk_sc[0], bk_sc[1]]

    with nc.Block() as b:
        cst = setup_consts(k, b, need_identf=True)
        s_rows = k.sem("s_rows")
        s_wldq = k.sem("wldq")

        @b.gpsimd
        def _(e):
            k.GQ.inc(e.dma_start(
                out=memb[:].rearrange("p (c n) -> p c n", c=4),
                in_=memi.rearrange("(c p) n -> p c n", p=128)), 16)
            k.GQ.inc(e.dma_start(out=memnb[:], in_=memn), 16)
        gq_mem = k.GQ.n

        @b.sync
        def _(e):
            s_rows.inc(e.dma_start(out=wrow[:, 0:1], in_=memw), 16)
        sy_w = s_rows.n

        @b.scalar
        def _(e):
            e.wait_ge(s_rows.h, sy_w)
            k.A.inc(e.activation(wrow[:, 0:1], wrow[:, 0:1], AF.Sigmoid))

        @b.vector
        def _(e):
            wA(e, k)
            wV(e, k, cst["v_done"])
            k.V.inc(e.tensor_scalar(wrow[:, 1:2], wrow[:, 0:1], -1.0, 1.0,
                                    ALU.mult, ALU.add))
            k.V.inc(e.tensor_scalar(bd2[:, 0:1], cst["iot"][:, 64:65], 0,
                                    None, ALU.is_gt))
            wV(e, k)
            k.V.inc(e.tensor_scalar(bd2[:, 1:2], bd2[:, 0:1], -1.0, 1.0,
                                    ALU.mult, ALU.add))
        v_w = k.V.n

        pro = Prologue(k, x_in, xlnT, bk_tr,
                       col_of=lambda c, i: c * SB + (i % 16) * TT)
        epi = Epilogue(k, x_in, x_out)

        xln_rd_p = 0
        fout_free_v = 0
        sc_free = [None, None, None]
        pt_free_p = [0, 0, 0]
        pv_free_v = [0, 0]
        rlb_free_v = 0
        lr_free_p = 0
        bc_free_a = 0
        attnT_free_p = 0
        sig_rd_p = 0

        for bi in range(B_PER):
            for i in range(bi * 16, bi * 16 + 16):
                ew = (lambda e, n=xln_rd_p: wP(e, k, n)) if xln_rd_p else None
                pro.tile(b, cst, i, evac_wait=ew)
            ev_all = pro.ev_a[bi * 16 + 15]

            def load_w(wsrc):
                @b.gpsimd
                def _(e, wsrc=wsrc):
                    wP(e, k)
                    s_wldq.inc(e.dma_start(
                        out=wb[:].rearrange("p (c n) -> p c n", c=NC_D),
                        in_=wsrc.rearrange("(c p) n -> p c n", p=128)), 16)
                return s_wldq.n

            for pi, (dst, wsrc) in enumerate(((qT, wq2), (kTr, wkr))):
                ldw = load_w(wsrc)
                for ec in range(NC_D):
                    for tc in range(4):
                        sl = (ec * 4 + tc) % 2
                        bk2 = bk_sc[sl]

                        @b.tensor
                        def _(e, ec=ec, tc=tc, bk2=bk2, ldw=ldw,
                              first=(ec == 0 and tc == 0),
                              scf=sc_free[sl]):
                            if first:
                                wV(e, k, ev_all)
                                e.wait_ge(s_wldq.h, ldw)
                            wT(e, k, scf)
                            for c in range(NC_D):
                                k.P.inc(e.matmul(
                                    bk2[:, 0:512],
                                    wb[:, c * D + ec * 128:c * D + (ec + 1) * 128],
                                    xlnT[:, c * SB + tc * 512:
                                         c * SB + (tc + 1) * 512],
                                    start=(c == 0), stop=(c == NC_D - 1)))
                        p_pj = k.P.n

                        @b.vector
                        def _(e, ec=ec, tc=tc, bk2=bk2, p_pj=p_pj, dst=dst):
                            wP(e, k, p_pj)
                            k.V.inc(e.tensor_copy(
                                dst[:, ec * SB + tc * 512:
                                    ec * SB + (tc + 1) * 512],
                                bk2[:, 0:512]))
                        sc_free[sl] = ("V", k.V.n)

            ldw = load_w(wvr)

            @b.vector
            def _(e):
                wP(e, k)
                wV(e, k)
                k.V.inc(e.memset(vp[:], 1.0))
            v_vp0 = k.V.n

            for tau in range(16):
                for half in range(2):
                    sl = (tau * 2 + half) % 2
                    bk2 = bk_sc[sl]

                    @b.tensor
                    def _(e, tau=tau, half=half, bk2=bk2, ldw=ldw,
                          v_vp0=v_vp0, scf=sc_free[sl]):
                        if tau == 0 and half == 0:
                            e.wait_ge(s_wldq.h, ldw)
                            wV(e, k, v_vp0)
                        wT(e, k, scf)
                        for c in range(NC_D):
                            k.P.inc(e.matmul(
                                bk2[:, 0:512],
                                xlnT[:, c * SB + tau * 128:
                                     c * SB + (tau + 1) * 128],
                                wb[:, c * D + half * 512:c * D + (half + 1) * 512],
                                start=(c == 0), stop=(c == NC_D - 1)))
                    p_v = k.P.n

                    @b.vector
                    def _(e, tau=tau, half=half, bk2=bk2, p_v=p_v):
                        wP(e, k, p_v)
                        k.V.inc(e.tensor_copy(
                            vp[:, (tau * 16 + half * 8) * 65:
                               (tau * 16 + half * 8 + 8) * 65]
                            .rearrange("p (h j) -> p h j", h=8)[:, :, 0:64],
                            bk2[:, 0:512]
                            .rearrange("p (h j) -> p h j", h=8)))
                    sc_free[sl] = ("V", k.V.n)
            xln_rd_p = k.P.n

            @b.vector
            def _(e):
                wV(e, k, v_w)
                wV(e, k)
                k.V.inc(e.memset(nn2[:, 0:2], 0.0))
            for (src_t, col) in ((qT, 0), (kTr, 1)):
                for ch in range(NC_D * 4):
                    @b.scalar
                    def _(e, src_t=src_t, ch=ch):
                        wV(e, k)
                        k.A.inc(e.activation(
                            sqn[:], src_t[:, ch * 512:(ch + 1) * 512],
                            AF.Square))
                    a_sq = k.A.n

                    @b.tensor
                    def _(e, a_sq=a_sq):
                        wA(e, k, a_sq)
                        wV(e, k)
                        k.P.inc(e.matmul(bk_bc[0:2, 0:512], bd2[:, 0:2],
                                         sqn[:], start=True, stop=True))
                    p_nn = k.P.n

                    @b.vector
                    def _(e, col=col, p_nn=p_nn):
                        wP(e, k, p_nn)
                        k.V.inc(e.reduce_max(nn2[0:2, 2:3], bk_bc[0:2, 0:512],
                                             axis=AX.X))
                        wV(e, k)
                        k.V.inc(e.tensor_max(nn2[0:2, col:col + 1],
                                             nn2[0:2, col:col + 1],
                                             nn2[0:2, 2:3]))
            v_nn = k.V.n

            @b.tensor
            def _(e):
                wV(e, k, v_nn)
                k.P.inc(e.transpose(bk_bc[0:2, 0:2], nn2[0:2, 0:2],
                                    cst["identf"][0:2, 0:2]))
            p_tr2 = k.P.n

            @b.vector
            def _(e, p_tr2=p_tr2):
                wP(e, k, p_tr2)
                k.V.inc(e.reduce_max(nr[:, 0:1], bk_bc[0:1, 0:2], axis=AX.X))

            @b.tensor
            def _(e):
                wV(e, k)
                k.P.inc(e.transpose(bk_bc[0:2, 4:6], nn2[0:2, 1:3],
                                    cst["identf"][0:2, 0:2]))
            p_tr3 = k.P.n

            @b.vector
            def _(e, p_tr3=p_tr3):
                wP(e, k, p_tr3)
                k.V.inc(e.reduce_max(nr[:, 1:2], bk_bc[0:1, 4:6], axis=AX.X))
                wV(e, k)
                k.V.inc(e.tensor_mul(nr[:, 2:3], nr[:, 0:1], nr[:, 1:2]))

            @b.scalar
            def _(e):
                wV(e, k)
                k.A.inc(e.activation(nr[:, 3:4], nr[:, 2:3], AF.Sqrt))

            @b.vector
            def _(e):
                wA(e, k)
                k.V.inc(e.tensor_scalar(nr[:, 3:4], nr[:, 3:4],
                                        -0.125 * 1.0005, None, ALU.mult))
            v_negm = emit_bcast_col(k, b, cst, nr[:, 3:4], negm[:], bk_bc,
                                    k.V.n, n=1)

            for sg in range(4):
                for h in range(16):
                    hc, hp = h // 2, (h % 2) * 64
                    hs = h % 2
                    pvb = bk_pv[hs]
                    ntau = 16
                    p0, a0 = k.P.n, k.A.n
                    recs = []
                    scf_l = list(sc_free)
                    for tau in range(ntau):
                        sl = tau % 3
                        r = dict(tau=tau, sl=sl, scf=scf_l[sl],
                                 a_exp=a0 + tau + 1)
                        recs.append(r)
                        scf_l[sl] = ("A", r["a_exp"])
                    order = []
                    for j in range(ntau + 2):
                        if j < ntau:
                            order.append(("qk", j))
                        if j >= 2:
                            order.append(("pv", j - 2))
                    for idx, (kind, j) in enumerate(order):
                        recs[j]["p_" + kind] = p0 + idx + 1
                    ptw_l2 = list(pt_free_p)
                    for r in recs:
                        r["ptw"] = ptw_l2[r["sl"]]
                        ptw_l2[r["sl"]] = r["p_pv"]
                    first_attn = (h == 0 and sg == 0)
                    pvf = pv_free_v[hs]

                    @b.tensor
                    def _(e, recs=recs, order=order, hc=hc, hp=hp, sg=sg,
                          h=h, pvb=pvb, pvf=pvf, first=first_attn,
                          ntau=ntau):
                        if first:
                            wV(e, k, v_negm)
                            e.wait_ge(k.GQ.h, gq_mem)
                        for kind, j in order:
                            r = recs[j]
                            tau = r["tau"]
                            if kind == "qk":
                                wT(e, k, r["scf"])
                                e.matmul(
                                    bk_sc[r["sl"]][0:128, 0:512],
                                    kTr[hp:hp + 64,
                                        hc * SB + tau * 128:
                                        hc * SB + (tau + 1) * 128],
                                    qT[hp:hp + 64,
                                       hc * SB + sg * 512:
                                       hc * SB + (sg + 1) * 512],
                                    start=True, stop=True).then_inc(k.P.h, 1)
                            else:
                                wA(e, k, r["a_exp"])
                                if tau == 0 and pvf:
                                    wV(e, k, pvf)
                                e.matmul(
                                    pvb[0:65, 0:512],
                                    vp[:, (tau * 16 + h) * 65:
                                       (tau * 16 + h + 1) * 65],
                                    ptile[:, r["sl"] * 512:
                                          (r["sl"] + 1) * 512],
                                    start=(tau == 0),
                                    stop=(tau == ntau - 1)) \
                                    .then_inc(k.P.h, 1)

                    @b.scalar
                    def _(e, recs=recs):
                        for r in recs:
                            e.wait_ge(k.P.h, r["p_qk"])
                            if r["ptw"]:
                                wP(e, k, r["ptw"])
                            e.activation(
                                ptile[:, r["sl"] * 512:(r["sl"] + 1) * 512],
                                bk_sc[r["sl"]][0:128, 0:512], AF.Exp,
                                bias=negm[:],
                                scale=0.125).then_inc(k.A.h, 1)

                    k.P.n = p0 + 2 * ntau
                    k.A.n = a0 + ntau
                    sc_free[:] = scf_l
                    pt_free_p[:] = ptw_l2
                    p_pv = k.P.n

                    lrs = lr[:, hs * 512:(hs + 1) * 512]
                    rlbs = rlb[:, hs * 512:(hs + 1) * 512]

                    @b.vector
                    def _(e, p_pv=p_pv, lfp=lr_free_p, pvb=pvb, lrs=lrs):
                        wP(e, k, p_pv)
                        if lfp:
                            wP(e, k, lfp)
                        k.V.inc(e.tensor_copy(lrs, pvb[64:65, 0:512]))
                    v_lrc = k.V.n

                    @b.scalar
                    def _(e, v_lrc=v_lrc, lrs=lrs):
                        wV(e, k, v_lrc)
                        k.A.inc(e.activation(lrs, lrs, AF.Ln))
                        wA(e, k)
                        k.A.inc(e.activation(lrs, lrs, AF.Exp, scale=-1.0))

                    @b.vector
                    def _(e, lrs=lrs):
                        wA(e, k)
                        k.V.inc(e.tensor_scalar(lrs, lrs, wrow[:, 1:2],
                                                None, ALU.mult))
                    v_lr = k.V.n

                    @b.tensor
                    def _(e, v_lr=v_lr, lrs=lrs, bca=bc_free_a):
                        wV(e, k, v_lr)
                        if bca:
                            wV(e, k, bca)
                        k.P.inc(e.matmul(bk_bc[0:64, 0:512],
                                         cst["ones_rf"][0:1, 0:64], lrs,
                                         start=True, stop=True))
                    lr_free_p = k.P.n
                    p_bc = k.P.n

                    @b.vector
                    def _(e, p_bc=p_bc, rfv=rlb_free_v, rlbs=rlbs):
                        wP(e, k, p_bc)
                        if rfv:
                            wV(e, k, rfv)
                        k.V.inc(e.tensor_copy(rlbs, bk_bc[0:64, 0:512]))
                    v_rlb = k.V.n
                    bc_free_a = v_rlb

                    @b.vector
                    def _(e, h=h, hc=hc, hp=hp, pvb=pvb,
                          rlbs=rlbs, atf=attnT_free_p):
                        wV(e, k)
                        if atf and h == 0:
                            wP(e, k, atf)
                        k.V.inc(e.tensor_mul(
                            attnT[hp:hp + 64, hc * 512:(hc + 1) * 512],
                            pvb[0:64, 0:512], rlbs))
                    rlb_free_v = k.V.n
                    pv_free_v[hs] = k.V.n
                v_adot = k.V.n

                for g in range(2):
                    for c4 in range(4):
                        @b.vector
                        def _(e, g=g, c4=c4, sg=sg, srp=sig_rd_p):
                            wV(e, k)
                            if srp and c4 == 0:
                                wP(e, k, srp)
                            qs = qT[:, (4 * g + c4) * SB + sg * 512:
                                    (4 * g + c4) * SB + (sg + 1) * 512]
                            k.V.inc(e.tensor_scalar(scrA[:], qs, 0.0, None,
                                                    ALU.min))

                        @b.scalar
                        def _(e):
                            wV(e, k)
                            k.A.inc(e.activation(scrA[:], scrA[:], AF.Exp))

                        @b.vector
                        def _(e, g=g, c4=c4, sg=sg):
                            wA(e, k)
                            qs = qT[:, (4 * g + c4) * SB + sg * 512:
                                    (4 * g + c4) * SB + (sg + 1) * 512]
                            k.V.inc(e.tensor_scalar(scrB[:], qs, 0.0, None,
                                                    ALU.max))
                            wV(e, k)
                            k.V.inc(e.tensor_scalar(scrA[:], scrA[:], -1.0,
                                                    None, ALU.add))
                            wV(e, k)
                            k.V.inc(e.tensor_add(
                                sigc[:, c4 * 512:(c4 + 1) * 512],
                                scrA[:], scrB[:]))
                    v_sig = k.V.n

                    @b.tensor
                    def _(e, v_sig=v_sig):
                        wV(e, k, v_sig)
                        for c4 in range(4):
                            k.P.inc(e.matmul(
                                bk_bc[0:1, 0:512], memnb[:, c4:c4 + 1],
                                sigc[:, c4 * 512:(c4 + 1) * 512],
                                start=(c4 == 0), stop=(c4 == 3)))
                    p_den = k.P.n

                    @b.vector
                    def _(e, p_den=p_den):
                        wP(e, k, p_den)
                        k.V.inc(e.tensor_copy(dr[:], bk_bc[0:1, 0:512]))
                        wV(e, k)
                        k.V.inc(e.reciprocal(dr[:], dr[:]))
                        wV(e, k)
                        k.V.inc(e.tensor_scalar(dr[:], dr[:], wrow[:, 0:1],
                                                None, ALU.mult))
                    v_dr = k.V.n

                    @b.tensor
                    def _(e, v_dr=v_dr):
                        wV(e, k, v_dr)
                        k.P.inc(e.matmul(bk_bc[0:128, 0:512],
                                         cst["ones_rf"][0:1, 0:128], dr[:],
                                         start=True, stop=True))
                    p_rd = k.P.n

                    @b.vector
                    def _(e, p_rd=p_rd):
                        wP(e, k, p_rd)
                        k.V.inc(e.tensor_copy(rdb[:], bk_bc[0:128, 0:512]))
                    v_rdb = k.V.n

                    for kvc_i in range(4):
                        bkn = bk_nm[kvc_i % 2]

                        @b.tensor
                        def _(e, g=g, kvc_i=kvc_i, bkn=bkn, v_rdb=v_rdb,
                              nmf=sc_free[kvc_i % 2]):
                            if kvc_i == 0:
                                wV(e, k, v_rdb)
                            wT(e, k, nmf)
                            for c4 in range(4):
                                k.P.inc(e.matmul(
                                    bkn[:, 0:512],
                                    memb[:, c4 * KV + kvc_i * 128:
                                         c4 * KV + (kvc_i + 1) * 128],
                                    sigc[:, c4 * 512:(c4 + 1) * 512],
                                    start=(c4 == 0), stop=(c4 == 3)))
                        p_num = k.P.n
                        if kvc_i == 3:
                            sig_rd_p = p_num

                        @b.vector
                        def _(e, g=g, kvc_i=kvc_i, bkn=bkn, p_num=p_num,
                              v_adot=v_adot):
                            wP(e, k, p_num)
                            wV(e, k)
                            k.V.inc(e.tensor_mul(scrB[:], bkn[:, 0:512],
                                                 rdb[:]))
                            wV(e, k)
                            ac = attnT[:, (4 * g + kvc_i) * 512:
                                       (4 * g + kvc_i + 1) * 512]
                            k.V.inc(e.tensor_add(ac, ac, scrB[:]))
                        sc_free[kvc_i % 2] = ("V", k.V.n)
                v_mix = k.V.n

                trc_free_v = [0, 0]
                for j in range(4):
                    for c in range(NC_D):
                        @b.tensor
                        def _(e, j=j, c=c, v_mix=v_mix,
                              tfv=trc_free_v[c % 2], ffv=fout_free_v):
                            if c == 0 and j == 0:
                                wV(e, k, v_mix)
                            if c == 0 and ffv:
                                wV(e, k, ffv)
                            if tfv:
                                wV(e, k, tfv)
                            k.P.inc(e.transpose(
                                bk_tr[c % 2][:, 0:128],
                                attnT[:, c * 512 + j * 128:
                                      c * 512 + (j + 1) * 128],
                                cst["ident"][:]))
                        p_tr = k.P.n

                        @b.vector
                        def _(e, c=c, p_tr=p_tr):
                            wP(e, k, p_tr)
                            wV(e, k)
                            k.V.inc(e.tensor_copy(
                                fout[:, c * 128:(c + 1) * 128],
                                bk_tr[c % 2][:, 0:128]))
                        trc_free_v[c % 2] = k.V.n
                    v_f = k.V.n

                    i_tile = bi * 16 + sg * 4 + j
                    epi.tile(b, cst, i_tile, fout[:, 0:D],
                             lambda e, v_f=v_f: wV(e, k, v_f))
                    fout_free_v = k.V.n
                attnT_free_p = k.P.n

        epi.finish(b)

    k.close()
    return nc



def build_L3a():
    k = K("l3a")
    nc = k.nc
    SB = S
    NCH = SB // TT
    x_in = k.din("x", [T, D])
    wqT = k.din("wqT", [D, D])
    wkT = k.din("wkT", [D, D])
    wvT = k.din("wvT", [D, D])
    woT = k.din("woT", [D, D])
    glr = k.din("glr", [1, 64])
    x_out = k.dout("x_new", [T, D])

    xlnT = k.sb("xlnT", [128, NC_D * SB], BF16)
    qT = k.sb("qT", [128, NC_D * SB], BF16)
    vT = k.sb("vT", [128, NC_D * SB], BF16)
    ktok = k.sb("ktok", [128, NCH * D], BF16)
    wb = k.sb("wb", [128, NC_D * D], BF16)
    yT = xlnT
    Sst = k.sb("Sst", [64, 16 * 64])
    Lbuf = k.sb("Lbuf", [128, NCH * 16])
    Cbuf = k.sb("Cbuf", [128, NCH * 16])
    sqk = k.sb("sqk", [128, D], BF16)
    kb = k.sb("kb", [128, 16])
    kb2 = k.sb("kb2", [128, 16])
    kb3 = k.sb("kb3", [128, 16])
    glb = k.sb("glb", [128, 64])
    glrow = k.sb("glrow", [1, 64])
    onesf = k.sb("onesf", [128, 128])
    trilf = k.sb("trilf", [128, 128])
    NLbuf = k.sb("NLbuf", [128, NCH * 16])
    bcL = k.sb("bcL", [128, 2 * 128])
    Am = k.sb("Am", [128, 2 * 128])
    Gm = k.sb("Gm", [128, 2 * 128])
    Bm = k.sb("Bm", [128, 2 * 128], BF16)
    qe = k.sb("qe", [64, 2 * 128])
    eL = k.sb("eL", [128, 2 * 128])
    vts = k.sb("vts", [128, 2 * 64], BF16)
    ck = k.sb("ck", [128, 2 * 64], BF16)
    cc2 = k.sb("cc2", [128, 2])
    fout = k.sb("fout", [128, D])

    bk_tr = [k.bankb("tr0"), k.bankb("tr1")]
    bk_bcT = k.bank("bcT")
    bk_G = k.bank("G")
    bk_y = k.bank("y")
    bk_S = k.bank("Sb")
    bk_bc = k.bank("bc")
    bk_p2 = k.bank("p2")

    with nc.Block() as b:
        cst = setup_consts(k, b, need_identf=True)
        s_rows = k.sem("s_rows")
        s_wldq = k.sem("wldq")

        @b.sync
        def _(e):
            s_rows.inc(e.dma_start(out=glrow[:], in_=glr), 16)
        sy_g = s_rows.n

        @b.vector
        def _(e):
            wV(e, k, cst["v_done"])
            k.V.inc(e.memset(onesf[:], 0.0))
            k.V.inc(e.tensor_scalar(trilf[:], cst["iot"][:], 0, None,
                                    ALU.is_ge))
        v_c3 = k.V.n

        @b.tensor
        def _(e):
            wV(e, k, v_c3)
            e.wait_ge(s_rows.h, sy_g)
            k.P.inc(e.matmul(bk_bc[0:128, 0:64], cst["ones_rf"][0:1, 0:128],
                             glrow[:], start=True, stop=True))
        p_gl = k.P.n

        @b.vector
        def _(e, p_gl=p_gl):
            wP(e, k, p_gl)
            k.V.inc(e.tensor_copy(glb[:], bk_bc[0:128, 0:64]))
        v_gl = k.V.n

        pro = Prologue(k, x_in, xlnT, bk_tr,
                       col_of=lambda c, i: c * SB + (i % 16) * TT)
        epi = Epilogue(k, x_in, x_out)

        xln_rd_p = 0
        fout_free_v = 0
        pj_free = [None, None]

        for bi in range(B_PER):
            for i in range(bi * 16, bi * 16 + 16):
                ew = (lambda e, n=xln_rd_p: wP(e, k, n)) if xln_rd_p else None
                pro.tile(b, cst, i, evac_wait=ew)
            ev_all = pro.ev_a[bi * 16 + 15]
            pro.tr_free_a = [k.V.n, k.V.n]

            def load_w(wsrc):
                @b.gpsimd
                def _(e, wsrc=wsrc):
                    wP(e, k)
                    s_wldq.inc(e.dma_start(
                        out=wb[:].rearrange("p (c n) -> p c n", c=NC_D),
                        in_=wsrc.rearrange("(c p) n -> p c n", p=128)), 16)
                return s_wldq.n

            for pi, (dst, wsrc) in enumerate(((qT, wqT), (vT, wvT))):
                ldw = load_w(wsrc)
                P0, V0 = k.P.n, k.V.n
                pjf0 = list(pj_free)

                @b.tensor
                def _(e, ldw=ldw, P0=P0, V0=V0, pjf0=pjf0, pi=pi):
                    if pi == 0:
                        wV(e, k, ev_all)
                    e.wait_ge(s_wldq.h, ldw)
                    for f in pjf0:
                        wT(e, k, f)
                    for ii in range(32):
                        ec, tc = ii // 4, ii % 4
                        bk2 = (bk_G, bk_y)[ii % 2]
                        if ii >= 2:
                            e.wait_ge(k.V.h, V0 + ii - 1)
                        for c in range(NC_D):
                            k.P.inc(e.matmul(
                                bk2[:, 0:512],
                                wb[:, c * D + ec * 128:c * D + (ec + 1) * 128],
                                xlnT[:, c * SB + tc * 512:
                                     c * SB + (tc + 1) * 512],
                                start=(c == 0), stop=(c == NC_D - 1)))

                @b.vector
                def _(e, P0=P0, dst=dst):
                    for ii in range(32):
                        ec, tc = ii // 4, ii % 4
                        bk2 = (bk_G, bk_y)[ii % 2]
                        e.wait_ge(k.P.h, P0 + 8 * (ii + 1))
                        e.wait_ge(k.V.h, k.V.n)
                        k.V.inc(e.tensor_copy(
                            dst[:, ec * SB + tc * 512:
                                ec * SB + (tc + 1) * 512],
                            bk2[:, 0:512]))
                pj_free[0] = ("V", k.V.n - 1)
                pj_free[1] = ("V", k.V.n)

            ldw = load_w(wkT)
            P0, V0 = k.P.n, k.V.n
            pjf0 = list(pj_free)

            @b.tensor
            def _(e, ldw=ldw, P0=P0, V0=V0, pjf0=pjf0):
                e.wait_ge(s_wldq.h, ldw)
                for f in pjf0:
                    wT(e, k, f)
                for ii in range(32):
                    tau, half = ii // 2, ii % 2
                    bk2 = (bk_G, bk_y)[ii % 2]
                    if ii >= 2:
                        e.wait_ge(k.V.h, V0 + ii - 1)
                    for c in range(NC_D):
                        k.P.inc(e.matmul(
                            bk2[:, 0:512],
                            xlnT[:, c * SB + tau * 128:
                                 c * SB + (tau + 1) * 128],
                            wb[:, c * D + half * 512:
                               c * D + (half + 1) * 512],
                            start=(c == 0), stop=(c == NC_D - 1)))

            @b.vector
            def _(e, P0=P0):
                for ii in range(32):
                    tau, half = ii // 2, ii % 2
                    bk2 = (bk_G, bk_y)[ii % 2]
                    e.wait_ge(k.P.h, P0 + 8 * (ii + 1))
                    e.wait_ge(k.V.h, k.V.n)
                    k.V.inc(e.tensor_copy(
                        ktok[:, tau * D + half * 512:
                             tau * D + (half + 1) * 512],
                        bk2[:, 0:512]))
            pj_free[0] = ("V", k.V.n - 1)
            pj_free[1] = ("V", k.V.n)
            xln_rd_p = k.P.n

            for tau in range(NCH):
                V0, A0, P0 = k.V.n, k.A.n, k.P.n

                @b.scalar
                def _(e, tau=tau, V0=V0):
                    e.wait_ge(k.V.h, V0)
                    e.wait_ge(k.A.h, k.A.n)
                    k.A.inc(e.activation(sqk[:],
                                         ktok[:, tau * D:(tau + 1) * D],
                                         AF.Square))
                    e.wait_ge(k.V.h, V0 + 32)
                    e.wait_ge(k.A.h, k.A.n)
                    k.A.inc(e.activation(kb[:], kb[:], AF.Sqrt))
                    e.wait_ge(k.V.h, V0 + 38)
                    e.wait_ge(k.A.h, k.A.n)
                    k.A.inc(e.activation(kb3[:], kb3[:], AF.Sigmoid))
                    e.wait_ge(k.V.h, V0 + 40)
                    e.wait_ge(k.A.h, k.A.n)
                    k.A.inc(e.activation(kb2[:], kb2[:], AF.Sigmoid))
                    e.wait_ge(k.V.h, V0 + 45)
                    e.wait_ge(k.A.h, k.A.n)
                    k.A.inc(e.activation(kb2[:], kb2[:], AF.Ln))

                @b.vector
                def _(e, tau=tau, A0=A0, P0=P0):
                    e.wait_ge(k.A.h, A0 + 1)
                    e.wait_ge(k.V.h, k.V.n)
                    for h in range(16):
                        k.V.inc(e.reduce_sum(kb[:, h:h + 1],
                                             sqk[:, h * 64:(h + 1) * 64],
                                             axis=AX.X))
                        k.V.inc(e.reduce_sum(
                            kb2[:, h:h + 1],
                            ktok[:, tau * D + h * 64:tau * D + (h + 1) * 64],
                            axis=AX.X))
                    e.wait_ge(k.A.h, A0 + 2)
                    e.wait_ge(k.V.h, k.V.n)
                    k.V.inc(e.tensor_scalar(kb[:], kb[:], 1e-12, None,
                                            ALU.max))
                    e.wait_ge(k.V.h, k.V.n)
                    k.V.inc(e.reciprocal(kb[:], kb[:]))
                    e.wait_ge(k.V.h, k.V.n)
                    k.V.inc(e.tensor_mul(kb2[:], kb2[:], kb[:]))
                    e.wait_ge(k.V.h, k.V.n)
                    k.V.inc(e.tensor_scalar(kb2[:], kb2[:], 1.0 / 64.0, None,
                                            ALU.mult))
                    e.wait_ge(k.V.h, k.V.n)
                    k.V.inc(e.tensor_mul(kb3[:], kb2[:], glb[:, 0:16]))
                    e.wait_ge(k.V.h, k.V.n)
                    k.V.inc(e.tensor_add(kb3[:], kb3[:], glb[:, 16:32]))
                    e.wait_ge(k.A.h, A0 + 3)
                    e.wait_ge(k.V.h, k.V.n)
                    k.V.inc(e.tensor_mul(kb2[:], kb2[:], glb[:, 32:48]))
                    e.wait_ge(k.V.h, k.V.n)
                    k.V.inc(e.tensor_add(kb2[:], kb2[:], glb[:, 48:64]))
                    e.wait_ge(k.A.h, A0 + 4)
                    e.wait_ge(k.V.h, k.V.n)
                    k.V.inc(e.tensor_scalar(kb2[:], kb2[:], -1.0, 1.0,
                                            ALU.mult, ALU.add))
                    e.wait_ge(k.V.h, k.V.n)
                    k.V.inc(e.tensor_mul(kb2[:], kb2[:], kb3[:]))
                    e.wait_ge(k.V.h, k.V.n)
                    k.V.inc(e.tensor_scalar(kb2[:], kb2[:], 1e-38, None,
                                            ALU.max))
                    e.wait_ge(k.V.h, k.V.n)
                    k.V.inc(e.tensor_scalar(kb3[:], kb2[:], -1.0, 1.0,
                                            ALU.mult, ALU.add))
                    e.wait_ge(k.V.h, k.V.n)
                    k.V.inc(e.tensor_mul(Cbuf[:, tau * 16:(tau + 1) * 16],
                                         kb3[:], kb[:]))
                    e.wait_ge(k.P.h, P0 + 1)
                    e.wait_ge(k.V.h, k.V.n)
                    k.V.inc(e.tensor_copy(Lbuf[:, tau * 16:(tau + 1) * 16],
                                          bk_p2[0:128, 0:16]))
                    e.wait_ge(k.V.h, k.V.n)
                    k.V.inc(e.tensor_scalar(
                        NLbuf[:, tau * 16:(tau + 1) * 16],
                        Lbuf[:, tau * 16:(tau + 1) * 16],
                        -1.0, None, ALU.mult))

                @b.tensor
                def _(e, A0=A0):
                    e.wait_ge(k.A.h, A0 + 5)
                    k.P.inc(e.matmul(bk_p2[0:128, 0:16], trilf[:], kb2[:],
                                     start=True, stop=True))
            v_gates = k.V.n

            ldwo = load_w(woT)

            @b.vector
            def _(e):
                wV(e, k, v_gates)
                k.V.inc(e.memset(Sst[:], 0.0))
            vInit = k.V.n

            NIT = NCH * 16

            def VQ(it):
                t_, h_ = it // 16, it % 16
                return (h_ % 2) * 64, (h_ // 2) * SB + t_ * 128

            def KTo(it):
                t_, h_ = it // 16, it % 16
                return t_ * D + h_ * 64

            a0s, p0s, v0s, g0s = k.A.n, k.P.n, k.V.n, k.G.n
            gBcl = {}; aAm = {}; aEl = {}
            pBct = {}; pG = {}; pVt = {}; pYs = {}; pSadd = {}
            vGm = {}; vQe = {}; vVts = {}; vYev = {}; vSupd = {}
            an, pn, vn, gn = a0s, p0s, v0s, g0s
            gn += 1; gBcl[0] = gn
            pn += 1; pBct[0] = pn
            pn += 1; pG[0] = pn
            for it in range(NIT):
                if it < NIT - 1:
                    gn += 1; gBcl[it + 1] = gn
                an += 1; aAm[it] = an
                an += 1; aEl[it] = an
                if it < NIT - 1:
                    pn += 1; pBct[it + 1] = pn
                    pn += 1; pG[it + 1] = pn
                pn += 1; pVt[it] = pn
                pn += 2; pYs[it] = pn
                pn += 1; pSadd[it] = pn
                vn += 1; vGm[it] = vn
                vn += 2; vQe[it] = vn
                vn += 2
                vn += 1; vVts[it] = vn
                vn += 1; vYev[it] = vn
                vn += 1; vSupd[it] = vn

            bcT_bk = (bk_bcT, bk_bc)
            G_bk = (bk_G, bk_p2)

            for tau in range(NCH):
                its = list(range(tau * 16, tau * 16 + 16))

                @b.gpsimd
                def _(e, tau=tau, its=its, vInit=vInit):
                    if tau == 0:
                        e.wait_ge(k.V.h, vInit)
                        e.wait_ge(k.G.h, k.G.n)
                        k.G.inc(e.tensor_scalar(bcL[:, 0:128], onesf[:],
                                                Lbuf[:, 0:1], None, ALU.add))
                    for it in its:
                        sl = (it + 1) % 2
                        if it < NIT - 1:
                            e.wait_ge(k.G.h, k.G.n)
                            if it >= 1:
                                e.wait_ge(k.P.h, pBct[it - 1])
                            k.G.inc(e.tensor_scalar(
                                bcL[:, sl * 128:(sl + 1) * 128], onesf[:],
                                Lbuf[:, it + 1:it + 2], None, ALU.add))

                @b.scalar
                def _(e, tau=tau, its=its, vInit=vInit):
                    if tau == 0:
                        e.wait_ge(k.V.h, vInit)
                    for it in its:
                        s0 = it % 2
                        e.wait_ge(k.A.h, k.A.n)
                        e.wait_ge(k.P.h, pBct[it])
                        if it >= 2:
                            e.wait_ge(k.V.h, vSupd[it - 2])
                        k.A.inc(e.activation(
                            Am[:, s0 * 128:(s0 + 1) * 128],
                            bcT_bk[s0][0:128, 0:128], AF.Exp,
                            bias=NLbuf[:, it:it + 1]))
                        e.wait_ge(k.A.h, k.A.n)
                        k.A.inc(e.activation(
                            eL[:, s0 * 128:(s0 + 1) * 128],
                            bcT_bk[s0][0:128, 0:128], AF.Exp))

                @b.tensor
                def _(e, tau=tau, its=its, vInit=vInit):
                    if tau == 0:
                        e.wait_ge(k.V.h, vInit)
                        e.wait_ge(k.G.h, gBcl[0])
                        k.P.inc(e.transpose(bcT_bk[0][:, 0:128],
                                            bcL[:, 0:128],
                                            cst["identf"][:]))
                        hp0, c0 = VQ(0)
                        k.P.inc(e.matmul(
                            G_bk[0][0:128, 0:128],
                            vT[hp0:hp0 + 64, c0:c0 + 128],
                            qT[hp0:hp0 + 64, c0:c0 + 128],
                            start=True, stop=True))
                    for it in its:
                        t_, h_ = it // 16, it % 16
                        s0 = it % 2
                        if it < NIT - 1:
                            s1 = (it + 1) % 2
                            e.wait_ge(k.G.h, gBcl[it + 1])
                            if it >= 1:
                                e.wait_ge(k.A.h, aEl[it - 1])
                            k.P.inc(e.transpose(
                                bcT_bk[s1][:, 0:128],
                                bcL[:, s1 * 128:(s1 + 1) * 128],
                                cst["identf"][:]))
                            hp1, c1 = VQ(it + 1)
                            if it >= 1:
                                e.wait_ge(k.V.h, vGm[it - 1])
                            k.P.inc(e.matmul(
                                G_bk[s1][0:128, 0:128],
                                vT[hp1:hp1 + 64, c1:c1 + 128],
                                qT[hp1:hp1 + 64, c1:c1 + 128],
                                start=True, stop=True))
                        hp, cc = VQ(it)
                        if it >= 1:
                            e.wait_ge(k.V.h, vVts[it - 1])
                        k.P.inc(e.transpose(
                            bk_tr[0][0:128, 0:64],
                            vT[hp:hp + 64, cc:cc + 128],
                            cst["ident"][hp:hp + 64, hp:hp + 64]))
                        e.wait_ge(k.V.h, vQe[it])
                        k.P.inc(e.matmul(
                            bk_y[0:64, 0:128],
                            Sst[:, h_ * 64:(h_ + 1) * 64],
                            qe[0:64, s0 * 128:s0 * 128 + 128],
                            start=True, stop=False))
                        k.P.inc(e.matmul(
                            bk_y[0:64, 0:128],
                            ktok[:, KTo(it):KTo(it) + 64],
                            Bm[:, s0 * 128:s0 * 128 + 128],
                            start=False, stop=True))
                        e.wait_ge(k.V.h, vVts[it])
                        k.P.inc(e.matmul(
                            bk_S[0:64, 0:64],
                            vts[:, s0 * 64:s0 * 64 + 64],
                            ck[:, s0 * 64:s0 * 64 + 64],
                            start=True, stop=True))

                @b.vector
                def _(e, tau=tau, its=its):
                    for it in its:
                        t_, h_ = it // 16, it % 16
                        s0 = it % 2
                        hp, cc = VQ(it)
                        e.wait_ge(k.V.h, k.V.n)
                        e.wait_ge(k.P.h, pG[it])
                        k.V.inc(e.scalar_tensor_tensor(
                            Gm[:, s0 * 128:(s0 + 1) * 128],
                            G_bk[s0][0:128, 0:128],
                            Cbuf[:, it:it + 1], trilf[:],
                            ALU.mult, ALU.mult))
                        e.wait_ge(k.V.h, k.V.n)
                        e.wait_ge(k.A.h, aAm[it])
                        k.V.inc(e.scalar_tensor_tensor(
                            Bm[:, s0 * 128:(s0 + 1) * 128],
                            Am[:, s0 * 128:(s0 + 1) * 128], 1.0,
                            Gm[:, s0 * 128:(s0 + 1) * 128],
                            ALU.min, ALU.mult))
                        e.wait_ge(k.V.h, k.V.n)
                        e.wait_ge(k.A.h, aEl[it])
                        k.V.inc(e.tensor_mul(
                            qe[0:64, s0 * 128:s0 * 128 + 128],
                            qT[hp:hp + 64, cc:cc + 128],
                            eL[hp:hp + 64, s0 * 128:s0 * 128 + 128]))
                        e.wait_ge(k.V.h, k.V.n)
                        k.V.inc(e.tensor_mul(
                            cc2[:, s0:s0 + 1],
                            Am[:, s0 * 128 + 127:s0 * 128 + 128],
                            Cbuf[:, it:it + 1]))
                        e.wait_ge(k.V.h, k.V.n)
                        k.V.inc(e.tensor_scalar(
                            ck[:, s0 * 64:s0 * 64 + 64],
                            ktok[:, KTo(it):KTo(it) + 64],
                            cc2[:, s0:s0 + 1], None, ALU.mult))
                        e.wait_ge(k.V.h, k.V.n)
                        e.wait_ge(k.P.h, pVt[it])
                        k.V.inc(e.tensor_copy(vts[:, s0 * 64:s0 * 64 + 64],
                                              bk_tr[0][0:128, 0:64]))
                        e.wait_ge(k.V.h, k.V.n)
                        e.wait_ge(k.P.h, pYs[it])
                        k.V.inc(e.tensor_copy(
                            yT[hp:hp + 64, cc:cc + 128],
                            bk_y[0:64, 0:128]))
                        e.wait_ge(k.V.h, k.V.n)
                        e.wait_ge(k.P.h, pSadd[it])
                        k.V.inc(e.scalar_tensor_tensor(
                            Sst[:, h_ * 64:(h_ + 1) * 64],
                            Sst[:, h_ * 64:(h_ + 1) * 64],
                            eL[0:64, s0 * 128 + 127:s0 * 128 + 128],
                            bk_S[0:64, 0:64],
                            ALU.mult, ALU.add))
            assert (k.A.n, k.P.n, k.V.n, k.G.n) == (an, pn, vn, gn), \
                ((k.A.n, k.P.n, k.V.n, k.G.n), (an, pn, vn, gn))
            v_scan = k.V.n

            for tc in range(4):
                for j in range(4):
                    @b.tensor
                    def _(e, tc=tc, j=j, first=(tc == 0 and j == 0),
                          ofv=fout_free_v, ldwo=ldwo):
                        if first:
                            wV(e, k)
                            e.wait_ge(s_wldq.h, ldwo)
                        if ofv:
                            wV(e, k, ofv)
                        for half in range(2):
                            bko = (bk_G, bk_y)[half]
                            for c in range(NC_D):
                                k.P.inc(e.matmul(
                                    bko[:, 0:512],
                                    yT[:, c * SB + tc * 512 + j * 128:
                                       c * SB + tc * 512 + (j + 1) * 128],
                                    wb[:, c * D + half * 512:
                                       c * D + (half + 1) * 512],
                                    start=(c == 0), stop=(c == NC_D - 1)))
                    p_op = k.P.n

                    @b.vector
                    def _(e, p_op=p_op, ffv=fout_free_v):
                        wP(e, k, p_op)
                        if ffv:
                            wV(e, k, ffv)
                        k.V.inc(e.tensor_copy(fout[:, 0:512],
                                              bk_G[:, 0:512]))
                        k.V.inc(e.tensor_copy(fout[:, 512:1024],
                                              bk_y[:, 0:512]))
                    v_f = k.V.n

                    i_tile = bi * 16 + tc * 4 + j
                    epi.tile(b, cst, i_tile, fout[:, 0:D],
                             lambda e, v_f=v_f: wV(e, k, v_f))
                    fout_free_v = k.V.n
            yT_rd_p = k.P.n
            xln_rd_p = k.P.n

        epi.finish(b)

    k.close()
    return nc



def np_shard_x(x):
    xs = np.ascontiguousarray(np.asarray(x).reshape(16, S, D))
    return [np.ascontiguousarray(xs[2 * c:2 * c + 2].reshape(T, D))
            for c in range(NCORES)]


def run_L3f(x_full, tw, bias, ascale):
    nc = build_L3f()
    shards = np_shard_x(x_full)
    twT = np.ascontiguousarray(np.asarray(tw).T)
    b1 = np.ascontiguousarray(np.asarray(bias).reshape(1, D))
    asc = np.ascontiguousarray(np.asarray(ascale).reshape(1, 1))
    in_maps = [{"x": s, "twT": twT, "bias": b1, "ascale": asc} for s in shards]
    res = run_bass_kernel_spmd(nc, in_maps, core_ids=list(range(NCORES)))
    return np.concatenate([r["x_new"].reshape(B_PER, S, D)
                           for r in res.results], axis=0)


def _run(nc, in_maps):
    return run_bass_kernel_spmd(nc, in_maps, core_ids=list(range(NCORES)))


def kernel(**inputs):
    cc = np.ascontiguousarray
    g = {n: np.asarray(v) for n, v in inputs.items()}
    xs = np_shard_x(g["L0_attn_x"])

    com = {
        "wq": cc(g["L0_attn_wq"].reshape(D, D)),
        "wk": cc(g["L0_attn_wk"].reshape(D, D)),
        "wv": cc(g["L0_attn_wv"].reshape(D, D)),
        "wo": cc(g["L0_attn_wo"].reshape(D, D)),
        "bqc": cc(g["L0_attn_bq"].reshape(-1).reshape(8, 128).T),
        "bkc": cc(g["L0_attn_bk"].reshape(-1).reshape(8, 128).T),
        "bvc": cc(g["L0_attn_bv"].reshape(-1).reshape(8, 128).T),
        "bor": cc(g["L0_attn_bo"].reshape(1, D)),
    }
    res = _run(build_L0a(), [dict(com, x=s) for s in xs])
    xs = [np.asarray(r["x_new"]) for r in res.results]

    res = _run(build_maxln(), [{"x": s} for s in xs])
    pmax8 = cc(np.concatenate([np.asarray(r["pmax"]).reshape(1, 1)
                               for r in res.results], axis=1))

    com = {
        "pmax8": pmax8,
        "w1": cc(g["L0_ffn_w1"]),
        "b1c": cc(g["L0_ffn_b1"].reshape(32, 128).T),
    }
    res = _run(build_L0f1(), [dict(com, x=s) for s in xs])
    hs = [np.asarray(r["h_out"]) for r in res.results]
    pmaxh8 = cc(np.concatenate([np.asarray(r["pmaxh"]).reshape(1, 1)
                                for r in res.results], axis=1))

    com = {
        "pmaxh8": pmaxh8,
        "w2": cc(g["L0_ffn_w2"]),
        "b2c": cc(g["L0_ffn_b2"].reshape(8, 128).T),
    }
    res = _run(build_L0f2(), [dict(com, x=s, h_in=h) for s, h in zip(xs, hs)])
    xs = [np.asarray(r["x_new"]) for r in res.results]
    del hs

    com = {
        "wqT": cc(g["L1_attn_Wq"].T), "wkT": cc(g["L1_attn_Wk"].T),
        "wvT": cc(g["L1_attn_Wv"].T), "woT": cc(g["L1_attn_Wo"].T),
        "bqc": cc(g["L1_attn_bq"].reshape(8, 128).T),
        "bkc": cc(g["L1_attn_bk"].reshape(8, 128).T),
        "bvr": cc(g["L1_attn_bv"].reshape(1, D)),
        "bor": cc(g["L1_attn_bo"].reshape(1, D)),
    }
    res = _run(build_L1a(), [dict(com, x=s) for s in xs])
    xs = [np.asarray(r["x_new"]) for r in res.results]

    com = {
        "wfcT": cc(g["L1_ffn_W_fc"].T), "wprT": cc(g["L1_ffn_W_proj"].T),
        "afcT": cc(g["L1_ffn_A_fc"].T), "bfcT": cc(g["L1_ffn_B_fc"].T),
        "aprT": cc(g["L1_ffn_A_proj"].T), "bprT": cc(g["L1_ffn_B_proj"].T),
        "bfc_pc": cc(g["L1_ffn_b_fc"].reshape(32, 128).T),
        "bpr": cc(g["L1_ffn_b_proj"].reshape(1, D)),
    }
    res = _run(build_L1f(), [dict(com, x=s) for s in xs])
    xs = [np.asarray(r["x_new"]) for r in res.results]

    wkr = cc(np.repeat(g["L2_attn_Wk"].reshape(D, 8, 64), 2,
                       axis=1).reshape(D, D))
    wvr = cc(np.repeat(g["L2_attn_Wv"].reshape(D, 8, 64), 2,
                       axis=1).reshape(D, D))
    com = {
        "wq2": cc(g["L2_attn_Wq"]), "wkr": wkr, "wvr": wvr,
        "memi": cc(g["L2_attn_memory"]),
        "memn": cc(g["L2_attn_memory_norm"].reshape(4, 128).T),
        "memw": cc(g["L2_attn_memory_weight"].reshape(1, 1)),
    }
    res = _run(build_L2a(), [dict(com, x=s) for s in xs])
    xs = [np.asarray(r["x_new"]) for r in res.results]

    com = {"wupT": cc(g["L2_ffn_w_up"].T), "wdnT": cc(g["L2_ffn_w_down"].T)}
    res = _run(build_L2f(), [dict(com, x=s) for s in xs])
    xs = [np.asarray(r["x_new"]) for r in res.results]

    com = {
        "wqT": cc(g["L3_attn_Wq"].T), "wkT": cc(g["L3_attn_Wk"].T),
        "wvT": cc(g["L3_attn_Wv"].T), "woT": cc(g["L3_attn_Wo"].T),
        "glr": cc(np.concatenate([
            g["L3_attn_W_gamma"], g["L3_attn_b_gamma"],
            g["L3_attn_W_lambda"], g["L3_attn_b_lambda"]]).reshape(1, 64)),
    }
    res = _run(build_L3a(), [dict(com, x=s) for s in xs])
    xs = [np.asarray(r["x_new"]) for r in res.results]

    com = {
        "twT": cc(g["L3_ffn_ternary_weight"].T),
        "bias": cc(g["L3_ffn_bias"].reshape(1, D)),
        "ascale": cc(g["L3_ffn_act_scale"].reshape(1, 1).astype(np.float32)),
    }
    res = _run(build_L3f(), [dict(com, x=s) for s in xs])
    xs = [np.asarray(r["x_new"]) for r in res.results]

    return np.concatenate([s.reshape(B_PER, S, D) for s in xs],
                          axis=0)


# version 10
# speedup vs baseline: 1.4908x; 1.0166x over previous
from contextlib import ExitStack
import numpy as np
import concourse.bass as bass
import concourse.mybir as mybir
from concourse.bass_utils import run_bass_kernel_spmd

F32 = mybir.dt.float32
BF16 = mybir.dt.bfloat16
I32 = mybir.dt.int32
AF = mybir.ActivationFunctionType
ALU = mybir.AluOpType
AX = mybir.AxisListType

NCORES = 8
B_PER = 2
S = 2048
T = B_PER * S
D = 1024
NC_D = D // 128
TT = 128
NTT = T // TT

C32 = 12582912.0
C64 = 6755399441055744.0
LN_EPS = 1e-5


class Sem:
    def __init__(self, h):
        self.h = h
        self.n = 0

    def inc(self, instr, k=1):
        instr.then_inc(self.h, k)
        self.n += k
        return self.n


class K:

    def __init__(self, name="k"):
        self.nc = bass.Bass("TRN2", target_bir_lowering=False)
        self.es = ExitStack()
        self.name = name
        self._nbank = 0
        self.V = self.sem("semV")
        self.A = self.sem("semA")
        self.P = self.sem("semP")
        self.G = self.sem("semG")
        self.SY = self.sem("semSY")
        self.GQ = self.sem("semGQ")

    def sb(self, name, shape, dt=F32):
        return self.es.enter_context(self.nc.sbuf_tensor(name, shape, dt))

    def bank(self, name):
        self._nbank += 1
        assert self._nbank <= 8, "out of PSUM banks"
        return self.es.enter_context(self.nc.psum_tensor(name, [128, 512], F32))

    def bankb(self, name):
        self._nbank += 1
        assert self._nbank <= 8, "out of PSUM banks"
        return self.es.enter_context(self.nc.psum_tensor(name, [128, 1024], BF16))

    def sem(self, name):
        return Sem(self.es.enter_context(self.nc.semaphore(name)))

    def din(self, name, shape, dt=F32):
        return self.nc.dram_tensor(name, shape, dt, kind="ExternalInput").ap()

    def dout(self, name, shape, dt=F32):
        return self.nc.dram_tensor(name, shape, dt, kind="ExternalOutput").ap()

    def close(self):
        self.es.close()


def wV(e, k, n=None):
    e.wait_ge(k.V.h, k.V.n if n is None else n)


def wA(e, k, n=None):
    e.wait_ge(k.A.h, k.A.n if n is None else n)


def wP(e, k, n=None):
    e.wait_ge(k.P.h, k.P.n if n is None else n)


def wG(e, k, n=None):
    e.wait_ge(k.G.h, k.G.n if n is None else n)


def wSY(e, k, n=None):
    e.wait_ge(k.SY.h, k.SY.n if n is None else n)


def wGQ(e, k, n=None):
    e.wait_ge(k.GQ.h, k.GQ.n if n is None else n)


def wT(e, k, t):
    if t:
        kind, n = t
        e.wait_ge(getattr(k, kind).h, n)



def setup_consts(k, b, need_identf=False):
    ident = k.sb("c_ident", [128, 128], BF16)
    identf = k.sb("c_identf", [128, 128], F32) if need_identf else None
    iot = k.sb("c_iota", [128, 128], I32)
    ceff = k.sb("c_ceff", [128, 1], F32)
    epsc = k.sb("c_eps", [128, 1], F32)
    ones_rf = k.sb("c_ones_rf", [1, 128], F32)
    onescol = k.sb("c_onescol", [128, 1], BF16)
    onescolf = k.sb("c_onescolf", [128, 1], F32)

    @b.gpsimd
    def _(e):
        k.G.inc(e.iota(iot[:], [[1, 128]], base=0, channel_multiplier=-1))
    g_iota = k.G.n

    @b.vector
    def _(e):
        wG(e, k, g_iota)
        k.V.inc(e.tensor_scalar(ident[:], iot[:], 0, None, ALU.is_equal))
        if need_identf:
            k.V.inc(e.tensor_scalar(identf[:], iot[:], 0, None, ALU.is_equal))
        k.V.inc(e.memset(epsc[:], LN_EPS))
        k.V.inc(e.memset(ones_rf[:], 1.0))
        k.V.inc(e.memset(onescol[:], 1.0))
        k.V.inc(e.memset(onescolf[:], 1.0))
        k.V.inc(e.memset(ceff[:], 1.5))
        wV(e, k)
        k.V.inc(e.tensor_scalar(ceff[:], ceff[:], C32, C32, ALU.add,
                                ALU.subtract))
        wV(e, k)
        k.V.inc(e.tensor_scalar(ceff[:], ceff[:], -1.0, 2.0, ALU.mult,
                                ALU.add))
        wV(e, k)
        k.V.inc(e.tensor_scalar(ceff[:], ceff[:], 2.0 * (C64 - C32), C32,
                                ALU.mult, ALU.add))

    return dict(ident=ident, identf=identf, iot=iot, ceff=ceff, epsc=epsc,
                ones_rf=ones_rf, onescol=onescol, onescolf=onescolf,
                v_done=k.V.n)


def emit_ln(k, b, i, xt, st, mv, rstd, cst, dep_wait):

    @b.vector
    def _(e):
        dep_wait(e)
        k.V.inc(e.bn_stats(st[:, 0:6], xt[:, 0:512]))
        k.V.inc(e.bn_stats(st[:, 6:12], xt[:, 512:1024]))
        wV(e, k)
        k.V.inc(e.bn_aggr(mv, st))

    @b.scalar
    def _(e):
        wV(e, k)
        k.A.inc(e.activation(rstd, mv[:, 1:2], AF.Ln, bias=cst["epsc"][:]))
        wA(e, k)
        k.A.inc(e.activation(rstd, rstd, AF.Exp, scale=-0.5))

    @b.vector
    def _(e):
        wA(e, k)
        k.V.inc(e.tensor_scalar(xt, xt, mv[:, 0:1], rstd, ALU.subtract,
                                ALU.mult))
    return k.V.n


class Prologue:

    def __init__(self, k, x_in, xlnT, tr_banks, col_of=None):
        self.k = k
        self.x_in = x_in
        self.xlnT = xlnT
        self.col_of = col_of or (lambda c, i: c * T + i * TT)
        self.trv = [tb[:, 0:128] for tb in tr_banks]
        self.xa = k.sb("p_x", [128, 2 * D])
        self.st = k.sb("p_st", [128, 24])
        self.mv = k.sb("p_mv", [128, 4])
        self.rstd = k.sb("p_rstd", [128, 2])
        self.lnb = k.sb("p_lnb", [128, 2 * D], BF16)
        self.s_ld = [k.sem("p_ld0"), k.sem("p_ld1")]
        self.slot_free_v = [0, 0]
        self.lnb_free_p = [0, 0]
        self.tr_free_a = [0, 0]
        self.ld_t = {}
        self.ev_a = {}

    def tile(self, b, cst, i, post_ln=None, evac_wait=None):
        k = self.k
        sl = i % 2
        xt = self.xa[:, sl * D:(sl + 1) * D]
        lt = self.lnb[:, sl * D:(sl + 1) * D]
        st = self.st[:, sl * 12:(sl + 1) * 12]
        mv = self.mv[:, sl * 2:(sl + 1) * 2]
        rstd = self.rstd[:, sl:sl + 1]
        free_v = self.slot_free_v[sl]
        lnb_free = self.lnb_free_p[sl]

        s_ld = self.s_ld[sl]

        @b.sync
        def _(e):
            if free_v:
                wV(e, k, free_v)
            s_ld.inc(e.dma_start(out=xt, in_=self.x_in[i * TT:(i + 1) * TT, :]),
                     16)
        ld = s_ld.n
        self.ld_t[i] = (s_ld, ld)

        emit_ln(k, b, i, xt, st, mv, rstd, cst,
                lambda e, s_ld=s_ld, ld=ld: e.wait_ge(s_ld.h, ld))

        if post_ln is not None and getattr(post_ln, "multi", False):
            post_ln(b, i, sl, xt, lt, lnb_free)
        else:
            @b.vector
            def _(e):
                wV(e, k)
                if lnb_free:
                    wP(e, k, lnb_free)
                if post_ln is not None:
                    last = post_ln(e, xt, lt, i)
                    if last is not None:
                        k.V.inc(last)
                else:
                    k.V.inc(e.tensor_copy(lt, xt))
        self.slot_free_v[sl] = k.V.n
        v_lt = k.V.n

        for c in range(NC_D):
            @b.tensor
            def _(e, c=c):
                if c == 0:
                    wV(e, k, max(v_lt, cst["v_done"]))
                if self.tr_free_a[c % 2]:
                    wV(e, k, self.tr_free_a[c % 2])
                k.P.inc(e.transpose(
                    self.trv[c % 2],
                    lt[:, c * 128:(c + 1) * 128],
                    cst["ident"][:]))
            p_tr = k.P.n

            @b.vector
            def _(e, c=c, p_tr=p_tr):
                wP(e, k, p_tr)
                if c == 0 and evac_wait is not None:
                    evac_wait(e)
                col = self.col_of(c, i)
                k.V.inc(e.tensor_copy(
                    self.xlnT[:, col:col + TT],
                    self.trv[c % 2]))
            self.tr_free_a[c % 2] = k.V.n
        self.lnb_free_p[sl] = k.P.n
        self.ev_a[i] = k.V.n


class Epilogue:

    def __init__(self, k, x_in, x_out, init_wait_v=0):
        self.k = k
        self.init_wait_v = init_wait_v
        self.x_in, self.x_out = x_in, x_out
        self.xo = k.sb("e_xo", [128, 2 * D])
        self.st = k.sb("e_st", [128, 24])
        self.mv = k.sb("e_mv", [128, 4])
        self.rstd = k.sb("e_rstd", [128, 2])
        self.s_ld = [k.sem("e_ld0"), k.sem("e_ld1")]
        self.s_st = [k.sem("e_st0"), k.sem("e_st1")]

    def tile(self, b, cst, i, fout_ap, ready_wait):
        k = self.k
        sl = i % 2
        xo = self.xo[:, sl * D:(sl + 1) * D]
        st = self.st[:, sl * 12:(sl + 1) * 12]
        mv = self.mv[:, sl * 2:(sl + 1) * 2]
        rstd = self.rstd[:, sl:sl + 1]
        s_ld, s_st = self.s_ld[sl], self.s_st[sl]

        @b.sync
        def _(e):
            if s_st.n:
                e.wait_ge(s_st.h, s_st.n)
            elif self.init_wait_v:
                wV(e, k, self.init_wait_v)
            s_ld.inc(e.dma_start(out=xo, in_=self.x_in[i * TT:(i + 1) * TT, :]),
                     16)
        ld = s_ld.n

        emit_ln(k, b, i, fout_ap, st, mv, rstd, cst, ready_wait)

        @b.vector
        def _(e):
            wV(e, k)
            e.wait_ge(s_ld.h, ld)
            k.V.inc(e.tensor_add(xo, xo, fout_ap))
        v_add = k.V.n

        @b.sync
        def _(e):
            wV(e, k, v_add)
            s_st.inc(e.dma_start(out=self.x_out[i * TT:(i + 1) * TT, :],
                                 in_=xo), 16)

    def finish(self, b):
        k = self.k
        s_st = self.s_st

        @b.sync
        def _(e):
            for s in s_st:
                e.wait_ge(s.h, s.n)



def build_L3f():
    k = K("l3f")
    nc = k.nc
    x_in = k.din("x", [T, D])
    twT = k.din("twT", [D, D], I32)
    bias = k.din("bias", [1, D])
    ascale = k.din("ascale", [1, 1])
    x_out = k.dout("x_new", [T, D])

    xlnT = k.sb("xlnT", [128, NC_D * T], BF16)
    wb = k.sb("wb", [128, NC_D * D], BF16)
    wi = k.sb("wi", [128, NC_D * D], I32)
    srow = k.sb("srow", [1, 2])
    scol = k.sb("scol", [128, 2])
    bsc = k.sb("bsc", [1, D])
    fout = k.sb("fout", [128, 2 * D])

    bk_mm = [k.bank("mm0"), k.bank("mm1"), k.bank("mm2"), k.bank("mm3")]
    bk_tr = [k.bankb("tr0"), k.bankb("tr1")]
    bk_bc = k.bank("bc")

    with nc.Block() as b:
        cst = setup_consts(k, b)

        @b.gpsimd
        def _(e):
            k.GQ.inc(e.dma_start(
                out=wi[:].rearrange("p (c n) -> p c n", c=NC_D),
                in_=twT.rearrange("(c p) n -> p c n", p=128)), 16)
            k.GQ.inc(e.dma_start(out=srow[:, 0:1], in_=ascale), 16)
            k.GQ.inc(e.dma_start(out=bsc[:], in_=bias), 16)
        gq_w = k.GQ.n

        @b.vector
        def _(e):
            wGQ(e, k, gq_w)
            k.V.inc(e.tensor_scalar(wb[:], wi[:], 1.0, None, ALU.subtract))
            k.V.inc(e.tensor_scalar(srow[:, 0:1], srow[:, 0:1], 1e-5, None,
                                    ALU.max))
            wV(e, k)
            k.V.inc(e.reciprocal(srow[:, 1:2], srow[:, 0:1]))
            wV(e, k)
            k.V.inc(e.tensor_scalar(bsc[:], bsc[:], srow[:, 1:2], None,
                                    ALU.mult))
        v_scale = k.V.n

        @b.tensor
        def _(e):
            wV(e, k, max(v_scale, cst["v_done"]))
            k.P.inc(e.matmul(bk_bc[0:128, 0:2], cst["ones_rf"][0:1, 0:128],
                             srow[:, 0:2], start=True, stop=True))
        p_bc = k.P.n

        @b.vector
        def _(e):
            wP(e, k, p_bc)
            k.V.inc(e.tensor_copy(scol[:], bk_bc[0:128, 0:2]))
        v_scol = k.V.n

        def post_ln(e, xt, lt, i):
            if i == 0:
                wV(e, k, v_scol)
            k.V.inc(e.tensor_scalar(xt, xt, scol[:, 1:2], 127.0, ALU.mult,
                                    ALU.min))
            wV(e, k)
            k.V.inc(e.tensor_scalar(xt, xt, -127.0, cst["ceff"][:], ALU.max,
                                    ALU.add))
            wV(e, k)
            k.V.inc(e.tensor_scalar(lt, xt, cst["ceff"][:], None,
                                    ALU.subtract))

        pro = Prologue(k, x_in, xlnT, bk_tr)
        epi = Epilogue(k, x_in, x_out)
        mm_free_v = [0, 0]
        fout_free_v = [0, 0]

        for i in range(NTT):
            pro.tile(b, cst, i, post_ln=post_ln)
            bs = (i % 2) * 2
            pA, pB = bk_mm[bs], bk_mm[bs + 1]
            ev = pro.ev_a[i]
            freev = mm_free_v[i % 2]

            @b.tensor
            def _(e, i=i, pA=pA, pB=pB, ev=ev, freev=freev):
                wV(e, k, ev)
                if i == 0:
                    wV(e, k, v_scol)
                if freev:
                    wV(e, k, freev)
                for c in range(NC_D):
                    lhs = xlnT[:, c * T + i * TT:c * T + (i + 1) * TT]
                    e.matmul(pA[:, 0:512], lhs, wb[:, c * D:c * D + 512],
                             start=(c == 0), stop=False)
                    e.matmul(pB[:, 0:512], lhs, wb[:, c * D + 512:(c + 1) * D],
                             start=(c == 0), stop=False)
                e.matmul(pA[0:128, 0:512], cst["ones_rf"][0:1, 0:128],
                         bsc[:, 0:512], start=False, stop=True)
                k.P.inc(e.matmul(pB[0:128, 0:512], cst["ones_rf"][0:1, 0:128],
                                 bsc[:, 512:1024], start=False, stop=True))
            p_mm = k.P.n

            ft = fout[:, (i % 2) * D:(i % 2 + 1) * D]
            ffv = fout_free_v[i % 2]

            @b.vector
            def _(e, p_mm=p_mm, ft=ft, ffv=ffv):
                wP(e, k, p_mm)
                k.V.inc(e.tensor_scalar(ft[:, 0:512], pA[:, 0:512],
                                        scol[:, 0:1], None, ALU.mult))
                k.V.inc(e.tensor_scalar(ft[:, 512:1024], pB[:, 0:512],
                                        scol[:, 0:1], None, ALU.mult))
            v_f = k.V.n
            mm_free_v[i % 2] = v_f

            epi.tile(b, cst, i, ft, lambda e, v_f=v_f: wV(e, k, v_f))
            fout_free_v[i % 2] = k.V.n

        epi.finish(b)

    k.close()
    return nc



def emit_colsum_to_row(k, b, cst, col_ap, row_ap, bank, wait_v):
    @b.tensor
    def _(e):
        wV(e, k, max(wait_v, cst["v_done"]))
        k.P.inc(e.matmul(bank[0:1, 0:1], col_ap, cst["onescolf"][:],
                         start=True, stop=True))
    p = k.P.n

    @b.vector
    def _(e):
        wP(e, k, p)
        k.V.inc(e.tensor_copy(row_ap, bank[0:1, 0:1]))
    return k.V.n


def emit_colmax_to_row(k, b, cst, col_ap, row_ap, bank, wait_v):
    @b.tensor
    def _(e):
        wV(e, k, max(wait_v, cst["v_done"]))
        k.P.inc(e.transpose(bank[0:1, 0:128], col_ap, cst["identf"][:]))
    p = k.P.n

    @b.vector
    def _(e):
        wP(e, k, p)
        k.V.inc(e.reduce_max(row_ap, bank[0:1, 0:128], axis=AX.X))
    return k.V.n


def emit_bcast_col(k, b, cst, row_ap, col_ap, bank, wait_v, n=1):
    @b.tensor
    def _(e):
        wV(e, k, max(wait_v, cst["v_done"]))
        k.P.inc(e.matmul(bank[0:128, 0:n], cst["ones_rf"][0:1, 0:128],
                         row_ap, start=True, stop=True))
    p = k.P.n

    @b.vector
    def _(e):
        wP(e, k, p)
        k.V.inc(e.tensor_copy(col_ap, bank[0:128, 0:n]))
    return k.V.n



def build_L2f():
    k = K("l2f")
    nc = k.nc
    F = 4096
    NC_F = F // 128
    x_in = k.din("x", [T, D])
    wupT = k.din("wupT", [D, F])
    wdnT = k.din("wdnT", [F, D])
    x_out = k.dout("x_new", [T, D])

    xlnT = k.sb("xlnT", [128, NC_D * 512], BF16)
    wub = k.sb("wub", [128, NC_D * F], BF16)
    wdb = k.sb("wdb", [128, NC_F * D], BF16)
    acol = k.sb("acol", [128, 3])
    arow = k.sb("arow", [1, 2])
    tcol = k.sb("tcol", [128, 2])
    s2r = k.sb("s2r", [1, 256])
    s2c = k.sb("s2c", [128, 2])
    rmsc = k.sb("rmsc", [128, 2])

    bk_tr = [k.bankb("tr0"), k.bankb("tr1")]
    bk_m1 = [k.bank("m1a"), k.bank("m1b")]
    bk_ss = k.bank("ss")
    bk_m2 = [k.bank("m2a"), k.bank("m2b")]
    bk_bc = k.bank("bc")

    with nc.Block() as b:
        cst = setup_consts(k, b)
        wes = ExitStack()
        wf = wes.enter_context(nc.sbuf_tensor("wf", [128, F], F32))
        tscr = wes.enter_context(nc.sbuf_tensor("tscr", [128, F], BF16))

        s_wld = [k.sem("wld0"), k.sem("wld1")]

        @b.vector
        def _(e):
            k.V.inc(e.memset(acol[:, 0:2], 0.0))
        v_ac0 = k.V.n

        def w_dma(e, ci, wslot):
            if ci < 8:
                return e.dma_start(out=wslot,
                                   in_=wupT[ci * 128:(ci + 1) * 128, :])
            cj = ci - 8
            return e.dma_start(
                out=wslot.rearrange("p (g n) -> p g n", g=4),
                in_=wdnT[cj * 512:(cj + 1) * 512, :]
                .rearrange("(g p) n -> p g n", p=128))

        slot_rd_v = 0
        for ci in range(16):
            wslot = wf[:, 0:F]

            @b.sync
            def _(e, ci=ci, wslot=wslot):
                if slot_rd_v:
                    wV(e, k, slot_rd_v)
                s_wld[0].inc(w_dma(e, ci, wslot), 16)
            ldn = s_wld[0].n

            @b.vector
            def _(e, ci=ci, wslot=wslot, ldn=ldn):
                e.wait_ge(s_wld[0].h, ldn)
                wV(e, k)
                if ci < 2:
                    wV(e, k, v_ac0)
                ac = acol[:, (0 if ci < 8 else 1):(1 if ci < 8 else 2)]
                k.V.inc(e.reduce_sum(acol[:, 2:3], wslot, axis=AX.X,
                                     apply_absolute_value=True))
                wV(e, k)
                k.V.inc(e.tensor_add(ac, ac, acol[:, 2:3]))
            slot_rd_v = k.V.n
        v_wsum = k.V.n

        v_a1 = emit_colsum_to_row(k, b, cst, acol[:, 0:1], arow[:, 0:1],
                                  bk_bc, v_wsum)
        v_a2 = emit_colsum_to_row(k, b, cst, acol[:, 1:2], arow[:, 1:2],
                                  bk_bc, v_a1)

        @b.vector
        def _(e):
            wV(e, k, v_a2)
            k.V.inc(e.tensor_scalar(arow[:, 0:2], arow[:, 0:2],
                                    0.5 / (4096.0 * 1024.0), None, ALU.mult))
        v_tau = emit_bcast_col(k, b, cst, arow[:, 0:2], tcol[:, 0:2], bk_bc,
                               k.V.n, n=2)

        slot_rd_v = 0
        for ci in range(16):
            wslot = wf[:, 0:F]

            @b.sync
            def _(e, ci=ci, wslot=wslot):
                wV(e, k, max(slot_rd_v, v_tau))
                s_wld[0].inc(w_dma(e, ci, wslot), 16)
            ldn = s_wld[0].n

            up = ci < 8
            wt = (wub[:, ci * F:(ci + 1) * F] if up else
                  wdb[:, (ci - 8) * F:(ci - 7) * F])
            tc = tcol[:, 0:1] if up else tcol[:, 1:2]

            @b.vector
            def _(e, wslot=wslot, ldn=ldn, wt=wt, tc=tc):
                e.wait_ge(s_wld[0].h, ldn)
                wV(e, k)
                k.V.inc(e.tensor_scalar(tscr[:], wslot, -1.0, tc, ALU.mult,
                                        ALU.is_gt))
                k.V.inc(e.tensor_scalar(wt, wslot, tc, None, ALU.is_gt))
                wV(e, k)
                k.V.inc(e.tensor_sub(wt, wt, tscr[:]))
            slot_rd_v = k.V.n
        v_wready = k.V.n
        wes.close()
        hbuf = k.sb("hbuf", [128, NC_F * 256], BF16)
        sq = k.sb("sq", [128, 256], BF16)
        fout = k.sb("fout", [128, D])

        def col_of(c, i):
            return c * 512 + ((i // 2) % 2) * 256 + (i % 2) * TT
        pro = Prologue(k, x_in, xlnT, bk_tr, col_of=col_of)
        pro.slot_free_v = [v_wready, v_wready]
        epi = Epilogue(k, x_in, x_out, init_wait_v=v_wready)

        def post_ln_multi(b2, i, sl, xt, lt, lnb_free):
            mvv = pro.mv[:, sl * 2 + 1:sl * 2 + 2]
            rst = pro.rstd[:, sl:sl + 1]
            rc = rmsc[:, sl:sl + 1]

            @b2.scalar
            def _(e):
                wV(e, k)
                k.A.inc(e.activation(rc, mvv, AF.Sqrt))

            @b2.vector
            def _(e):
                wA(e, k)
                if lnb_free:
                    wP(e, k, lnb_free)
                k.V.inc(e.tensor_scalar(rc, rc, rst, 1e-8, ALU.mult, ALU.add))
                wV(e, k)
                k.V.inc(e.reciprocal(rc, rc))
                wV(e, k)
                k.V.inc(e.tensor_scalar(lt, xt, rc, None, ALU.mult))
        post_ln_multi.multi = True

        m1_free_a = [0, 0]
        m2_free_v = 0
        fout_free = 0
        m1_last_p = [0, 0]

        for tci in range(NTT // 2):
            tiles = list(range(2 * tci, 2 * tci + 2))
            slw = m1_last_p[tci % 2]
            for i in tiles:
                ew = (lambda e, slw=slw: wP(e, k, slw)) if slw else None
                pro.tile(b, cst, i, post_ln=post_ln_multi, evac_wait=ew)
            ev = pro.ev_a[tiles[-1]]
            xsl = (tci % 2) * 256

            a_sq_prev = 0
            for fc in range(NC_F):
                bk = bk_m1[fc % 2]
                freea = m1_free_a[fc % 2]

                @b.tensor
                def _(e, fc=fc, bk=bk, freea=freea, ev=ev, tci=tci):
                    if fc == 0:
                        wV(e, k, ev)
                        if tci == 0:
                            wV(e, k, v_wready)
                    if freea:
                        wA(e, k, freea)
                    for c in range(NC_D):
                        k.P.inc(e.matmul(
                            bk[:, 0:256],
                            wub[:, c * F + fc * 128:c * F + (fc + 1) * 128],
                            xlnT[:, c * 512 + xsl:c * 512 + xsl + 256],
                            start=(c == 0), stop=(c == NC_D - 1)))
                p_m1 = k.P.n
                m1_last_p[tci % 2] = p_m1

                @b.scalar
                def _(e, fc=fc, bk=bk, p_m1=p_m1):
                    wP(e, k, p_m1)
                    k.A.inc(e.activation(hbuf[:, fc * 256:(fc + 1) * 256],
                                         bk[:, 0:256], AF.Gelu))
                    wA(e, k)
                    k.A.inc(e.activation(sq[:],
                                         hbuf[:, fc * 256:(fc + 1) * 256],
                                         AF.Square))
                a_sq = k.A.n
                m1_free_a[fc % 2] = a_sq

                @b.tensor
                def _(e, fc=fc, a_sq=a_sq):
                    wA(e, k, a_sq)
                    k.P.inc(e.matmul(bk_ss[0:1, 0:256], cst["onescol"][:],
                                     sq[:], start=(fc == 0),
                                     stop=(fc == NC_F - 1)))
            p_ss = k.P.n

            @b.scalar
            def _(e, p_ss=p_ss):
                wP(e, k, p_ss)
                k.A.inc(e.activation(s2r[:], bk_ss[0:1, 0:256], AF.Sqrt,
                                     scale=1.0 / 4096.0))
            a_s2 = k.A.n

            @b.vector
            def _(e, a_s2=a_s2):
                wA(e, k, a_s2)
                k.V.inc(e.tensor_scalar(s2r[:], s2r[:], 1e-8, None, ALU.add))
                wV(e, k)
                k.V.inc(e.reciprocal(s2r[:], s2r[:]))
            v_s2 = k.V.n

            @b.tensor
            def _(e, v_s2=v_s2):
                wV(e, k, v_s2)
                for j in range(2):
                    k.P.inc(e.matmul(bk_bc[0:128, j:j + 1],
                                     s2r[0:1, j * 128:(j + 1) * 128],
                                     cst["ones_rf"][0:1, 0:1],
                                     start=True, stop=True))
            p_s2c = k.P.n

            @b.vector
            def _(e, p_s2c=p_s2c):
                wP(e, k, p_s2c)
                k.V.inc(e.tensor_copy(s2c[:, 0:2], bk_bc[0:128, 0:2]))
            v_s2c = k.V.n

            for j, i in enumerate(tiles):
                @b.tensor
                def _(e, j=j, i=i, v_s2c=v_s2c):
                    if j == 0:
                        wV(e, k, v_s2c)
                    if m2_free_v:
                        wV(e, k, m2_free_v)
                    for half in range(2):
                        bk = bk_m2[half]
                        for fc in range(NC_F):
                            k.P.inc(e.matmul(
                                bk[:, 0:512],
                                hbuf[:, fc * 256 + j * 128:
                                     fc * 256 + (j + 1) * 128],
                                wdb[:, fc * D + half * 512:
                                    fc * D + (half + 1) * 512],
                                start=(fc == 0), stop=(fc == NC_F - 1)))
                p_m2 = k.P.n

                ft = fout[:, 0:D]
                ffv = fout_free

                @b.vector
                def _(e, p_m2=p_m2, ft=ft, ffv=ffv, j=j):
                    wP(e, k, p_m2)
                    if ffv:
                        wV(e, k, ffv)
                    k.V.inc(e.tensor_scalar(ft[:, 0:512], bk_m2[0][:, 0:512],
                                            s2c[:, j:j + 1], None, ALU.mult))
                    k.V.inc(e.tensor_scalar(ft[:, 512:1024],
                                            bk_m2[1][:, 0:512],
                                            s2c[:, j:j + 1], None, ALU.mult))
                v_f = k.V.n
                m2_free_v = v_f

                epi.tile(b, cst, i, ft, lambda e, v_f=v_f: wV(e, k, v_f))
                fout_free = k.V.n

        epi.finish(b)

    k.close()
    return nc


def run_L2f(x_full, w_up, w_down):
    nc = build_L2f()
    shards = np_shard_x(x_full)
    wupT = np.ascontiguousarray(np.asarray(w_up).T)
    wdnT = np.ascontiguousarray(np.asarray(w_down).T)
    in_maps = [{"x": s, "wupT": wupT, "wdnT": wdnT} for s in shards]
    res = run_bass_kernel_spmd(nc, in_maps, core_ids=list(range(NCORES)))
    return np.concatenate([r["x_new"].reshape(B_PER, S, D)
                           for r in res.results], axis=0)



def build_L1f():
    k = K("l1f")
    nc = k.nc
    F = 4096
    NC_F = F // 128
    x_in = k.din("x", [T, D])
    wfcT = k.din("wfcT", [D, F])
    wprT = k.din("wprT", [F, D])
    afcT = k.din("afcT", [D, 16])
    bfcT = k.din("bfcT", [16, F])
    aprT = k.din("aprT", [F, 16])
    bprT = k.din("bprT", [16, D])
    bfc_pc = k.din("bfc_pc", [128, 32])
    bpr = k.din("bpr", [1, D])
    x_out = k.dout("x_new", [T, D])

    wqf = k.sb("wqf", [128, NC_D * F], BF16)
    wqp = k.sb("wqp", [128, NC_F * D], BF16)
    afb = k.sb("afb", [128, NC_D * 16], BF16)
    bfb = k.sb("bfb", [16, F], BF16)
    apb = k.sb("apb", [128, NC_F * 16], BF16)
    bpb = k.sb("bpb", [16, D], BF16)
    bfcc = k.sb("bfcc", [128, 32])
    bprr = k.sb("bprr", [1, D])
    xaT = k.sb("xaT", [16, 256], BF16)
    haT = k.sb("haT", [16, 256], BF16)
    mcol = k.sb("mcol", [128, 3])
    mrow = k.sb("mrow", [1, 4])
    scl = k.sb("scl", [128, 4])
    xlnT = k.sb("xlnT", [128, NC_D * 512], BF16)

    bk_tr = [k.bankb("tr0"), k.bankb("tr1")]
    bk_m1 = [k.bank("m1a"), k.bank("m1b")]
    bk_lo = k.bank("lo")
    bk_m2 = [k.bank("m2a"), k.bank("m2b")]
    bk_bc = k.bank("bc")

    with nc.Block() as b:
        cst = setup_consts(k, b, need_identf=True)
        wes = ExitStack()
        wf = wes.enter_context(nc.sbuf_tensor("wf", [128, F], F32))

        s_wld = [k.sem("wld0")]

        @b.gpsimd
        def _(e):
            k.GQ.inc(e.dma_start(out=afb[:].rearrange("p (c n) -> p c n", c=NC_D),
                                 in_=afcT.rearrange("(c p) n -> p c n", p=128)), 16)
            k.GQ.inc(e.dma_start(out=bfb[:], in_=bfcT), 16)
            k.GQ.inc(e.dma_start(out=apb[:].rearrange("p (c n) -> p c n", c=NC_F),
                                 in_=aprT.rearrange("(c p) n -> p c n", p=128)), 16)
            k.GQ.inc(e.dma_start(out=bpb[:], in_=bprT), 16)
        gq_small = k.GQ.n

        @b.sync
        def _(e):
            k.SY.inc(e.dma_start(out=bfcc[:], in_=bfc_pc), 16)
            k.SY.inc(e.dma_start(out=bprr[:], in_=bpr), 16)
        sy_b = k.SY.n

        @b.vector
        def _(e):
            k.V.inc(e.memset(mcol[:, 0:2], 0.0))
        v_m0 = k.V.n

        def w_dma(e, ci, dst):
            if ci < 8:
                return e.dma_start(out=dst, in_=wfcT[ci * 128:(ci + 1) * 128, :])
            cj = ci - 8
            return e.dma_start(
                out=dst.rearrange("p (g n) -> p g n", g=4),
                in_=wprT[cj * 512:(cj + 1) * 512, :]
                .rearrange("(g p) n -> p g n", p=128))

        slot_rd = 0
        for ci in range(16):
            @b.sync
            def _(e, ci=ci):
                if slot_rd:
                    wV(e, k, slot_rd)
                s_wld[0].inc(w_dma(e, ci, wf[:]), 16)
            ldn = s_wld[0].n

            @b.vector
            def _(e, ci=ci, ldn=ldn):
                e.wait_ge(s_wld[0].h, ldn)
                wV(e, k)
                if ci < 2:
                    wV(e, k, v_m0)
                mc = mcol[:, (0 if ci < 8 else 1):(1 if ci < 8 else 2)]
                k.V.inc(e.reduce_max(mcol[:, 2:3], wf[:], axis=AX.X,
                                     apply_absolute_value=True))
                wV(e, k)
                k.V.inc(e.tensor_max(mc, mc, mcol[:, 2:3]))
            slot_rd = k.V.n

        v_x1 = emit_colmax_to_row(k, b, cst, mcol[:, 0:1], mrow[:, 0:1],
                                  bk_bc, slot_rd)
        v_x2 = emit_colmax_to_row(k, b, cst, mcol[:, 1:2], mrow[:, 2:3],
                                  bk_bc, v_x1)

        @b.vector
        def _(e):
            wV(e, k, v_x2)
            k.V.inc(e.tensor_scalar(mrow[:, 0:1], mrow[:, 0:1], 1.0 / 127.0,
                                    None, ALU.mult))
            k.V.inc(e.tensor_scalar(mrow[:, 2:3], mrow[:, 2:3], 1.0 / 127.0,
                                    None, ALU.mult))
            wV(e, k)
            k.V.inc(e.reciprocal(mrow[:, 1:2], mrow[:, 0:1]))
            k.V.inc(e.reciprocal(mrow[:, 3:4], mrow[:, 2:3]))
        v_s = emit_bcast_col(k, b, cst, mrow[:, 0:4], scl[:, 0:4], bk_bc,
                             k.V.n, n=4)

        slot_rd = 0
        for ci in range(16):
            @b.sync
            def _(e, ci=ci):
                wV(e, k, max(slot_rd, v_s))
                s_wld[0].inc(w_dma(e, ci, wf[:]), 16)
            ldn = s_wld[0].n

            up = ci < 8
            wt = (wqf[:, ci * F:(ci + 1) * F] if up else
                  wqp[:, (ci - 8) * F:(ci - 7) * F])
            rsc = scl[:, 1:2] if up else scl[:, 3:4]
            sc = scl[:, 0:1] if up else scl[:, 2:3]

            @b.vector
            def _(e, ldn=ldn, wt=wt, rsc=rsc, sc=sc):
                e.wait_ge(s_wld[0].h, ldn)
                wV(e, k)
                k.V.inc(e.tensor_scalar(wf[:], wf[:], rsc, 127.0, ALU.mult,
                                        ALU.min))
                wV(e, k)
                k.V.inc(e.tensor_scalar(wf[:], wf[:], -128.0, cst["ceff"][:],
                                        ALU.max, ALU.add))
                wV(e, k)
                k.V.inc(e.tensor_scalar(wf[:], wf[:], cst["ceff"][:], None,
                                        ALU.subtract))
                wV(e, k)
                k.V.inc(e.tensor_scalar(wt, wf[:], sc, None, ALU.mult))
            slot_rd = k.V.n
        v_wready = k.V.n
        wes.close()
        hbuf = k.sb("hbuf", [128, NC_F * 256], BF16)
        fout = k.sb("fout", [128, D])

        def col_of(c, i):
            return c * 512 + ((i // 2) % 2) * 256 + (i % 2) * TT
        pro = Prologue(k, x_in, xlnT, bk_tr, col_of=col_of)
        pro.slot_free_v = [v_wready, v_wready]
        epi = Epilogue(k, x_in, x_out, init_wait_v=v_wready)

        m1_free_a = [0, 0]
        m2_free_v = 0
        fout_free = 0
        m1_last_p = [0, 0]
        xa_free_p = 0
        ha_free_p = 0

        for tci in range(NTT // 2):
            tiles = list(range(2 * tci, 2 * tci + 2))
            slw = m1_last_p[tci % 2]
            for i in tiles:
                ew = (lambda e, slw=slw: wP(e, k, slw)) if slw else None
                pro.tile(b, cst, i, evac_wait=ew)
            ev = pro.ev_a[tiles[-1]]
            xsl = (tci % 2) * 256

            @b.tensor
            def _(e, ev=ev, xsl=xsl, tci=tci, xa_free_p=xa_free_p):
                wV(e, k, ev)
                if tci == 0:
                    wV(e, k, v_wready)
                    wGQ(e, k, gq_small)
                    wSY(e, k, sy_b)
                for c in range(NC_D):
                    k.P.inc(e.matmul(bk_lo[0:16, 0:256],
                                     afb[:, c * 16:(c + 1) * 16],
                                     xlnT[:, c * 512 + xsl:c * 512 + xsl + 256],
                                     start=(c == 0), stop=(c == NC_D - 1)))
            p_xa = k.P.n

            @b.vector
            def _(e, p_xa=p_xa):
                wP(e, k, p_xa)
                k.V.inc(e.tensor_scalar(xaT[:, 0:256], bk_lo[0:16, 0:256],
                                        2.0, None, ALU.mult))
            v_xa = k.V.n

            for fc in range(NC_F):
                bk = bk_m1[fc % 2]
                freea = m1_free_a[fc % 2]

                @b.tensor
                def _(e, fc=fc, bk=bk, freea=freea, v_xa=v_xa, xsl=xsl):
                    if fc == 0:
                        wV(e, k, v_xa)
                    if freea:
                        wA(e, k, freea)
                    for c in range(NC_D):
                        e.matmul(bk[:, 0:256],
                                 wqf[:, c * F + fc * 128:c * F + (fc + 1) * 128],
                                 xlnT[:, c * 512 + xsl:c * 512 + xsl + 256],
                                 start=(c == 0), stop=False)
                    k.P.inc(e.matmul(bk[:, 0:256],
                                     bfb[:, fc * 128:(fc + 1) * 128],
                                     xaT[:, 0:256], start=False, stop=True))
                p_m1 = k.P.n
                m1_last_p[tci % 2] = p_m1

                @b.scalar
                def _(e, fc=fc, bk=bk, p_m1=p_m1):
                    wP(e, k, p_m1)
                    k.A.inc(e.activation(hbuf[:, fc * 256:(fc + 1) * 256],
                                         bk[:, 0:256], AF.Gelu,
                                         bias=bfcc[:, fc:fc + 1]))
                m1_free_a[fc % 2] = k.A.n
            a_h = k.A.n

            @b.tensor
            def _(e, a_h=a_h):
                wA(e, k, a_h)
                for fc in range(NC_F):
                    k.P.inc(e.matmul(bk_lo[0:16, 0:256],
                                     apb[:, fc * 16:(fc + 1) * 16],
                                     hbuf[:, fc * 256:(fc + 1) * 256],
                                     start=(fc == 0), stop=(fc == NC_F - 1)))
            p_ha = k.P.n

            @b.vector
            def _(e, p_ha=p_ha):
                wP(e, k, p_ha)
                k.V.inc(e.tensor_scalar(haT[:, 0:256], bk_lo[0:16, 0:256],
                                        2.0, None, ALU.mult))
            v_ha = k.V.n

            for j, i in enumerate(tiles):
                @b.tensor
                def _(e, j=j, v_ha=v_ha, m2f=m2_free_v):
                    if j == 0:
                        wV(e, k, v_ha)
                    if m2f:
                        wV(e, k, m2f)
                    for half in range(2):
                        bk = bk_m2[half]
                        for fc in range(NC_F):
                            e.matmul(bk[:, 0:512],
                                     hbuf[:, fc * 256 + j * 128:
                                          fc * 256 + (j + 1) * 128],
                                     wqp[:, fc * D + half * 512:
                                         fc * D + (half + 1) * 512],
                                     start=(fc == 0), stop=False)
                        e.matmul(bk[:, 0:512], haT[:, j * 128:(j + 1) * 128],
                                 bpb[:, half * 512:(half + 1) * 512],
                                 start=False, stop=False)
                        k.P.inc(e.matmul(bk[0:128, 0:512],
                                         cst["ones_rf"][0:1, 0:128],
                                         bprr[:, half * 512:(half + 1) * 512],
                                         start=False, stop=True))
                p_m2 = k.P.n

                ft = fout[:, 0:D]
                ffv = fout_free

                @b.vector
                def _(e, p_m2=p_m2, ft=ft, ffv=ffv):
                    wP(e, k, p_m2)
                    if ffv:
                        wV(e, k, ffv)
                    k.V.inc(e.tensor_copy(ft[:, 0:512], bk_m2[0][:, 0:512]))
                    k.V.inc(e.tensor_copy(ft[:, 512:1024], bk_m2[1][:, 0:512]))
                v_f = k.V.n
                m2_free_v = v_f

                epi.tile(b, cst, i, ft, lambda e, v_f=v_f: wV(e, k, v_f))
                fout_free = k.V.n

        epi.finish(b)

    k.close()
    return nc


def run_L1f(x_full, W_fc, b_fc, A_fc, B_fc, W_proj, b_proj, A_proj, B_proj):
    nc = build_L1f()
    shards = np_shard_x(x_full)
    cc = np.ascontiguousarray
    in_common = {
        "wfcT": cc(np.asarray(W_fc).T), "wprT": cc(np.asarray(W_proj).T),
        "afcT": cc(np.asarray(A_fc).T), "bfcT": cc(np.asarray(B_fc).T),
        "aprT": cc(np.asarray(A_proj).T), "bprT": cc(np.asarray(B_proj).T),
        "bfc_pc": cc(np.asarray(b_fc).reshape(32, 128).T),
        "bpr": cc(np.asarray(b_proj).reshape(1, D)),
    }
    in_maps = [dict(in_common, x=s) for s in shards]
    res = run_bass_kernel_spmd(nc, in_maps, core_ids=list(range(NCORES)))
    return np.concatenate([r["x_new"].reshape(B_PER, S, D)
                           for r in res.results], axis=0)



def build_maxln():
    k = K("maxln")
    nc = k.nc
    x_in = k.din("x", [T, D])
    pmax = k.dout("pmax", [1, 1])

    xa = k.sb("xa", [128, 2 * D])
    st = k.sb("st", [128, 24])
    mv = k.sb("mv", [128, 4])
    rstd = k.sb("rstd", [128, 2])
    mcol = k.sb("mcol", [128, 2])
    mrow = k.sb("mrow", [1, 1])
    bk = k.bank("b0")
    s_ld = [k.sem("ld0"), k.sem("ld1")]

    with nc.Block() as b:
        cst = setup_consts(k, b, need_identf=True)

        @b.vector
        def _(e):
            k.V.inc(e.memset(mcol[:, 0:1], -1e30))
        free_v = [k.V.n, k.V.n]

        for i in range(NTT):
            sl = i % 2
            xt = xa[:, sl * D:(sl + 1) * D]

            @b.sync
            def _(e, i=i, sl=sl, xt=xt):
                wV(e, k, free_v[sl])
                s_ld[sl].inc(e.dma_start(
                    out=xt, in_=x_in[i * TT:(i + 1) * TT, :]), 16)
            ldn = s_ld[sl].n

            emit_ln(k, b, i, xt, st[:, sl * 12:(sl + 1) * 12],
                    mv[:, sl * 2:(sl + 1) * 2], rstd[:, sl:sl + 1], cst,
                    lambda e, sl=sl, ldn=ldn: e.wait_ge(s_ld[sl].h, ldn))

            @b.vector
            def _(e, xt=xt):
                wV(e, k)
                k.V.inc(e.reduce_max(mcol[:, 1:2], xt, axis=AX.X))
                wV(e, k)
                k.V.inc(e.tensor_max(mcol[:, 0:1], mcol[:, 0:1], mcol[:, 1:2]))
            free_v[sl] = k.V.n

        v_m = emit_colmax_to_row(k, b, cst, mcol[:, 0:1], mrow[:, 0:1],
                                 bk, k.V.n)

        @b.sync
        def _(e):
            wV(e, k, v_m)
            k.SY.inc(e.dma_start(out=pmax, in_=mrow[:, 0:1]), 16)
            wSY(e, k)

    k.close()
    return nc



def build_L0f1():
    k = K("l0f1")
    nc = k.nc
    F = 4096
    NC_F = F // 128
    x_in = k.din("x", [T, D])
    pmax8 = k.din("pmax8", [1, 8])
    w1 = k.din("w1", [F, D])
    b1c = k.din("b1c", [128, 32])
    h_out = k.dout("h_out", [32, 128, T], BF16)
    pmaxh = k.dout("pmaxh", [1, 1])

    wq1T = k.sb("wq1T", [128, NC_D * F], BF16)
    xlnT = k.sb("xlnT", [128, NC_D * 512], BF16)
    sw = k.sb("sw", [128, 32])
    bq = k.sb("bq", [128, 32])
    srow = k.sb("srow", [1, 4])
    scol = k.sb("scol", [128, 2])
    hstage = k.sb("hstage", [128, 2 * 512], BF16)
    hmaxc = k.sb("hmaxc", [128, 3])
    hmaxr = k.sb("hmaxr", [1, 1])
    pm8 = k.sb("pm8", [1, 8])

    bk_tr = [k.bankb("tr0"), k.bankb("tr1")]
    bk_m1 = [k.bank("m1a"), k.bank("m1b")]
    bk_bc = k.bank("bc")

    with nc.Block() as b:
        cst = setup_consts(k, b, need_identf=True)
        wes = ExitStack()
        wstg = wes.enter_context(nc.sbuf_tensor("wstg", [128, D], F32))
        wqs = wes.enter_context(nc.sbuf_tensor("wqs", [128, D], BF16))

        @b.sync
        def _(e):
            k.SY.inc(e.dma_start(out=pm8[:], in_=pmax8), 16)
            k.SY.inc(e.dma_start(out=bq[:, 0:32], in_=b1c), 16)
        sy_p2 = k.SY.n

        @b.vector
        def _(e):
            e.wait_ge(k.SY.h, sy_p2)
            k.V.inc(e.reduce_max(srow[:, 0:1], pm8[:], axis=AX.X))
            wV(e, k)
            k.V.inc(e.tensor_scalar(srow[:, 1:2], srow[:, 0:1], 1.0 / 255.0,
                                    None, ALU.mult))
            wV(e, k)
            k.V.inc(e.reciprocal(srow[:, 2:3], srow[:, 1:2]))
        v_sin = emit_bcast_col(k, b, cst, srow[:, 1:3], scol[:, 0:2], bk_bc,
                               k.V.n, n=2)

        s_wld = [k.sem("wld0")]
        slot_rd = 0
        tr_free_a = [0, 0]
        wqs_free_p = 0
        trv = [tb[:, 0:128] for tb in bk_tr]
        for fc in range(32):
            @b.sync
            def _(e, fc=fc):
                if slot_rd:
                    wV(e, k, slot_rd)
                s_wld[0].inc(e.dma_start(
                    out=wstg[:], in_=w1[fc * 128:(fc + 1) * 128, :]), 16)
            ldn = s_wld[0].n

            @b.vector
            def _(e, fc=fc, ldn=ldn, wqs_free_p=wqs_free_p):
                e.wait_ge(s_wld[0].h, ldn)
                wV(e, k)
                if wqs_free_p:
                    wP(e, k, wqs_free_p)
                k.V.inc(e.reduce_max(sw[:, fc:fc + 1], wstg[:], axis=AX.X,
                                     apply_absolute_value=True))
                wV(e, k)
                k.V.inc(e.tensor_scalar(sw[:, fc:fc + 1], sw[:, fc:fc + 1],
                                        1e-8, 1.0 / 127.0, ALU.max, ALU.mult))
                wV(e, k)
                k.V.inc(e.reciprocal(hmaxc[:, 2:3], sw[:, fc:fc + 1]))
                wV(e, k)
                k.V.inc(e.tensor_scalar(wstg[:], wstg[:], hmaxc[:, 2:3],
                                        127.0, ALU.mult, ALU.min))
                wV(e, k)
                k.V.inc(e.tensor_scalar(wstg[:], wstg[:], -128.0,
                                        cst["ceff"][:], ALU.max, ALU.add))
                wV(e, k)
                k.V.inc(e.tensor_scalar(wstg[:], wstg[:], cst["ceff"][:],
                                        None, ALU.subtract))
                wV(e, k)
                k.V.inc(e.tensor_scalar(wqs[:], wstg[:], sw[:, fc:fc + 1],
                                        None, ALU.mult))
            slot_rd = k.V.n
            v_wq = k.V.n

            for c in range(NC_D):
                @b.tensor
                def _(e, fc=fc, c=c, v_wq=v_wq):
                    if c == 0:
                        wV(e, k, max(v_wq, cst["v_done"]))
                    if tr_free_a[c % 2]:
                        wV(e, k, tr_free_a[c % 2])
                    k.P.inc(e.transpose(trv[c % 2],
                                        wqs[:, c * 128:(c + 1) * 128],
                                        cst["ident"][:]))
                p_tr = k.P.n

                @b.vector
                def _(e, fc=fc, c=c, p_tr=p_tr):
                    wP(e, k, p_tr)
                    k.V.inc(e.tensor_copy(
                        wq1T[:, c * F + fc * 128:c * F + (fc + 1) * 128],
                        trv[c % 2]))
                tr_free_a[c % 2] = k.V.n
            wqs_free_p = k.P.n
        a_wq = k.V.n

        @b.vector
        def _(e):
            wV(e, k, a_wq)
            wV(e, k, v_sin)
            for fc in range(32):
                k.V.inc(e.reciprocal(hmaxc[:, 2:3], sw[:, fc:fc + 1]))
                wV(e, k)
                k.V.inc(e.tensor_scalar(bq[:, fc:fc + 1], bq[:, fc:fc + 1],
                                        scol[:, 1:2], hmaxc[:, 2:3],
                                        ALU.mult, ALU.mult))
                wV(e, k)
                k.V.inc(e.tensor_scalar(bq[:, fc:fc + 1], bq[:, fc:fc + 1],
                                        cst["ceff"][:], cst["ceff"][:],
                                        ALU.add, ALU.subtract))
                wV(e, k)
                k.V.inc(e.tensor_scalar(bq[:, fc:fc + 1], bq[:, fc:fc + 1],
                                        scol[:, 0:1], sw[:, fc:fc + 1],
                                        ALU.mult, ALU.mult))
                wV(e, k)
            k.V.inc(e.memset(hmaxc[:, 0:1], 0.0))
        v_bq = k.V.n
        wes.close()

        def post_ln(e, xt, lt, i):
            if i == 0:
                wV(e, k, v_bq)
            k.V.inc(e.tensor_scalar(xt, xt, scol[:, 1:2], 255.0, ALU.mult,
                                    ALU.min))
            wV(e, k)
            k.V.inc(e.tensor_scalar(xt, xt, 0.0, cst["ceff"][:], ALU.max,
                                    ALU.add))
            wV(e, k)
            return e.tensor_scalar(lt, xt, cst["ceff"][:], None, ALU.subtract)

        def col_of(c, i):
            return c * 512 + ((i // 2) % 2) * 256 + (i % 2) * TT
        pro = Prologue(k, x_in, xlnT, bk_tr, col_of=col_of)
        pro.slot_free_v = [v_bq, v_bq]

        s_hst = [k.sem("hst0"), k.sem("hst1")]
        m1_last_p = [0, 0]
        m1_free_a = [0, 0]

        for tci in range(NTT // 2):
            tiles = list(range(2 * tci, 2 * tci + 2))
            slw = m1_last_p[tci % 2]
            for i in tiles:
                ew = (lambda e, slw=slw: wP(e, k, slw)) if slw else None
                pro.tile(b, cst, i, post_ln=post_ln, evac_wait=ew)
            ev = pro.ev_a[tiles[-1]]
            xsl = (tci % 2) * 256
            t0 = tci * 256

            for fc in range(NC_F):
                bk = bk_m1[fc % 2]

                @b.tensor
                def _(e, fc=fc, bk=bk, ev=ev, xsl=xsl, tci=tci,
                      freea=m1_free_a[fc % 2]):
                    if fc == 0:
                        wV(e, k, ev)
                    if freea:
                        wA(e, k, freea)
                    for c in range(NC_D):
                        k.P.inc(e.matmul(
                            bk[:, 0:256],
                            wq1T[:, c * F + fc * 128:c * F + (fc + 1) * 128],
                            xlnT[:, c * 512 + xsl:c * 512 + xsl + 256],
                            start=(c == 0), stop=(c == NC_D - 1)))
                p_m1 = k.P.n
                m1_last_p[tci % 2] = p_m1

                hs = hstage[:, (fc % 2) * 512:(fc % 2) * 512 + 256]

                @b.scalar
                def _(e, fc=fc, bk=bk, p_m1=p_m1, hs=hs):
                    wP(e, k, p_m1)
                    e.wait_ge(s_hst[fc % 2].h, s_hst[fc % 2].n)
                    k.A.inc(e.activation(hs, bk[:, 0:256], AF.Relu,
                                         bias=bq[:, fc:fc + 1],
                                         scale=scol[:, 0:1]))
                a_h = k.A.n
                m1_free_a[fc % 2] = a_h

                @b.vector
                def _(e, hs=hs, a_h=a_h):
                    wA(e, k, a_h)
                    wV(e, k)
                    k.V.inc(e.reduce_max(hmaxc[:, 1:2], hs, axis=AX.X))
                    wV(e, k)
                    k.V.inc(e.tensor_max(hmaxc[:, 0:1], hmaxc[:, 0:1],
                                         hmaxc[:, 1:2]))
                v_hm = k.V.n

                @b.sync
                def _(e, fc=fc, hs=hs, v_hm=v_hm, t0=t0):
                    wV(e, k, v_hm)
                    s_hst[fc % 2].inc(e.dma_start(
                        out=h_out[fc, :, t0:t0 + 256], in_=hs), 16)

        v_hmax = emit_colmax_to_row(k, b, cst, hmaxc[:, 0:1], hmaxr[:, 0:1],
                                    bk_bc, k.V.n)

        @b.sync
        def _(e):
            wV(e, k, v_hmax)
            k.SY.inc(e.dma_start(out=pmaxh, in_=hmaxr[:, 0:1]), 16)
            for s in s_hst:
                e.wait_ge(s.h, s.n)
            wSY(e, k)

    k.close()
    return nc



def build_L0f2():
    k = K("l0f2")
    nc = k.nc
    F = 4096
    NC_F = F // 128
    x_in = k.din("x", [T, D])
    h_in = k.din("h_in", [32, 128, T], BF16)
    pmaxh8 = k.din("pmaxh8", [1, 8])
    w2 = k.din("w2", [D, F])
    b2c = k.din("b2c", [128, 8])
    x_out = k.dout("x_new", [T, D])

    wq2T = k.sb("wq2T", [128, NC_F * D], BF16)
    sw = k.sb("sw", [128, 8])
    bq = k.sb("bq", [128, 8])
    pm8 = k.sb("pm8", [1, 8])
    srow = k.sb("srow", [1, 4])
    scol = k.sb("scol", [128, 2])
    foutT = k.sb("foutT", [128, NC_D * 512], BF16)
    fout = k.sb("fout", [128, D])
    rcol = k.sb("rcol", [128, 1])

    bk_tr = [k.bankb("tr0"), k.bankb("tr1")]
    bk_m2 = [k.bank("m2a"), k.bank("m2b")]
    bk_bc = k.bank("bc")

    with nc.Block() as b:
        cst = setup_consts(k, b, need_identf=True)
        wes = ExitStack()
        wstg = wes.enter_context(nc.sbuf_tensor("wstg", [128, F], F32))
        wqs = wes.enter_context(nc.sbuf_tensor("wqs", [128, F], BF16))

        @b.sync
        def _(e):
            k.SY.inc(e.dma_start(out=pm8[:], in_=pmaxh8), 16)
            k.SY.inc(e.dma_start(out=bq[:, 0:8], in_=b2c), 16)
        sy_p = k.SY.n

        @b.vector
        def _(e):
            e.wait_ge(k.SY.h, sy_p)
            k.V.inc(e.reduce_max(srow[:, 0:1], pm8[:], axis=AX.X))
            wV(e, k)
            k.V.inc(e.tensor_scalar(srow[:, 1:2], srow[:, 0:1], 1.0 / 255.0,
                                    None, ALU.mult))
            wV(e, k)
            k.V.inc(e.reciprocal(srow[:, 2:3], srow[:, 1:2]))
        v_sh = emit_bcast_col(k, b, cst, srow[:, 1:3], scol[:, 0:2], bk_bc,
                              k.V.n, n=2)

        s_wld = [k.sem("wld0")]
        slot_rd = 0
        tr_free_a = [0, 0]
        wqs_free_p = 0
        trv = [tb[:, 0:128] for tb in bk_tr]
        for oc in range(NC_D):
            @b.sync
            def _(e, oc=oc):
                if slot_rd:
                    wV(e, k, slot_rd)
                s_wld[0].inc(e.dma_start(
                    out=wstg[:], in_=w2[oc * 128:(oc + 1) * 128, :]), 16)
            ldn = s_wld[0].n

            @b.vector
            def _(e, oc=oc, ldn=ldn, wqs_free_p=wqs_free_p):
                e.wait_ge(s_wld[0].h, ldn)
                wV(e, k)
                if wqs_free_p:
                    wP(e, k, wqs_free_p)
                k.V.inc(e.reduce_max(sw[:, oc:oc + 1], wstg[:], axis=AX.X,
                                     apply_absolute_value=True))
                wV(e, k)
                k.V.inc(e.tensor_scalar(sw[:, oc:oc + 1], sw[:, oc:oc + 1],
                                        1e-8, 1.0 / 127.0, ALU.max, ALU.mult))
                wV(e, k)
                k.V.inc(e.reciprocal(rcol[:], sw[:, oc:oc + 1]))
                wV(e, k)
                k.V.inc(e.tensor_scalar(wstg[:], wstg[:], rcol[:], 127.0,
                                        ALU.mult, ALU.min))
                wV(e, k)
                k.V.inc(e.tensor_scalar(wstg[:], wstg[:], -128.0,
                                        cst["ceff"][:], ALU.max, ALU.add))
                wV(e, k)
                k.V.inc(e.tensor_scalar(wstg[:], wstg[:], cst["ceff"][:],
                                        None, ALU.subtract))
                wV(e, k)
                k.V.inc(e.tensor_scalar(wqs[:], wstg[:], sw[:, oc:oc + 1],
                                        None, ALU.mult))
            slot_rd = k.V.n
            v_wq = k.V.n

            for fc in range(NC_F):
                @b.tensor
                def _(e, oc=oc, fc=fc, v_wq=v_wq):
                    if fc == 0:
                        wV(e, k, max(v_wq, cst["v_done"]))
                    if tr_free_a[fc % 2]:
                        wV(e, k, tr_free_a[fc % 2])
                    k.P.inc(e.transpose(trv[fc % 2],
                                        wqs[:, fc * 128:(fc + 1) * 128],
                                        cst["ident"][:]))
                p_tr = k.P.n

                @b.vector
                def _(e, oc=oc, fc=fc, p_tr=p_tr):
                    wP(e, k, p_tr)
                    k.V.inc(e.tensor_copy(
                        wq2T[:, fc * D + oc * 128:fc * D + (oc + 1) * 128],
                        trv[fc % 2]))
                tr_free_a[fc % 2] = k.V.n
            wqs_free_p = k.P.n
        a_wq = k.V.n

        @b.vector
        def _(e):
            wV(e, k, a_wq)
            wV(e, k, v_sh)
            for oc in range(NC_D):
                k.V.inc(e.reciprocal(rcol[:], sw[:, oc:oc + 1]))
                wV(e, k)
                k.V.inc(e.tensor_scalar(bq[:, oc:oc + 1], bq[:, oc:oc + 1],
                                        scol[:, 1:2], rcol[:], ALU.mult,
                                        ALU.mult))
                wV(e, k)
                k.V.inc(e.tensor_scalar(bq[:, oc:oc + 1], bq[:, oc:oc + 1],
                                        cst["ceff"][:], cst["ceff"][:],
                                        ALU.add, ALU.subtract))
                wV(e, k)
                k.V.inc(e.tensor_scalar(bq[:, oc:oc + 1], bq[:, oc:oc + 1],
                                        scol[:, 0:1], sw[:, oc:oc + 1],
                                        ALU.mult, ALU.mult))
                wV(e, k)
        v_bq = k.V.n
        wes.close()
        hq = k.sb("hq", [128, NC_F * 512], BF16)
        qscr = k.sb("qscr", [128, 512])

        epi = Epilogue(k, x_in, x_out, init_wait_v=v_bq)
        s_hld = [k.sem("hld0"), k.sem("hld1")]
        hq_rd_p = [0, 0]
        m2_free_v = 0
        foutT_free_p = 0
        fout_free_v = 0

        for tci in range(NTT // 4):
            t0 = tci * 512
            hsl = tci % 2

            @b.sync
            def _(e, t0=t0, hsl=hsl, tci=tci):
                if hq_rd_p[hsl]:
                    wP(e, k, hq_rd_p[hsl])
                if tci == 0:
                    wV(e, k, v_bq)
                for fc in range(NC_F):
                    s_hld[hsl].inc(e.dma_start(
                        out=hq[:, fc * 512:(fc + 1) * 512],
                        in_=h_in[fc, :, t0:t0 + 512]), 16)
            ldn = s_hld[hsl].n

            for fc in range(NC_F):
                @b.vector
                def _(e, fc=fc, hsl=hsl, ldn=ldn):
                    if fc == 0:
                        e.wait_ge(s_hld[hsl].h, ldn)
                    wV(e, k)
                    hqs = hq[:, fc * 512:(fc + 1) * 512]
                    k.V.inc(e.tensor_scalar(qscr[:], hqs, scol[:, 1:2], 255.0,
                                            ALU.mult, ALU.min))
                    wV(e, k)
                    k.V.inc(e.tensor_scalar(qscr[:], qscr[:], 0.0,
                                            cst["ceff"][:], ALU.max, ALU.add))
                    wV(e, k)
                    k.V.inc(e.tensor_scalar(hqs, qscr[:], cst["ceff"][:], None,
                                            ALU.subtract))
            v_q_all = k.V.n

            for oc in range(NC_D):
                bk = bk_m2[oc % 2]

                @b.tensor
                def _(e, oc=oc, bk=bk, v_q_all=v_q_all):
                    if oc == 0:
                        wV(e, k, v_q_all)
                    if m2_free_v:
                        wV(e, k, m2_free_v)
                    for fc in range(NC_F):
                        k.P.inc(e.matmul(
                            bk[:, 0:512],
                            wq2T[:, fc * D + oc * 128:fc * D + (oc + 1) * 128],
                            hq[:, fc * 512:(fc + 1) * 512],
                            start=(fc == 0), stop=(fc == NC_F - 1)))
                p_m2 = k.P.n
                if oc == NC_D - 1:
                    hq_rd_p[hsl] = p_m2
                    hq_rd_p[1 - hsl] = p_m2

                @b.vector
                def _(e, oc=oc, bk=bk, p_m2=p_m2):
                    wP(e, k, p_m2)
                    if foutT_free_p:
                        wP(e, k, foutT_free_p)
                    k.V.inc(e.tensor_scalar(
                        foutT[:, oc * 512:(oc + 1) * 512], bk[:, 0:512],
                        scol[:, 0:1], bq[:, oc:oc + 1], ALU.mult, ALU.add))
                m2_free_v = k.V.n
            v_foutT = k.V.n

            trc_free_v = [0, 0]
            for j in range(4):
                i = 4 * tci + j

                for c in range(NC_D):
                    @b.tensor
                    def _(e, j=j, c=c, v_foutT=v_foutT, tfv=trc_free_v[c % 2]):
                        if c == 0 and j == 0:
                            wV(e, k, v_foutT)
                        if c == 0 and fout_free_v:
                            wV(e, k, fout_free_v)
                        if tfv:
                            wV(e, k, tfv)
                        k.P.inc(e.transpose(
                            bk_tr[c % 2][:, 0:128],
                            foutT[:, c * 512 + j * 128:c * 512 + (j + 1) * 128],
                            cst["ident"][:]))
                    p_tr = k.P.n

                    @b.vector
                    def _(e, c=c, p_tr=p_tr):
                        wP(e, k, p_tr)
                        wV(e, k)
                        k.V.inc(e.tensor_copy(fout[:, c * 128:(c + 1) * 128],
                                              bk_tr[c % 2][:, 0:128]))
                    trc_free_v[c % 2] = k.V.n
                v_f = k.V.n
                if j == 3:
                    foutT_free_p = k.P.n

                epi.tile(b, cst, i, fout[:, 0:D],
                         lambda e, v_f=v_f: wV(e, k, v_f))
                fout_free_v = k.V.n

        epi.finish(b)

    k.close()
    return nc



def build_L1a():
    k = K("l1a")
    nc = k.nc
    SB = S
    x_in = k.din("x", [T, D])
    wqT = k.din("wqT", [D, D])
    wkT = k.din("wkT", [D, D])
    wvT = k.din("wvT", [D, D])
    woT = k.din("woT", [D, D])
    bqc = k.din("bqc", [128, 8])
    bkc = k.din("bkc", [128, 8])
    bvr = k.din("bvr", [1, D])
    bor = k.din("bor", [1, D])
    x_out = k.dout("x_new", [T, D])

    xlnT = k.sb("xlnT", [128, NC_D * SB], BF16)
    qT = k.sb("qT", [128, NC_D * SB], BF16)
    kTb = k.sb("kTb", [128, NC_D * SB], BF16)
    vp = k.sb("vp", [128, 16 * 16 * 65], BF16)
    wb = k.sb("wb", [128, NC_D * D], BF16)
    attnT = k.sb("attnT", [128, NC_D * 512], BF16)
    ptile = k.sb("ptile", [128, 4 * 512], BF16)
    rlb = k.sb("rlb", [64, 2 * 512])
    lr = k.sb("lr", [1, 2 * 512])
    sqn = k.sb("sqn", [128, 512], BF16)
    bcin = k.sb("bcin", [128, 8])
    bvrow = k.sb("bvrow", [1, D])
    borow = k.sb("borow", [1, D])
    bd2 = k.sb("bd2", [128, 2], BF16)
    nn2 = k.sb("nn2", [2, 3])
    nr = k.sb("nr", [1, 4])
    negm = k.sb("negm", [128, 1])
    fout = k.sb("fout", [128, D])

    bk_tr = [k.bankb("tr0"), k.bankb("tr1")]
    bk_sc = [k.bank("sc0"), k.bank("sc1"), k.bank("sc2")]
    bk_pv = [k.bank("pv0"), k.bank("pv1")]
    bk_bc = k.bank("bc")
    bk_op = [bk_sc[0], bk_sc[1]]

    with nc.Block() as b:
        cst = setup_consts(k, b, need_identf=True)

        s_rows = k.sem("s_rows")
        s_bcin = k.sem("s_bcin")

        @b.sync
        def _(e):
            s_rows.inc(e.dma_start(out=bvrow[:], in_=bvr), 16)
            s_rows.inc(e.dma_start(out=borow[:], in_=bor), 16)
        sy_rows = s_rows.n

        @b.vector
        def _(e):
            wV(e, k, cst["v_done"])
            k.V.inc(e.tensor_scalar(bd2[:, 0:1], cst["iot"][:, 64:65], 0,
                                    None, ALU.is_gt))
            wV(e, k)
            k.V.inc(e.tensor_scalar(bd2[:, 1:2], bd2[:, 0:1], -1.0, 1.0,
                                    ALU.mult, ALU.add))
        v_bd2 = k.V.n

        pro = Prologue(k, x_in, xlnT, bk_tr,
                       col_of=lambda c, i: c * SB + (i % 16) * TT)
        epi = Epilogue(k, x_in, x_out)

        s_wldq = k.sem("wldq")
        xln_rd_p = 0
        attn_rd_p = 0
        fout_free_v = 0
        proj_ev_v = 0
        sc_free = [None, None, None]
        pt_free_p = [0, 0, 0, 0]
        pv_free_v = [0, 0]
        bc_free_a = 0
        bc_free = 0
        rlb_free_v = 0
        lr_free_p = 0
        attnT_free_p = 0

        for bi in range(B_PER):
            tb0 = bi * SB
            for i in range(bi * 16, bi * 16 + 16):
                ew = (lambda e, n=xln_rd_p: wP(e, k, n)) if xln_rd_p else None
                pro.tile(b, cst, i, evac_wait=ew)
            ev_all = pro.ev_a[bi * 16 + 15]

            def load_w(wsrc):
                @b.gpsimd
                def _(e, wsrc=wsrc):
                    wP(e, k)
                    s_wldq.inc(e.dma_start(
                        out=wb[:].rearrange("p (c n) -> p c n", c=NC_D),
                        in_=wsrc.rearrange("(c p) n -> p c n", p=128)), 16)
                return s_wldq.n

            for pi, (dst, wsrc, bsrc) in enumerate(
                    ((qT, wqT, bqc), (kTb, wkT, bkc))):
                ldw = load_w(wsrc)

                @b.sync
                def _(e, bsrc=bsrc):
                    wV(e, k)
                    s_bcin.inc(e.dma_start(out=bcin[:], in_=bsrc), 16)
                sy_bc = s_bcin.n

                for ec in range(NC_D):
                    for tc in range(4):
                        sl = (ec * 4 + tc) % 2
                        bk2 = bk_sc[sl]

                        @b.tensor
                        def _(e, ec=ec, tc=tc, bk2=bk2, ldw=ldw,
                              first=(ec == 0 and tc == 0),
                              arp=attn_rd_p, scf=sc_free[sl]):
                            if first:
                                wV(e, k, ev_all)
                                e.wait_ge(s_wldq.h, ldw)
                                if arp:
                                    wP(e, k, arp)
                            wT(e, k, scf)
                            for c in range(NC_D):
                                k.P.inc(e.matmul(
                                    bk2[:, 0:512],
                                    wb[:, c * D + ec * 128:c * D + (ec + 1) * 128],
                                    xlnT[:, c * SB + tc * 512:
                                         c * SB + (tc + 1) * 512],
                                    start=(c == 0), stop=(c == NC_D - 1)))
                        p_pj = k.P.n

                        @b.vector
                        def _(e, ec=ec, tc=tc, bk2=bk2, p_pj=p_pj, dst=dst,
                              sy_bc=sy_bc):
                            wP(e, k, p_pj)
                            e.wait_ge(s_bcin.h, sy_bc)
                            k.V.inc(e.tensor_scalar(
                                dst[:, ec * SB + tc * 512:
                                    ec * SB + (tc + 1) * 512],
                                bk2[:, 0:512], bcin[:, ec:ec + 1], None,
                                ALU.add))
                        sc_free[sl] = ("V", k.V.n)
                proj_ev_v = k.V.n

            ldw = load_w(wvT)

            @b.vector
            def _(e):
                wP(e, k)
                wV(e, k)
                k.V.inc(e.memset(vp[:], 1.0))
            v_vp0 = k.V.n

            for tau in range(16):
                for half in range(2):
                    sl = (tau * 2 + half) % 2
                    bk2 = bk_sc[sl]

                    @b.tensor
                    def _(e, tau=tau, half=half, bk2=bk2, ldw=ldw,
                          v_vp0=v_vp0, scf=sc_free[sl]):
                        if tau == 0 and half == 0:
                            e.wait_ge(s_wldq.h, ldw)
                            wV(e, k, v_vp0)
                            e.wait_ge(s_rows.h, sy_rows)
                        wT(e, k, scf)
                        for c in range(NC_D):
                            e.matmul(
                                bk2[:, 0:512],
                                xlnT[:, c * SB + tau * 128:
                                     c * SB + (tau + 1) * 128],
                                wb[:, c * D + half * 512:c * D + (half + 1) * 512],
                                start=(c == 0), stop=False)
                        k.P.inc(e.matmul(
                            bk2[0:128, 0:512], cst["ones_rf"][0:1, 0:128],
                            bvrow[:, half * 512:(half + 1) * 512],
                            start=False, stop=True))
                    p_v = k.P.n

                    @b.vector
                    def _(e, tau=tau, half=half, bk2=bk2, p_v=p_v):
                        wP(e, k, p_v)
                        k.V.inc(e.tensor_copy(
                            vp[:, (tau * 16 + half * 8) * 65:
                               (tau * 16 + half * 8 + 8) * 65]
                            .rearrange("p (h j) -> p h j", h=8)[:, :, 0:64],
                            bk2[:, 0:512]
                            .rearrange("p (h j) -> p h j", h=8)))
                    sc_free[sl] = ("V", k.V.n)
            v_vp = k.V.n
            xln_rd_p = k.P.n

            @b.vector
            def _(e):
                wV(e, k, v_bd2)
                k.V.inc(e.memset(nn2[:, 0:2], 0.0))
            for (src_t, col) in ((qT, 0), (kTb, 1)):
                for ch in range(NC_D * 4):
                    @b.scalar
                    def _(e, src_t=src_t, ch=ch):
                        wV(e, k)
                        k.A.inc(e.activation(
                            sqn[:], src_t[:, ch * 512:(ch + 1) * 512],
                            AF.Square))
                    a_sq = k.A.n

                    @b.tensor
                    def _(e, a_sq=a_sq):
                        wA(e, k, a_sq)
                        wV(e, k)
                        k.P.inc(e.matmul(bk_bc[0:2, 0:512], bd2[:, 0:2],
                                         sqn[:], start=True, stop=True))
                    p_nn = k.P.n

                    @b.vector
                    def _(e, col=col, p_nn=p_nn):
                        wP(e, k, p_nn)
                        k.V.inc(e.reduce_max(nn2[0:2, 2:3], bk_bc[0:2, 0:512],
                                             axis=AX.X))
                        wV(e, k)
                        k.V.inc(e.tensor_max(nn2[0:2, col:col + 1],
                                             nn2[0:2, col:col + 1],
                                             nn2[0:2, 2:3]))
            v_nn = k.V.n

            @b.tensor
            def _(e):
                wV(e, k, v_nn)
                k.P.inc(e.transpose(bk_bc[0:2, 0:2], nn2[0:2, 0:2],
                                    cst["identf"][0:2, 0:2]))
            p_tr2 = k.P.n

            @b.vector
            def _(e, p_tr2=p_tr2):
                wP(e, k, p_tr2)
                k.V.inc(e.reduce_max(nr[:, 0:1], bk_bc[0:1, 0:2], axis=AX.X))
            @b.tensor
            def _(e):
                wV(e, k)
                k.P.inc(e.transpose(bk_bc[0:2, 4:6], nn2[0:2, 1:3],
                                    cst["identf"][0:2, 0:2]))
            p_tr3 = k.P.n

            @b.vector
            def _(e, p_tr3=p_tr3):
                wP(e, k, p_tr3)
                k.V.inc(e.reduce_max(nr[:, 1:2], bk_bc[0:1, 4:6], axis=AX.X))
                wV(e, k)
                k.V.inc(e.tensor_mul(nr[:, 2:3], nr[:, 0:1], nr[:, 1:2]))

            @b.scalar
            def _(e):
                wV(e, k)
                k.A.inc(e.activation(nr[:, 3:4], nr[:, 2:3], AF.Sqrt))

            @b.vector
            def _(e):
                wA(e, k)
                k.V.inc(e.tensor_scalar(nr[:, 3:4], nr[:, 3:4],
                                        -0.125 * 1.0005, None, ALU.mult))
            v_negm = emit_bcast_col(k, b, cst, nr[:, 3:4], negm[:], bk_bc,
                                    k.V.n, n=1)

            ldwo = load_w(woT)

            for sg in range(4):
                for h in range(16):
                    hc, hp = h // 2, (h % 2) * 64
                    hs = h % 2
                    pvb = bk_pv[hs]
                    ntau = 4 * sg + 4
                    p0, a0, g0 = k.P.n, k.A.n, k.G.n
                    recs = []
                    scf_l = list(sc_free)
                    ptw_l = list(pt_free_p)
                    gi = 0
                    for tau in range(ntau):
                        diag = tau >= 4 * sg
                        sl = tau % 3
                        psl = 3 if diag else sl
                        r = dict(tau=tau, diag=diag, sl=sl, psl=psl,
                                 scf=scf_l[sl], ptw=ptw_l[psl],
                                 p_qk=p0 + tau + 1, a_exp=a0 + tau + 1,
                                 p_pv=p0 + ntau + tau + 1, g_msk=0)
                        if diag:
                            gi += 1
                            r["g_msk"] = g0 + gi
                        recs.append(r)
                        scf_l[sl] = ("A", r["a_exp"])
                        ptw_l[psl] = r["p_pv"]
                    first_attn = (h == 0 and sg == 0)
                    pvf = pv_free_v[hs]
                    order = []
                    for j in range(ntau + 2):
                        if j < ntau:
                            order.append(("qk", j))
                        if j >= 2:
                            order.append(("pv", j - 2))
                    for idx, (kind, j) in enumerate(order):
                        recs[j]["p_" + kind] = p0 + idx + 1
                    for r in recs:
                        r["p_pv"] = r["p_pv"]
                    ptw_l2 = list(pt_free_p)
                    for r in recs:
                        r["ptw"] = ptw_l2[r["psl"]]
                        ptw_l2[r["psl"]] = r["p_pv"]
                    ptw_fin = ptw_l2

                    @b.tensor
                    def _(e, recs=recs, order=order, hc=hc, hp=hp, sg=sg,
                          h=h, pvb=pvb, pvf=pvf, first=first_attn):
                        if first:
                            wV(e, k, v_negm)
                        for kind, j in order:
                            r = recs[j]
                            tau = r["tau"]
                            if kind == "qk":
                                wT(e, k, r["scf"])
                                e.matmul(
                                    bk_sc[r["sl"]][0:128, 0:512],
                                    kTb[hp:hp + 64,
                                        hc * SB + tau * 128:
                                        hc * SB + (tau + 1) * 128],
                                    qT[hp:hp + 64,
                                       hc * SB + sg * 512:
                                       hc * SB + (sg + 1) * 512],
                                    start=True, stop=True).then_inc(k.P.h, 1)
                            else:
                                if r["diag"]:
                                    wG(e, k, r["g_msk"])
                                else:
                                    wA(e, k, r["a_exp"])
                                if tau == 0 and pvf:
                                    wV(e, k, pvf)
                                e.matmul(
                                    pvb[0:65, 0:512],
                                    vp[:, (tau * 16 + h) * 65:
                                       (tau * 16 + h + 1) * 65],
                                    ptile[:, r["psl"] * 512:
                                          (r["psl"] + 1) * 512],
                                    start=(tau == 0),
                                    stop=(tau == len(recs) - 1)) \
                                    .then_inc(k.P.h, 1)

                    @b.scalar
                    def _(e, recs=recs):
                        for r in recs:
                            e.wait_ge(k.P.h, r["p_qk"])
                            if r["ptw"]:
                                wP(e, k, r["ptw"])
                            e.activation(
                                ptile[:, r["psl"] * 512:(r["psl"] + 1) * 512],
                                bk_sc[r["sl"]][0:128, 0:512], AF.Exp,
                                bias=negm[:],
                                scale=0.125).then_inc(k.A.h, 1)

                    @b.gpsimd
                    def _(e, recs=recs, sg=sg):
                        for r in recs:
                            if r["diag"]:
                                wA(e, k, r["a_exp"])
                                e.affine_select(
                                    ptile[:, 3 * 512:4 * 512],
                                    ptile[:, 3 * 512:4 * 512],
                                    [[1, 512]], ALU.is_ge, 0.0,
                                    base=sg * 512 - r["tau"] * 128,
                                    channel_multiplier=-1).then_inc(k.G.h, 1)

                    k.P.n = p0 + 2 * ntau
                    k.A.n = a0 + ntau
                    k.G.n = g0 + gi
                    sc_free[:] = scf_l
                    pt_free_p[:] = ptw_fin
                    p_pv = k.P.n

                    lrs = lr[:, hs * 512:(hs + 1) * 512]
                    rlbs = rlb[:, hs * 512:(hs + 1) * 512]

                    @b.vector
                    def _(e, p_pv=p_pv, lfp=lr_free_p, pvb=pvb, lrs=lrs):
                        wP(e, k, p_pv)
                        if lfp:
                            wP(e, k, lfp)
                        k.V.inc(e.tensor_copy(lrs, pvb[64:65, 0:512]))
                    v_lrc = k.V.n

                    @b.scalar
                    def _(e, v_lrc=v_lrc, lrs=lrs):
                        wV(e, k, v_lrc)
                        k.A.inc(e.activation(lrs, lrs, AF.Ln))
                        wA(e, k)
                        k.A.inc(e.activation(lrs, lrs, AF.Exp, scale=-1.0))
                    a_rl = k.A.n

                    @b.tensor
                    def _(e, a_rl=a_rl, lrs=lrs, bca=bc_free_a):
                        wA(e, k, a_rl)
                        if bca:
                            wV(e, k, bca)
                        k.P.inc(e.matmul(bk_bc[0:64, 0:512],
                                         cst["ones_rf"][0:1, 0:64], lrs,
                                         start=True, stop=True))
                    lr_free_p = k.P.n
                    p_bc = k.P.n

                    @b.vector
                    def _(e, p_bc=p_bc, rfv=rlb_free_v, rlbs=rlbs):
                        wP(e, k, p_bc)
                        if rfv:
                            wV(e, k, rfv)
                        k.V.inc(e.tensor_copy(rlbs, bk_bc[0:64, 0:512]))
                    v_rlb = k.V.n
                    bc_free_a = v_rlb

                    @b.vector
                    def _(e, h=h, hc=hc, hp=hp, pvb=pvb,
                          rlbs=rlbs, atf=attnT_free_p):
                        wV(e, k)
                        if atf and h == 0:
                            wP(e, k, atf)
                        k.V.inc(e.tensor_mul(
                            attnT[hp:hp + 64, hc * 512:(hc + 1) * 512],
                            pvb[0:64, 0:512], rlbs))
                    rlb_free_v = k.V.n
                    pv_free_v[hs] = k.V.n
                v_at = k.V.n

                for j in range(4):
                    @b.tensor
                    def _(e, j=j, sg=sg, v_at=v_at, ofv=fout_free_v,
                          s0=sc_free[0], s1=sc_free[1], ldwo=ldwo):
                        if j == 0:
                            wV(e, k, v_at)
                            wT(e, k, s0)
                            wT(e, k, s1)
                            if sg == 0:
                                e.wait_ge(s_wldq.h, ldwo)
                        if ofv:
                            wV(e, k, ofv)
                        for half in range(2):
                            bko = bk_op[half]
                            for c in range(NC_D):
                                e.matmul(
                                    bko[:, 0:512],
                                    attnT[:, c * 512 + j * 128:
                                          c * 512 + (j + 1) * 128],
                                    wb[:, c * D + half * 512:
                                       c * D + (half + 1) * 512],
                                    start=(c == 0), stop=False)
                            k.P.inc(e.matmul(
                                bko[0:128, 0:512], cst["ones_rf"][0:1, 0:128],
                                borow[:, half * 512:(half + 1) * 512],
                                start=False, stop=True))
                    p_op = k.P.n

                    @b.vector
                    def _(e, p_op=p_op, ffv=fout_free_v):
                        wP(e, k, p_op)
                        if ffv:
                            wV(e, k, ffv)
                        k.V.inc(e.tensor_copy(fout[:, 0:512],
                                              bk_op[0][:, 0:512]))
                        k.V.inc(e.tensor_copy(fout[:, 512:1024],
                                              bk_op[1][:, 0:512]))
                    v_f = k.V.n
                    sc_free[0] = ("V", v_f)
                    sc_free[1] = ("V", v_f)

                    i_tile = bi * 16 + sg * 4 + j
                    epi.tile(b, cst, i_tile, fout[:, 0:D],
                             lambda e, v_f=v_f: wV(e, k, v_f))
                    fout_free_v = k.V.n
                attnT_free_p = k.P.n
            attn_rd_p = k.P.n

        epi.finish(b)

    k.close()
    return nc


def run_L1a(x_full, Wq, bq, Wk, bk_, Wv, bv, Wo, bo):
    nc = build_L1a()
    shards = np_shard_x(x_full)
    cc = np.ascontiguousarray
    com = {
        "wqT": cc(np.asarray(Wq).T), "wkT": cc(np.asarray(Wk).T),
        "wvT": cc(np.asarray(Wv).T), "woT": cc(np.asarray(Wo).T),
        "bqc": cc(np.asarray(bq).reshape(8, 128).T),
        "bkc": cc(np.asarray(bk_).reshape(8, 128).T),
        "bvr": cc(np.asarray(bv).reshape(1, D)),
        "bor": cc(np.asarray(bo).reshape(1, D)),
    }
    in_maps = [dict(com, x=s) for s in shards]
    res = run_bass_kernel_spmd(nc, in_maps, core_ids=list(range(NCORES)))
    return np.concatenate([r["x_new"].reshape(B_PER, S, D)
                           for r in res.results], axis=0)



def build_L0a():
    k = K("l0a")
    nc = k.nc
    SB = S
    x_in = k.din("x", [T, D])
    wq = k.din("wq", [D, D])
    wk = k.din("wk", [D, D])
    wv = k.din("wv", [D, D])
    wo = k.din("wo", [D, D])
    bqc = k.din("bqc", [128, 8])
    bkc = k.din("bkc", [128, 8])
    bvc = k.din("bvc", [128, 8])
    bor = k.din("bor", [1, D])
    x_out = k.dout("x_new", [T, D])

    xlnT = k.sb("xlnT", [128, NC_D * SB], BF16)
    qT = k.sb("qT", [128, NC_D * SB], BF16)
    kT = k.sb("kT", [128, NC_D * SB], BF16)
    vT = k.sb("vT", [128, NC_D * SB], BF16)
    wb = k.sb("wb", [128, NC_D * D], BF16)
    attnT = k.sb("attnT", [128, NC_D * 512], BF16)
    scr1 = k.sb("scr1", [128, 512])
    scr2 = k.sb("scr2", [128, 512])
    scr3 = k.sb("scr3", [128, 512])
    kvc = k.sb("kvc", [128, 8])
    ksc = k.sb("ksc", [128, 8])
    rc = k.sb("rc", [128, 1])
    bcin = k.sb("bcin", [128, 24])
    borow = k.sb("borow", [1, D])
    fout = k.sb("fout", [128, D])

    bk_tr = [k.bankb("tr0"), k.bankb("tr1")]
    bk_pj = [k.bank("pj0"), k.bank("pj1")]
    bk_op = [k.bank("op0"), k.bank("op1")]

    with nc.Block() as b:
        cst = setup_consts(k, b)

        s_rows = k.sem("s_rows")
        s_wldq = k.sem("wldq")

        @b.sync
        def _(e):
            s_rows.inc(e.dma_start(out=borow[:], in_=bor), 16)
            s_rows.inc(e.dma_start(out=bcin[:, 0:8], in_=bqc), 16)
            s_rows.inc(e.dma_start(out=bcin[:, 8:16], in_=bkc), 16)
            s_rows.inc(e.dma_start(out=bcin[:, 16:24], in_=bvc), 16)
        sy_rows = s_rows.n

        pro = Prologue(k, x_in, xlnT, bk_tr,
                       col_of=lambda c, i: c * SB + (i % 16) * TT)
        epi = Epilogue(k, x_in, x_out)

        xln_rd_p = 0
        fout_free_v = 0
        pj_free = [None, None]
        scr_busy_p = 0

        for bi in range(B_PER):
            for i in range(bi * 16, bi * 16 + 16):
                ew = (lambda e, n=xln_rd_p: wP(e, k, n)) if xln_rd_p else None
                pro.tile(b, cst, i, evac_wait=ew)
            ev_all = pro.ev_a[bi * 16 + 15]

            def load_w(wsrc):
                @b.gpsimd
                def _(e, wsrc=wsrc):
                    wP(e, k)
                    s_wldq.inc(e.dma_start(
                        out=wb[:].rearrange("p (c n) -> p c n", c=NC_D),
                        in_=wsrc.rearrange("(c p) n -> p c n", p=128)), 16)
                return s_wldq.n

            for pi, (dst, wsrc, bco, fmap) in enumerate(
                    ((qT, wq, 0, True), (kT, wk, 8, True), (vT, wv, 16, False))):
                ldw = load_w(wsrc)
                for ec in range(NC_D):
                    for tc in range(4):
                        sl = (ec * 4 + tc) % 2
                        bk2 = bk_pj[sl]

                        @b.tensor
                        def _(e, ec=ec, tc=tc, bk2=bk2, ldw=ldw,
                              first=(ec == 0 and tc == 0), arp=0,
                              pjf=pj_free[sl]):
                            if first:
                                wV(e, k, ev_all)
                                e.wait_ge(s_wldq.h, ldw)
                                e.wait_ge(s_rows.h, sy_rows)
                            wT(e, k, pjf)
                            for c in range(NC_D):
                                k.P.inc(e.matmul(
                                    bk2[:, 0:512],
                                    wb[:, c * D + ec * 128:c * D + (ec + 1) * 128],
                                    xlnT[:, c * SB + tc * 512:
                                         c * SB + (tc + 1) * 512],
                                    start=(c == 0), stop=(c == NC_D - 1)))
                        p_pj = k.P.n

                        if fmap:
                            @b.vector
                            def _(e, ec=ec, bk2=bk2, p_pj=p_pj, bco=bco):
                                wP(e, k, p_pj)
                                wV(e, k)
                                k.V.inc(e.tensor_scalar(
                                    scr1[:], bk2[:, 0:512],
                                    bcin[:, bco + ec:bco + ec + 1], None,
                                    ALU.add))
                                wV(e, k)
                                k.V.inc(e.tensor_scalar(scr2[:], scr1[:], 0.0,
                                                        None, ALU.min))

                            @b.scalar
                            def _(e):
                                wV(e, k)
                                k.A.inc(e.activation(scr2[:], scr2[:], AF.Exp))

                            @b.vector
                            def _(e, dst=dst, ec=ec, tc=tc):
                                wA(e, k)
                                k.V.inc(e.tensor_scalar(scr1[:], scr1[:], 0.0,
                                                        None, ALU.max))
                                wV(e, k)
                                k.V.inc(e.tensor_add(
                                    dst[:, ec * SB + tc * 512:
                                        ec * SB + (tc + 1) * 512],
                                    scr1[:], scr2[:]))
                        else:
                            @b.vector
                            def _(e, dst=dst, ec=ec, tc=tc, bk2=bk2,
                                  p_pj=p_pj, bco=bco):
                                wP(e, k, p_pj)
                                wV(e, k)
                                k.V.inc(e.tensor_scalar(
                                    dst[:, ec * SB + tc * 512:
                                        ec * SB + (tc + 1) * 512],
                                    bk2[:, 0:512],
                                    bcin[:, bco + ec:bco + ec + 1], None,
                                    ALU.add))
                        pj_free[sl] = ("V", k.V.n)
            xln_rd_p = k.P.n

            @b.vector
            def _(e):
                wV(e, k)
                k.V.inc(e.memset(kvc[:], 0.0))
                k.V.inc(e.memset(ksc[:], 0.0))
            for c in range(NC_D):
                for tc in range(4):
                    @b.vector
                    def _(e, c=c, tc=tc):
                        wV(e, k)
                        ks_ = kT[:, c * SB + tc * 512:c * SB + (tc + 1) * 512]
                        vs_ = vT[:, c * SB + tc * 512:c * SB + (tc + 1) * 512]
                        k.V.inc(e.tensor_mul(scr1[:], ks_, vs_))
                        wV(e, k)
                        k.V.inc(e.reduce_sum(rc[:], scr1[:], axis=AX.X))
                        wV(e, k)
                        k.V.inc(e.tensor_add(kvc[:, c:c + 1], kvc[:, c:c + 1],
                                             rc[:]))
                        wV(e, k)
                        k.V.inc(e.reduce_sum(rc[:], ks_, axis=AX.X))
                        wV(e, k)
                        k.V.inc(e.tensor_add(ksc[:, c:c + 1], ksc[:, c:c + 1],
                                             rc[:]))
            v_kv = k.V.n

            ldwo = load_w(wo)
            for tc in range(4):
                for c in range(NC_D):
                    @b.vector
                    def _(e, c=c, tc=tc, sbp=scr_busy_p):
                        wV(e, k, v_kv)
                        wV(e, k)
                        if sbp and c == 0:
                            wP(e, k, sbp)
                        qs = qT[:, c * SB + tc * 512:c * SB + (tc + 1) * 512]
                        k.V.inc(e.tensor_scalar(scr1[:], qs, ksc[:, c:c + 1],
                                                1e-6, ALU.mult, ALU.add))

                    @b.scalar
                    def _(e):
                        wV(e, k)
                        k.A.inc(e.activation(scr1[:], scr1[:], AF.Ln))
                        wA(e, k)
                        k.A.inc(e.activation(scr1[:], scr1[:], AF.Exp,
                                             scale=-1.0))

                    @b.vector
                    def _(e, c=c, tc=tc):
                        wA(e, k)
                        qs = qT[:, c * SB + tc * 512:c * SB + (tc + 1) * 512]
                        k.V.inc(e.tensor_scalar(scr2[:], qs, kvc[:, c:c + 1],
                                                None, ALU.mult))
                        wV(e, k)
                        k.V.inc(e.tensor_mul(attnT[:, c * 512:(c + 1) * 512],
                                             scr2[:], scr1[:]))
                v_at = k.V.n

                for j in range(4):
                    @b.tensor
                    def _(e, j=j, tc=tc, v_at=v_at, ofv=fout_free_v,
                          ldwo=ldwo):
                        if j == 0:
                            wV(e, k, v_at)
                            if tc == 0:
                                e.wait_ge(s_wldq.h, ldwo)
                        if ofv:
                            wV(e, k, ofv)
                        for half in range(2):
                            bko = bk_op[half]
                            for c in range(NC_D):
                                e.matmul(
                                    bko[:, 0:512],
                                    attnT[:, c * 512 + j * 128:
                                          c * 512 + (j + 1) * 128],
                                    wb[:, c * D + half * 512:
                                       c * D + (half + 1) * 512],
                                    start=(c == 0), stop=False)
                            k.P.inc(e.matmul(
                                bko[0:128, 0:512], cst["ones_rf"][0:1, 0:128],
                                borow[:, half * 512:(half + 1) * 512],
                                start=False, stop=True))
                    p_op = k.P.n
                    if j == 3:
                        scr_busy_p = p_op

                    for half in range(2):
                        bko = bk_op[half]

                        @b.scalar
                        def _(e, bko=bko, p_op=p_op):
                            wP(e, k, p_op)
                            wV(e, k)
                            k.A.inc(e.activation(scr1[:], bko[:, 0:512],
                                                 AF.Square))
                        a_sq = k.A.n

                        @b.vector
                        def _(e, bko=bko, a_sq=a_sq):
                            wA(e, k, a_sq)
                            k.V.inc(e.tensor_scalar(scr1[:], scr1[:],
                                                    0.044715, 1.0, ALU.mult,
                                                    ALU.add))
                            wV(e, k)
                            k.V.inc(e.tensor_mul(scr2[:], scr1[:],
                                                 bko[:, 0:512]))

                        @b.scalar
                        def _(e):
                            wV(e, k)
                            k.A.inc(e.activation(scr2[:], scr2[:], AF.Tanh,
                                                 scale=0.7978845608028654))

                        @b.vector
                        def _(e, bko=bko, half=half):
                            wA(e, k)
                            k.V.inc(e.tensor_scalar(scr2[:], scr2[:], 1.0,
                                                    0.5, ALU.add, ALU.mult))
                            wV(e, k)
                            k.V.inc(e.tensor_mul(
                                fout[:, half * 512:(half + 1) * 512],
                                scr2[:], bko[:, 0:512]))
                    v_f = k.V.n

                    i_tile = bi * 16 + tc * 4 + j
                    epi.tile(b, cst, i_tile, fout[:, 0:D],
                             lambda e, v_f=v_f: wV(e, k, v_f))
                    fout_free_v = k.V.n

        epi.finish(b)

    k.close()
    return nc


def run_L0a(x_full, wq4, bq4, wk4, bk4, wv4, bv4, wo4, bo4):
    nc = build_L0a()
    shards = np_shard_x(x_full)
    cc = np.ascontiguousarray
    com = {
        "wq": cc(np.asarray(wq4).reshape(D, D)),
        "wk": cc(np.asarray(wk4).reshape(D, D)),
        "wv": cc(np.asarray(wv4).reshape(D, D)),
        "wo": cc(np.asarray(wo4).reshape(D, D)),
        "bqc": cc(np.asarray(bq4).reshape(8, 128).T),
        "bkc": cc(np.asarray(bk4).reshape(8, 128).T),
        "bvc": cc(np.asarray(bv4).reshape(8, 128).T),
        "bor": cc(np.asarray(bo4).reshape(1, D)),
    }
    in_maps = [dict(com, x=s) for s in shards]
    res = run_bass_kernel_spmd(nc, in_maps, core_ids=list(range(NCORES)))
    return np.concatenate([r["x_new"].reshape(B_PER, S, D)
                           for r in res.results], axis=0)



def build_L2a():
    k = K("l2a")
    nc = k.nc
    SB = S
    KV = 512
    x_in = k.din("x", [T, D])
    wq2 = k.din("wq2", [D, D])
    wkr = k.din("wkr", [D, D])
    wvr = k.din("wvr", [D, D])
    memi = k.din("memi", [KV, KV])
    memn = k.din("memn", [128, 4])
    memw = k.din("memw", [1, 1])
    x_out = k.dout("x_new", [T, D])

    xlnT = k.sb("xlnT", [128, NC_D * SB], BF16)
    qT = k.sb("qT", [128, NC_D * SB], BF16)
    kTr = k.sb("kTr", [128, NC_D * SB], BF16)
    vp = k.sb("vp", [128, 16 * 16 * 65], BF16)
    wb = k.sb("wb", [128, NC_D * D], BF16)
    attnT = k.sb("attnT", [128, NC_D * 512], BF16)
    ptile = k.sb("ptile", [128, 3 * 512], BF16)
    rlb = k.sb("rlb", [64, 2 * 512])
    rdb = k.sb("rdb", [128, 512])
    lr = k.sb("lr", [1, 2 * 512])
    dr = k.sb("dr", [1, 512])
    sigc = k.sb("sigc", [128, 4 * 512], BF16)
    scrA = k.sb("scrA", [128, 512])
    scrB = k.sb("scrB", [128, 512])
    memb = k.sb("memb", [128, 4 * KV], BF16)
    memnb = k.sb("memnb", [128, 4], BF16)
    sqn = k.sb("sqn", [128, 512], BF16)
    bd2 = k.sb("bd2", [128, 2], BF16)
    nn2 = k.sb("nn2", [2, 3])
    nr = k.sb("nr", [1, 4])
    negm = k.sb("negm", [128, 1])
    wrow = k.sb("wrow", [1, 2])
    fout = k.sb("fout", [128, D])

    bk_tr = [k.bankb("tr0"), k.bankb("tr1")]
    bk_sc = [k.bank("sc0"), k.bank("sc1"), k.bank("sc2")]
    bk_pv = [k.bank("pv0"), k.bank("pv1")]
    bk_bc = k.bank("bc")
    bk_nm = [bk_sc[0], bk_sc[1]]

    with nc.Block() as b:
        cst = setup_consts(k, b, need_identf=True)
        s_rows = k.sem("s_rows")
        s_wldq = k.sem("wldq")

        @b.gpsimd
        def _(e):
            k.GQ.inc(e.dma_start(
                out=memb[:].rearrange("p (c n) -> p c n", c=4),
                in_=memi.rearrange("(c p) n -> p c n", p=128)), 16)
            k.GQ.inc(e.dma_start(out=memnb[:], in_=memn), 16)
        gq_mem = k.GQ.n

        @b.sync
        def _(e):
            s_rows.inc(e.dma_start(out=wrow[:, 0:1], in_=memw), 16)
        sy_w = s_rows.n

        @b.scalar
        def _(e):
            e.wait_ge(s_rows.h, sy_w)
            k.A.inc(e.activation(wrow[:, 0:1], wrow[:, 0:1], AF.Sigmoid))

        @b.vector
        def _(e):
            wA(e, k)
            wV(e, k, cst["v_done"])
            k.V.inc(e.tensor_scalar(wrow[:, 1:2], wrow[:, 0:1], -1.0, 1.0,
                                    ALU.mult, ALU.add))
            k.V.inc(e.tensor_scalar(bd2[:, 0:1], cst["iot"][:, 64:65], 0,
                                    None, ALU.is_gt))
            wV(e, k)
            k.V.inc(e.tensor_scalar(bd2[:, 1:2], bd2[:, 0:1], -1.0, 1.0,
                                    ALU.mult, ALU.add))
        v_w = k.V.n

        pro = Prologue(k, x_in, xlnT, bk_tr,
                       col_of=lambda c, i: c * SB + (i % 16) * TT)
        epi = Epilogue(k, x_in, x_out)

        xln_rd_p = 0
        fout_free_v = 0
        sc_free = [None, None, None]
        pt_free_p = [0, 0, 0]
        pv_free_v = [0, 0]
        lr_free_w = [0, 0]
        bc_free_a = 0
        attnT_free_p = 0
        sig_rd_p = 0

        for bi in range(B_PER):
            for i in range(bi * 16, bi * 16 + 16):
                ew = (lambda e, n=xln_rd_p: wP(e, k, n)) if xln_rd_p else None
                pro.tile(b, cst, i, evac_wait=ew)
            ev_all = pro.ev_a[bi * 16 + 15]

            def load_w(wsrc):
                @b.gpsimd
                def _(e, wsrc=wsrc):
                    wP(e, k)
                    s_wldq.inc(e.dma_start(
                        out=wb[:].rearrange("p (c n) -> p c n", c=NC_D),
                        in_=wsrc.rearrange("(c p) n -> p c n", p=128)), 16)
                return s_wldq.n

            for pi, (dst, wsrc) in enumerate(((qT, wq2), (kTr, wkr))):
                ldw = load_w(wsrc)
                for ec in range(NC_D):
                    for tc in range(4):
                        sl = (ec * 4 + tc) % 2
                        bk2 = bk_sc[sl]

                        @b.tensor
                        def _(e, ec=ec, tc=tc, bk2=bk2, ldw=ldw,
                              first=(ec == 0 and tc == 0),
                              scf=sc_free[sl]):
                            if first:
                                wV(e, k, ev_all)
                                e.wait_ge(s_wldq.h, ldw)
                            wT(e, k, scf)
                            for c in range(NC_D):
                                k.P.inc(e.matmul(
                                    bk2[:, 0:512],
                                    wb[:, c * D + ec * 128:c * D + (ec + 1) * 128],
                                    xlnT[:, c * SB + tc * 512:
                                         c * SB + (tc + 1) * 512],
                                    start=(c == 0), stop=(c == NC_D - 1)))
                        p_pj = k.P.n

                        @b.vector
                        def _(e, ec=ec, tc=tc, bk2=bk2, p_pj=p_pj, dst=dst):
                            wP(e, k, p_pj)
                            k.V.inc(e.tensor_copy(
                                dst[:, ec * SB + tc * 512:
                                    ec * SB + (tc + 1) * 512],
                                bk2[:, 0:512]))
                        sc_free[sl] = ("V", k.V.n)

            ldw = load_w(wvr)

            @b.vector
            def _(e):
                wP(e, k)
                wV(e, k)
                k.V.inc(e.memset(vp[:], 1.0))
            v_vp0 = k.V.n

            for tau in range(16):
                for half in range(2):
                    sl = (tau * 2 + half) % 2
                    bk2 = bk_sc[sl]

                    @b.tensor
                    def _(e, tau=tau, half=half, bk2=bk2, ldw=ldw,
                          v_vp0=v_vp0, scf=sc_free[sl]):
                        if tau == 0 and half == 0:
                            e.wait_ge(s_wldq.h, ldw)
                            wV(e, k, v_vp0)
                        wT(e, k, scf)
                        for c in range(NC_D):
                            k.P.inc(e.matmul(
                                bk2[:, 0:512],
                                xlnT[:, c * SB + tau * 128:
                                     c * SB + (tau + 1) * 128],
                                wb[:, c * D + half * 512:c * D + (half + 1) * 512],
                                start=(c == 0), stop=(c == NC_D - 1)))
                    p_v = k.P.n

                    @b.vector
                    def _(e, tau=tau, half=half, bk2=bk2, p_v=p_v):
                        wP(e, k, p_v)
                        k.V.inc(e.tensor_copy(
                            vp[:, (tau * 16 + half * 8) * 65:
                               (tau * 16 + half * 8 + 8) * 65]
                            .rearrange("p (h j) -> p h j", h=8)[:, :, 0:64],
                            bk2[:, 0:512]
                            .rearrange("p (h j) -> p h j", h=8)))
                    sc_free[sl] = ("V", k.V.n)
            xln_rd_p = k.P.n

            @b.vector
            def _(e):
                wV(e, k, v_w)
                wV(e, k)
                k.V.inc(e.memset(nn2[:, 0:2], 0.0))
            for (src_t, col) in ((qT, 0), (kTr, 1)):
                for ch in range(NC_D * 4):
                    @b.scalar
                    def _(e, src_t=src_t, ch=ch):
                        wV(e, k)
                        k.A.inc(e.activation(
                            sqn[:], src_t[:, ch * 512:(ch + 1) * 512],
                            AF.Square))
                    a_sq = k.A.n

                    @b.tensor
                    def _(e, a_sq=a_sq):
                        wA(e, k, a_sq)
                        wV(e, k)
                        k.P.inc(e.matmul(bk_bc[0:2, 0:512], bd2[:, 0:2],
                                         sqn[:], start=True, stop=True))
                    p_nn = k.P.n

                    @b.vector
                    def _(e, col=col, p_nn=p_nn):
                        wP(e, k, p_nn)
                        k.V.inc(e.reduce_max(nn2[0:2, 2:3], bk_bc[0:2, 0:512],
                                             axis=AX.X))
                        wV(e, k)
                        k.V.inc(e.tensor_max(nn2[0:2, col:col + 1],
                                             nn2[0:2, col:col + 1],
                                             nn2[0:2, 2:3]))
            v_nn = k.V.n

            @b.tensor
            def _(e):
                wV(e, k, v_nn)
                k.P.inc(e.transpose(bk_bc[0:2, 0:2], nn2[0:2, 0:2],
                                    cst["identf"][0:2, 0:2]))
            p_tr2 = k.P.n

            @b.vector
            def _(e, p_tr2=p_tr2):
                wP(e, k, p_tr2)
                k.V.inc(e.reduce_max(nr[:, 0:1], bk_bc[0:1, 0:2], axis=AX.X))

            @b.tensor
            def _(e):
                wV(e, k)
                k.P.inc(e.transpose(bk_bc[0:2, 4:6], nn2[0:2, 1:3],
                                    cst["identf"][0:2, 0:2]))
            p_tr3 = k.P.n

            @b.vector
            def _(e, p_tr3=p_tr3):
                wP(e, k, p_tr3)
                k.V.inc(e.reduce_max(nr[:, 1:2], bk_bc[0:1, 4:6], axis=AX.X))
                wV(e, k)
                k.V.inc(e.tensor_mul(nr[:, 2:3], nr[:, 0:1], nr[:, 1:2]))

            @b.scalar
            def _(e):
                wV(e, k)
                k.A.inc(e.activation(nr[:, 3:4], nr[:, 2:3], AF.Sqrt))

            @b.vector
            def _(e):
                wA(e, k)
                k.V.inc(e.tensor_scalar(nr[:, 3:4], nr[:, 3:4],
                                        -0.125 * 1.0005, None, ALU.mult))
            v_negm = emit_bcast_col(k, b, cst, nr[:, 3:4], negm[:], bk_bc,
                                    k.V.n, n=1)

            for sg in range(4):
                H = 16
                p0, a0, v0 = k.P.n, k.A.n, k.V.n
                scf_l = list(sc_free)
                ptw_l = list(pt_free_p)
                pvf_in = list(pv_free_v)
                bca_in = bc_free_a
                lrw_in = list(lr_free_w)
                Pops = []
                for h in range(H):
                    for j in range(18):
                        if j == 5 and h >= 1:
                            Pops.append(("bc", h - 1, 0))
                        if j < 16:
                            Pops.append(("qk", h, j))
                        if j >= 2:
                            Pops.append(("pv", h, j - 2))
                Pops.append(("bc", H - 1, 0))
                pQK = {}; pPV = {}; pBC = {}
                pc = p0
                for kind, h, j in Pops:
                    pc += 1
                    if kind == "qk":
                        pQK[(h, j)] = pc
                    elif kind == "pv":
                        pPV[(h, j)] = pc
                    else:
                        pBC[h] = pc
                aEx = {}; aLn = {}; aRL = {}
                ac = a0
                for h in range(H):
                    for j in range(16):
                        ac += 1
                        aEx[(h, j)] = ac
                    ac += 1
                    aLn[h] = ac
                    ac += 1
                    aRL[h] = ac
                vLC = {}; vRS = {}; vRC = {}; vAM = {}
                vc = v0
                for h in range(H):
                    vc += 1; vLC[h] = vc
                    vc += 1; vRS[h] = vc
                    vc += 1; vRC[h] = vc
                    vc += 1; vAM[h] = vc
                qk_w = {}; ex_w = {}
                for h in range(H):
                    for j in range(16):
                        sl = j % 3
                        qk_w[(h, j)] = scf_l[sl]
                        scf_l[sl] = ("A", aEx[(h, j)])
                        ex_w[(h, j)] = ptw_l[sl]
                        ptw_l[sl] = pPV[(h, j)]

                @b.tensor
                def _(e, sg=sg, Pops=Pops, qk_w=qk_w, aEx=aEx, vAM=vAM,
                      vRS=vRS, vRC=vRC, pvf_in=pvf_in, bca_in=bca_in,
                      first=(sg == 0)):
                    for kind, h, j in Pops:
                        hc, hp, hs = h // 2, (h % 2) * 64, h % 2
                        if kind == "qk":
                            if first and h == 0 and j == 0:
                                wV(e, k, v_negm)
                                e.wait_ge(k.GQ.h, gq_mem)
                            wT(e, k, qk_w[(h, j)])
                            e.matmul(
                                bk_sc[j % 3][0:128, 0:512],
                                kTr[hp:hp + 64,
                                    hc * SB + j * 128:hc * SB + (j + 1) * 128],
                                qT[hp:hp + 64,
                                   hc * SB + sg * 512:
                                   hc * SB + (sg + 1) * 512],
                                start=True, stop=True).then_inc(k.P.h, 1)
                        elif kind == "pv":
                            wA(e, k, aEx[(h, j)])
                            if j == 0:
                                pw = vAM[h - 2] if h >= 2 else pvf_in[h % 2]
                                if pw:
                                    wV(e, k, pw)
                            e.matmul(
                                bk_pv[hs][0:65, 0:512],
                                vp[:, (j * 16 + h) * 65:(j * 16 + h + 1) * 65],
                                ptile[:, (j % 3) * 512:(j % 3 + 1) * 512],
                                start=(j == 0), stop=(j == 15)) \
                                .then_inc(k.P.h, 1)
                        else:
                            wV(e, k, vRS[h])
                            bw = vRC[h - 1] if h >= 1 else bca_in
                            if bw:
                                wV(e, k, bw)
                            e.matmul(
                                bk_bc[0:64, 0:512],
                                cst["ones_rf"][0:1, 0:64],
                                lr[:, (h % 2) * 512:(h % 2 + 1) * 512],
                                start=True, stop=True).then_inc(k.P.h, 1)

                @b.scalar
                def _(e, pQK=pQK, ex_w=ex_w, vLC=vLC):
                    for h in range(16):
                        for j in range(16):
                            e.wait_ge(k.A.h, k.A.n)
                            e.wait_ge(k.P.h, pQK[(h, j)])
                            pw = ex_w[(h, j)]
                            if pw:
                                wP(e, k, pw)
                            k.A.inc(e.activation(
                                ptile[:, (j % 3) * 512:(j % 3 + 1) * 512],
                                bk_sc[j % 3][0:128, 0:512], AF.Exp,
                                bias=negm[:], scale=0.125))
                        lrs = lr[:, (h % 2) * 512:(h % 2 + 1) * 512]
                        e.wait_ge(k.A.h, k.A.n)
                        wV(e, k, vLC[h])
                        k.A.inc(e.activation(lrs, lrs, AF.Ln))
                        e.wait_ge(k.A.h, k.A.n)
                        k.A.inc(e.activation(lrs, lrs, AF.Exp, scale=-1.0))

                @b.vector
                def _(e, pPV=pPV, pBC=pBC, aRL=aRL, lrw_in=lrw_in,
                      atf=attnT_free_p):
                    for h in range(16):
                        hc, hp, hs = h // 2, (h % 2) * 64, h % 2
                        lrs = lr[:, hs * 512:(hs + 1) * 512]
                        rlbs = rlb[:, hs * 512:(hs + 1) * 512]
                        e.wait_ge(k.V.h, k.V.n)
                        e.wait_ge(k.P.h, pPV[(h, 15)])
                        lw = pBC[h - 2] if h >= 2 else lrw_in[hs]
                        if lw:
                            wP(e, k, lw)
                        k.V.inc(e.tensor_copy(lrs, bk_pv[hs][64:65, 0:512]))
                        e.wait_ge(k.V.h, k.V.n)
                        wA(e, k, aRL[h])
                        k.V.inc(e.tensor_scalar(lrs, lrs, wrow[:, 1:2],
                                                None, ALU.mult))
                        e.wait_ge(k.V.h, k.V.n)
                        e.wait_ge(k.P.h, pBC[h])
                        k.V.inc(e.tensor_copy(rlbs, bk_bc[0:64, 0:512]))
                        e.wait_ge(k.V.h, k.V.n)
                        if h == 0 and atf:
                            wP(e, k, atf)
                        k.V.inc(e.tensor_mul(
                            attnT[hp:hp + 64, hc * 512:(hc + 1) * 512],
                            bk_pv[hs][0:64, 0:512], rlbs))

                k.P.n = pc
                assert k.A.n == ac and k.V.n == vc, (k.A.n, ac, k.V.n, vc)
                sc_free[:] = scf_l
                pt_free_p[:] = ptw_l
                pv_free_v[0] = vAM[14]
                pv_free_v[1] = vAM[15]
                lr_free_w[0] = pBC[14]
                lr_free_w[1] = pBC[15]
                bc_free_a = vRC[15]
                v_adot = k.V.n

                for g in range(2):
                    for c4 in range(4):
                        @b.vector
                        def _(e, g=g, c4=c4, sg=sg, srp=sig_rd_p):
                            wV(e, k)
                            if srp and c4 == 0:
                                wP(e, k, srp)
                            qs = qT[:, (4 * g + c4) * SB + sg * 512:
                                    (4 * g + c4) * SB + (sg + 1) * 512]
                            k.V.inc(e.tensor_scalar(scrA[:], qs, 0.0, None,
                                                    ALU.min))

                        @b.scalar
                        def _(e):
                            wV(e, k)
                            k.A.inc(e.activation(scrA[:], scrA[:], AF.Exp))

                        @b.vector
                        def _(e, g=g, c4=c4, sg=sg):
                            wA(e, k)
                            qs = qT[:, (4 * g + c4) * SB + sg * 512:
                                    (4 * g + c4) * SB + (sg + 1) * 512]
                            k.V.inc(e.tensor_scalar(scrB[:], qs, 0.0, None,
                                                    ALU.max))
                            wV(e, k)
                            k.V.inc(e.tensor_scalar(scrA[:], scrA[:], -1.0,
                                                    None, ALU.add))
                            wV(e, k)
                            k.V.inc(e.tensor_add(
                                sigc[:, c4 * 512:(c4 + 1) * 512],
                                scrA[:], scrB[:]))
                    v_sig = k.V.n

                    @b.tensor
                    def _(e, v_sig=v_sig):
                        wV(e, k, v_sig)
                        for c4 in range(4):
                            k.P.inc(e.matmul(
                                bk_bc[0:1, 0:512], memnb[:, c4:c4 + 1],
                                sigc[:, c4 * 512:(c4 + 1) * 512],
                                start=(c4 == 0), stop=(c4 == 3)))
                    p_den = k.P.n

                    @b.vector
                    def _(e, p_den=p_den):
                        wP(e, k, p_den)
                        k.V.inc(e.tensor_copy(dr[:], bk_bc[0:1, 0:512]))
                        wV(e, k)
                        k.V.inc(e.reciprocal(dr[:], dr[:]))
                        wV(e, k)
                        k.V.inc(e.tensor_scalar(dr[:], dr[:], wrow[:, 0:1],
                                                None, ALU.mult))
                    v_dr = k.V.n

                    @b.tensor
                    def _(e, v_dr=v_dr):
                        wV(e, k, v_dr)
                        k.P.inc(e.matmul(bk_bc[0:128, 0:512],
                                         cst["ones_rf"][0:1, 0:128], dr[:],
                                         start=True, stop=True))
                    p_rd = k.P.n

                    @b.vector
                    def _(e, p_rd=p_rd):
                        wP(e, k, p_rd)
                        k.V.inc(e.tensor_copy(rdb[:], bk_bc[0:128, 0:512]))
                    v_rdb = k.V.n

                    for kvc_i in range(4):
                        bkn = bk_nm[kvc_i % 2]

                        @b.tensor
                        def _(e, g=g, kvc_i=kvc_i, bkn=bkn, v_rdb=v_rdb,
                              nmf=sc_free[kvc_i % 2]):
                            if kvc_i == 0:
                                wV(e, k, v_rdb)
                            wT(e, k, nmf)
                            for c4 in range(4):
                                k.P.inc(e.matmul(
                                    bkn[:, 0:512],
                                    memb[:, c4 * KV + kvc_i * 128:
                                         c4 * KV + (kvc_i + 1) * 128],
                                    sigc[:, c4 * 512:(c4 + 1) * 512],
                                    start=(c4 == 0), stop=(c4 == 3)))
                        p_num = k.P.n
                        if kvc_i == 3:
                            sig_rd_p = p_num

                        @b.vector
                        def _(e, g=g, kvc_i=kvc_i, bkn=bkn, p_num=p_num,
                              v_adot=v_adot):
                            wP(e, k, p_num)
                            wV(e, k)
                            k.V.inc(e.tensor_mul(scrB[:], bkn[:, 0:512],
                                                 rdb[:]))
                            wV(e, k)
                            ac = attnT[:, (4 * g + kvc_i) * 512:
                                       (4 * g + kvc_i + 1) * 512]
                            k.V.inc(e.tensor_add(ac, ac, scrB[:]))
                        sc_free[kvc_i % 2] = ("V", k.V.n)
                v_mix = k.V.n

                trc_free_v = [0, 0]
                for j in range(4):
                    for c in range(NC_D):
                        @b.tensor
                        def _(e, j=j, c=c, v_mix=v_mix,
                              tfv=trc_free_v[c % 2], ffv=fout_free_v):
                            if c == 0 and j == 0:
                                wV(e, k, v_mix)
                            if c == 0 and ffv:
                                wV(e, k, ffv)
                            if tfv:
                                wV(e, k, tfv)
                            k.P.inc(e.transpose(
                                bk_tr[c % 2][:, 0:128],
                                attnT[:, c * 512 + j * 128:
                                      c * 512 + (j + 1) * 128],
                                cst["ident"][:]))
                        p_tr = k.P.n

                        @b.vector
                        def _(e, c=c, p_tr=p_tr):
                            wP(e, k, p_tr)
                            wV(e, k)
                            k.V.inc(e.tensor_copy(
                                fout[:, c * 128:(c + 1) * 128],
                                bk_tr[c % 2][:, 0:128]))
                        trc_free_v[c % 2] = k.V.n
                    v_f = k.V.n

                    i_tile = bi * 16 + sg * 4 + j
                    epi.tile(b, cst, i_tile, fout[:, 0:D],
                             lambda e, v_f=v_f: wV(e, k, v_f))
                    fout_free_v = k.V.n
                attnT_free_p = k.P.n

        epi.finish(b)

    k.close()
    return nc



def build_L3a():
    k = K("l3a")
    nc = k.nc
    SB = S
    NCH = SB // TT
    x_in = k.din("x", [T, D])
    wqT = k.din("wqT", [D, D])
    wkT = k.din("wkT", [D, D])
    wvT = k.din("wvT", [D, D])
    woT = k.din("woT", [D, D])
    glr = k.din("glr", [1, 64])
    x_out = k.dout("x_new", [T, D])

    xlnT = k.sb("xlnT", [128, NC_D * SB], BF16)
    qT = k.sb("qT", [128, NC_D * SB], BF16)
    vT = k.sb("vT", [128, NC_D * SB], BF16)
    ktok = k.sb("ktok", [128, NCH * D], BF16)
    wb = k.sb("wb", [128, NC_D * D], BF16)
    yT = xlnT
    Sst = k.sb("Sst", [64, 16 * 64])
    Lbuf = k.sb("Lbuf", [128, NCH * 16])
    Cbuf = k.sb("Cbuf", [128, NCH * 16])
    sqk = k.sb("sqk", [128, D], BF16)
    kb = k.sb("kb", [128, 16])
    kb2 = k.sb("kb2", [128, 16])
    kb3 = k.sb("kb3", [128, 16])
    glb = k.sb("glb", [128, 64])
    glrow = k.sb("glrow", [1, 64])
    onesf = k.sb("onesf", [128, 128])
    trilf = k.sb("trilf", [128, 128])
    NLbuf = k.sb("NLbuf", [128, NCH * 16])
    bcL = k.sb("bcL", [128, 2 * 128])
    Am = k.sb("Am", [128, 2 * 128])
    Gm = k.sb("Gm", [128, 2 * 128])
    Bm = k.sb("Bm", [128, 2 * 128], BF16)
    qe = k.sb("qe", [64, 2 * 128])
    eL = k.sb("eL", [128, 2 * 128])
    vts = k.sb("vts", [128, 2 * 64], BF16)
    ck = k.sb("ck", [128, 2 * 64], BF16)
    cc2 = k.sb("cc2", [128, 2])
    fout = k.sb("fout", [128, D])

    bk_tr = [k.bankb("tr0"), k.bankb("tr1")]
    bk_bcT = k.bank("bcT")
    bk_G = k.bank("G")
    bk_y = k.bank("y")
    bk_S = k.bank("Sb")
    bk_bc = k.bank("bc")
    bk_p2 = k.bank("p2")

    with nc.Block() as b:
        cst = setup_consts(k, b, need_identf=True)
        s_rows = k.sem("s_rows")
        s_wldq = k.sem("wldq")

        @b.sync
        def _(e):
            s_rows.inc(e.dma_start(out=glrow[:], in_=glr), 16)
        sy_g = s_rows.n

        @b.vector
        def _(e):
            wV(e, k, cst["v_done"])
            k.V.inc(e.memset(onesf[:], 0.0))
            k.V.inc(e.tensor_scalar(trilf[:], cst["iot"][:], 0, None,
                                    ALU.is_ge))
        v_c3 = k.V.n

        @b.tensor
        def _(e):
            wV(e, k, v_c3)
            e.wait_ge(s_rows.h, sy_g)
            k.P.inc(e.matmul(bk_bc[0:128, 0:64], cst["ones_rf"][0:1, 0:128],
                             glrow[:], start=True, stop=True))
        p_gl = k.P.n

        @b.vector
        def _(e, p_gl=p_gl):
            wP(e, k, p_gl)
            k.V.inc(e.tensor_copy(glb[:], bk_bc[0:128, 0:64]))
        v_gl = k.V.n

        pro = Prologue(k, x_in, xlnT, bk_tr,
                       col_of=lambda c, i: c * SB + (i % 16) * TT)
        epi = Epilogue(k, x_in, x_out)

        xln_rd_p = 0
        fout_free_v = 0
        pj_free = [None, None]

        for bi in range(B_PER):
            for i in range(bi * 16, bi * 16 + 16):
                ew = (lambda e, n=xln_rd_p: wP(e, k, n)) if xln_rd_p else None
                pro.tile(b, cst, i, evac_wait=ew)
            ev_all = pro.ev_a[bi * 16 + 15]
            pro.tr_free_a = [k.V.n, k.V.n]

            def load_w(wsrc):
                @b.gpsimd
                def _(e, wsrc=wsrc):
                    wP(e, k)
                    s_wldq.inc(e.dma_start(
                        out=wb[:].rearrange("p (c n) -> p c n", c=NC_D),
                        in_=wsrc.rearrange("(c p) n -> p c n", p=128)), 16)
                return s_wldq.n

            for pi, (dst, wsrc) in enumerate(((qT, wqT), (vT, wvT))):
                ldw = load_w(wsrc)
                P0, V0 = k.P.n, k.V.n
                pjf0 = list(pj_free)

                @b.tensor
                def _(e, ldw=ldw, P0=P0, V0=V0, pjf0=pjf0, pi=pi):
                    if pi == 0:
                        wV(e, k, ev_all)
                    e.wait_ge(s_wldq.h, ldw)
                    for f in pjf0:
                        wT(e, k, f)
                    for ii in range(32):
                        ec, tc = ii // 4, ii % 4
                        bk2 = (bk_G, bk_y)[ii % 2]
                        if ii >= 2:
                            e.wait_ge(k.V.h, V0 + ii - 1)
                        for c in range(NC_D):
                            k.P.inc(e.matmul(
                                bk2[:, 0:512],
                                wb[:, c * D + ec * 128:c * D + (ec + 1) * 128],
                                xlnT[:, c * SB + tc * 512:
                                     c * SB + (tc + 1) * 512],
                                start=(c == 0), stop=(c == NC_D - 1)))

                @b.vector
                def _(e, P0=P0, dst=dst):
                    for ii in range(32):
                        ec, tc = ii // 4, ii % 4
                        bk2 = (bk_G, bk_y)[ii % 2]
                        e.wait_ge(k.P.h, P0 + 8 * (ii + 1))
                        e.wait_ge(k.V.h, k.V.n)
                        k.V.inc(e.tensor_copy(
                            dst[:, ec * SB + tc * 512:
                                ec * SB + (tc + 1) * 512],
                            bk2[:, 0:512]))
                pj_free[0] = ("V", k.V.n - 1)
                pj_free[1] = ("V", k.V.n)

            ldw = load_w(wkT)
            P0, V0 = k.P.n, k.V.n
            pjf0 = list(pj_free)

            @b.tensor
            def _(e, ldw=ldw, P0=P0, V0=V0, pjf0=pjf0):
                e.wait_ge(s_wldq.h, ldw)
                for f in pjf0:
                    wT(e, k, f)
                for ii in range(32):
                    tau, half = ii // 2, ii % 2
                    bk2 = (bk_G, bk_y)[ii % 2]
                    if ii >= 2:
                        e.wait_ge(k.V.h, V0 + ii - 1)
                    for c in range(NC_D):
                        k.P.inc(e.matmul(
                            bk2[:, 0:512],
                            xlnT[:, c * SB + tau * 128:
                                 c * SB + (tau + 1) * 128],
                            wb[:, c * D + half * 512:
                               c * D + (half + 1) * 512],
                            start=(c == 0), stop=(c == NC_D - 1)))

            @b.vector
            def _(e, P0=P0):
                for ii in range(32):
                    tau, half = ii // 2, ii % 2
                    bk2 = (bk_G, bk_y)[ii % 2]
                    e.wait_ge(k.P.h, P0 + 8 * (ii + 1))
                    e.wait_ge(k.V.h, k.V.n)
                    k.V.inc(e.tensor_copy(
                        ktok[:, tau * D + half * 512:
                             tau * D + (half + 1) * 512],
                        bk2[:, 0:512]))
            pj_free[0] = ("V", k.V.n - 1)
            pj_free[1] = ("V", k.V.n)
            xln_rd_p = k.P.n

            for tau in range(NCH):
                V0, A0, P0 = k.V.n, k.A.n, k.P.n

                @b.scalar
                def _(e, tau=tau, V0=V0):
                    e.wait_ge(k.V.h, V0)
                    e.wait_ge(k.A.h, k.A.n)
                    k.A.inc(e.activation(sqk[:],
                                         ktok[:, tau * D:(tau + 1) * D],
                                         AF.Square))
                    e.wait_ge(k.V.h, V0 + 32)
                    e.wait_ge(k.A.h, k.A.n)
                    k.A.inc(e.activation(kb[:], kb[:], AF.Sqrt))
                    e.wait_ge(k.V.h, V0 + 38)
                    e.wait_ge(k.A.h, k.A.n)
                    k.A.inc(e.activation(kb3[:], kb3[:], AF.Sigmoid))
                    e.wait_ge(k.V.h, V0 + 40)
                    e.wait_ge(k.A.h, k.A.n)
                    k.A.inc(e.activation(kb2[:], kb2[:], AF.Sigmoid))
                    e.wait_ge(k.V.h, V0 + 45)
                    e.wait_ge(k.A.h, k.A.n)
                    k.A.inc(e.activation(kb2[:], kb2[:], AF.Ln))

                @b.vector
                def _(e, tau=tau, A0=A0, P0=P0):
                    e.wait_ge(k.A.h, A0 + 1)
                    e.wait_ge(k.V.h, k.V.n)
                    for h in range(16):
                        k.V.inc(e.reduce_sum(kb[:, h:h + 1],
                                             sqk[:, h * 64:(h + 1) * 64],
                                             axis=AX.X))
                        k.V.inc(e.reduce_sum(
                            kb2[:, h:h + 1],
                            ktok[:, tau * D + h * 64:tau * D + (h + 1) * 64],
                            axis=AX.X))
                    e.wait_ge(k.A.h, A0 + 2)
                    e.wait_ge(k.V.h, k.V.n)
                    k.V.inc(e.tensor_scalar(kb[:], kb[:], 1e-12, None,
                                            ALU.max))
                    e.wait_ge(k.V.h, k.V.n)
                    k.V.inc(e.reciprocal(kb[:], kb[:]))
                    e.wait_ge(k.V.h, k.V.n)
                    k.V.inc(e.tensor_mul(kb2[:], kb2[:], kb[:]))
                    e.wait_ge(k.V.h, k.V.n)
                    k.V.inc(e.tensor_scalar(kb2[:], kb2[:], 1.0 / 64.0, None,
                                            ALU.mult))
                    e.wait_ge(k.V.h, k.V.n)
                    k.V.inc(e.tensor_mul(kb3[:], kb2[:], glb[:, 0:16]))
                    e.wait_ge(k.V.h, k.V.n)
                    k.V.inc(e.tensor_add(kb3[:], kb3[:], glb[:, 16:32]))
                    e.wait_ge(k.A.h, A0 + 3)
                    e.wait_ge(k.V.h, k.V.n)
                    k.V.inc(e.tensor_mul(kb2[:], kb2[:], glb[:, 32:48]))
                    e.wait_ge(k.V.h, k.V.n)
                    k.V.inc(e.tensor_add(kb2[:], kb2[:], glb[:, 48:64]))
                    e.wait_ge(k.A.h, A0 + 4)
                    e.wait_ge(k.V.h, k.V.n)
                    k.V.inc(e.tensor_scalar(kb2[:], kb2[:], -1.0, 1.0,
                                            ALU.mult, ALU.add))
                    e.wait_ge(k.V.h, k.V.n)
                    k.V.inc(e.tensor_mul(kb2[:], kb2[:], kb3[:]))
                    e.wait_ge(k.V.h, k.V.n)
                    k.V.inc(e.tensor_scalar(kb2[:], kb2[:], 1e-38, None,
                                            ALU.max))
                    e.wait_ge(k.V.h, k.V.n)
                    k.V.inc(e.tensor_scalar(kb3[:], kb2[:], -1.0, 1.0,
                                            ALU.mult, ALU.add))
                    e.wait_ge(k.V.h, k.V.n)
                    k.V.inc(e.tensor_mul(Cbuf[:, tau * 16:(tau + 1) * 16],
                                         kb3[:], kb[:]))
                    e.wait_ge(k.P.h, P0 + 1)
                    e.wait_ge(k.V.h, k.V.n)
                    k.V.inc(e.tensor_copy(Lbuf[:, tau * 16:(tau + 1) * 16],
                                          bk_p2[0:128, 0:16]))
                    e.wait_ge(k.V.h, k.V.n)
                    k.V.inc(e.tensor_scalar(
                        NLbuf[:, tau * 16:(tau + 1) * 16],
                        Lbuf[:, tau * 16:(tau + 1) * 16],
                        -1.0, None, ALU.mult))

                @b.tensor
                def _(e, A0=A0):
                    e.wait_ge(k.A.h, A0 + 5)
                    k.P.inc(e.matmul(bk_p2[0:128, 0:16], trilf[:], kb2[:],
                                     start=True, stop=True))
            v_gates = k.V.n

            ldwo = load_w(woT)

            @b.vector
            def _(e):
                wV(e, k, v_gates)
                k.V.inc(e.memset(Sst[:], 0.0))
            vInit = k.V.n

            NIT = NCH * 16

            def VQ(it):
                t_, h_ = it // 16, it % 16
                return (h_ % 2) * 64, (h_ // 2) * SB + t_ * 128

            def KTo(it):
                t_, h_ = it // 16, it % 16
                return t_ * D + h_ * 64

            a0s, p0s, v0s, g0s = k.A.n, k.P.n, k.V.n, k.G.n
            gBcl = {}; aAm = {}; aEl = {}
            pBct = {}; pG = {}; pVt = {}; pYs = {}; pSadd = {}
            vGm = {}; vQe = {}; vVts = {}; vYev = {}; vSupd = {}
            an, pn, vn, gn = a0s, p0s, v0s, g0s
            gn += 1; gBcl[0] = gn
            pn += 1; pBct[0] = pn
            pn += 1; pG[0] = pn
            for it in range(NIT):
                if it < NIT - 1:
                    gn += 1; gBcl[it + 1] = gn
                an += 1; aAm[it] = an
                an += 1; aEl[it] = an
                if it < NIT - 1:
                    pn += 1; pBct[it + 1] = pn
                    pn += 1; pG[it + 1] = pn
                pn += 1; pVt[it] = pn
                pn += 2; pYs[it] = pn
                pn += 1; pSadd[it] = pn
                vn += 1; vGm[it] = vn
                vn += 2; vQe[it] = vn
                vn += 2
                vn += 1; vVts[it] = vn
                vn += 1; vYev[it] = vn
                vn += 1; vSupd[it] = vn

            bcT_bk = (bk_bcT, bk_bc)
            G_bk = (bk_G, bk_p2)

            for tau in range(NCH):
                its = list(range(tau * 16, tau * 16 + 16))

                @b.gpsimd
                def _(e, tau=tau, its=its, vInit=vInit):
                    if tau == 0:
                        e.wait_ge(k.V.h, vInit)
                        e.wait_ge(k.G.h, k.G.n)
                        k.G.inc(e.tensor_scalar(bcL[:, 0:128], onesf[:],
                                                Lbuf[:, 0:1], None, ALU.add))
                    for it in its:
                        sl = (it + 1) % 2
                        if it < NIT - 1:
                            e.wait_ge(k.G.h, k.G.n)
                            if it >= 1:
                                e.wait_ge(k.P.h, pBct[it - 1])
                            k.G.inc(e.tensor_scalar(
                                bcL[:, sl * 128:(sl + 1) * 128], onesf[:],
                                Lbuf[:, it + 1:it + 2], None, ALU.add))

                @b.scalar
                def _(e, tau=tau, its=its, vInit=vInit):
                    if tau == 0:
                        e.wait_ge(k.V.h, vInit)
                    for it in its:
                        s0 = it % 2
                        e.wait_ge(k.A.h, k.A.n)
                        e.wait_ge(k.P.h, pBct[it])
                        if it >= 2:
                            e.wait_ge(k.V.h, vSupd[it - 2])
                        k.A.inc(e.activation(
                            Am[:, s0 * 128:(s0 + 1) * 128],
                            bcT_bk[s0][0:128, 0:128], AF.Exp,
                            bias=NLbuf[:, it:it + 1]))
                        e.wait_ge(k.A.h, k.A.n)
                        k.A.inc(e.activation(
                            eL[:, s0 * 128:(s0 + 1) * 128],
                            bcT_bk[s0][0:128, 0:128], AF.Exp))

                @b.tensor
                def _(e, tau=tau, its=its, vInit=vInit):
                    if tau == 0:
                        e.wait_ge(k.V.h, vInit)
                        e.wait_ge(k.G.h, gBcl[0])
                        k.P.inc(e.transpose(bcT_bk[0][:, 0:128],
                                            bcL[:, 0:128],
                                            cst["identf"][:]))
                        hp0, c0 = VQ(0)
                        k.P.inc(e.matmul(
                            G_bk[0][0:128, 0:128],
                            vT[hp0:hp0 + 64, c0:c0 + 128],
                            qT[hp0:hp0 + 64, c0:c0 + 128],
                            start=True, stop=True))
                    for it in its:
                        t_, h_ = it // 16, it % 16
                        s0 = it % 2
                        if it < NIT - 1:
                            s1 = (it + 1) % 2
                            e.wait_ge(k.G.h, gBcl[it + 1])
                            if it >= 1:
                                e.wait_ge(k.A.h, aEl[it - 1])
                            k.P.inc(e.transpose(
                                bcT_bk[s1][:, 0:128],
                                bcL[:, s1 * 128:(s1 + 1) * 128],
                                cst["identf"][:]))
                            hp1, c1 = VQ(it + 1)
                            if it >= 1:
                                e.wait_ge(k.V.h, vGm[it - 1])
                            k.P.inc(e.matmul(
                                G_bk[s1][0:128, 0:128],
                                vT[hp1:hp1 + 64, c1:c1 + 128],
                                qT[hp1:hp1 + 64, c1:c1 + 128],
                                start=True, stop=True))
                        hp, cc = VQ(it)
                        if it >= 1:
                            e.wait_ge(k.V.h, vVts[it - 1])
                        k.P.inc(e.transpose(
                            bk_tr[0][0:128, 0:64],
                            vT[hp:hp + 64, cc:cc + 128],
                            cst["ident"][hp:hp + 64, hp:hp + 64]))
                        e.wait_ge(k.V.h, vQe[it])
                        k.P.inc(e.matmul(
                            bk_y[0:64, 0:128],
                            Sst[:, h_ * 64:(h_ + 1) * 64],
                            qe[0:64, s0 * 128:s0 * 128 + 128],
                            start=True, stop=False))
                        k.P.inc(e.matmul(
                            bk_y[0:64, 0:128],
                            ktok[:, KTo(it):KTo(it) + 64],
                            Bm[:, s0 * 128:s0 * 128 + 128],
                            start=False, stop=True))
                        e.wait_ge(k.V.h, vVts[it])
                        k.P.inc(e.matmul(
                            bk_S[0:64, 0:64],
                            vts[:, s0 * 64:s0 * 64 + 64],
                            ck[:, s0 * 64:s0 * 64 + 64],
                            start=True, stop=True))

                @b.vector
                def _(e, tau=tau, its=its):
                    for it in its:
                        t_, h_ = it // 16, it % 16
                        s0 = it % 2
                        hp, cc = VQ(it)
                        e.wait_ge(k.V.h, k.V.n)
                        e.wait_ge(k.P.h, pG[it])
                        k.V.inc(e.scalar_tensor_tensor(
                            Gm[:, s0 * 128:(s0 + 1) * 128],
                            G_bk[s0][0:128, 0:128],
                            Cbuf[:, it:it + 1], trilf[:],
                            ALU.mult, ALU.mult))
                        e.wait_ge(k.V.h, k.V.n)
                        e.wait_ge(k.A.h, aAm[it])
                        k.V.inc(e.scalar_tensor_tensor(
                            Bm[:, s0 * 128:(s0 + 1) * 128],
                            Am[:, s0 * 128:(s0 + 1) * 128], 1.0,
                            Gm[:, s0 * 128:(s0 + 1) * 128],
                            ALU.min, ALU.mult))
                        e.wait_ge(k.V.h, k.V.n)
                        e.wait_ge(k.A.h, aEl[it])
                        k.V.inc(e.tensor_mul(
                            qe[0:64, s0 * 128:s0 * 128 + 128],
                            qT[hp:hp + 64, cc:cc + 128],
                            eL[hp:hp + 64, s0 * 128:s0 * 128 + 128]))
                        e.wait_ge(k.V.h, k.V.n)
                        k.V.inc(e.tensor_mul(
                            cc2[:, s0:s0 + 1],
                            Am[:, s0 * 128 + 127:s0 * 128 + 128],
                            Cbuf[:, it:it + 1]))
                        e.wait_ge(k.V.h, k.V.n)
                        k.V.inc(e.tensor_scalar(
                            ck[:, s0 * 64:s0 * 64 + 64],
                            ktok[:, KTo(it):KTo(it) + 64],
                            cc2[:, s0:s0 + 1], None, ALU.mult))
                        e.wait_ge(k.V.h, k.V.n)
                        e.wait_ge(k.P.h, pVt[it])
                        k.V.inc(e.tensor_copy(vts[:, s0 * 64:s0 * 64 + 64],
                                              bk_tr[0][0:128, 0:64]))
                        e.wait_ge(k.V.h, k.V.n)
                        e.wait_ge(k.P.h, pYs[it])
                        k.V.inc(e.tensor_copy(
                            yT[hp:hp + 64, cc:cc + 128],
                            bk_y[0:64, 0:128]))
                        e.wait_ge(k.V.h, k.V.n)
                        e.wait_ge(k.P.h, pSadd[it])
                        k.V.inc(e.scalar_tensor_tensor(
                            Sst[:, h_ * 64:(h_ + 1) * 64],
                            Sst[:, h_ * 64:(h_ + 1) * 64],
                            eL[0:64, s0 * 128 + 127:s0 * 128 + 128],
                            bk_S[0:64, 0:64],
                            ALU.mult, ALU.add))
            assert (k.A.n, k.P.n, k.V.n, k.G.n) == (an, pn, vn, gn), \
                ((k.A.n, k.P.n, k.V.n, k.G.n), (an, pn, vn, gn))
            v_scan = k.V.n

            for tc in range(4):
                for j in range(4):
                    @b.tensor
                    def _(e, tc=tc, j=j, first=(tc == 0 and j == 0),
                          ofv=fout_free_v, ldwo=ldwo):
                        if first:
                            wV(e, k)
                            e.wait_ge(s_wldq.h, ldwo)
                        if ofv:
                            wV(e, k, ofv)
                        for half in range(2):
                            bko = (bk_G, bk_y)[half]
                            for c in range(NC_D):
                                k.P.inc(e.matmul(
                                    bko[:, 0:512],
                                    yT[:, c * SB + tc * 512 + j * 128:
                                       c * SB + tc * 512 + (j + 1) * 128],
                                    wb[:, c * D + half * 512:
                                       c * D + (half + 1) * 512],
                                    start=(c == 0), stop=(c == NC_D - 1)))
                    p_op = k.P.n

                    @b.vector
                    def _(e, p_op=p_op, ffv=fout_free_v):
                        wP(e, k, p_op)
                        if ffv:
                            wV(e, k, ffv)
                        k.V.inc(e.tensor_copy(fout[:, 0:512],
                                              bk_G[:, 0:512]))
                        k.V.inc(e.tensor_copy(fout[:, 512:1024],
                                              bk_y[:, 0:512]))
                    v_f = k.V.n

                    i_tile = bi * 16 + tc * 4 + j
                    epi.tile(b, cst, i_tile, fout[:, 0:D],
                             lambda e, v_f=v_f: wV(e, k, v_f))
                    fout_free_v = k.V.n
            yT_rd_p = k.P.n
            xln_rd_p = k.P.n

        epi.finish(b)

    k.close()
    return nc



def np_shard_x(x):
    xs = np.ascontiguousarray(np.asarray(x).reshape(16, S, D))
    return [np.ascontiguousarray(xs[2 * c:2 * c + 2].reshape(T, D))
            for c in range(NCORES)]


def run_L3f(x_full, tw, bias, ascale):
    nc = build_L3f()
    shards = np_shard_x(x_full)
    twT = np.ascontiguousarray(np.asarray(tw).T)
    b1 = np.ascontiguousarray(np.asarray(bias).reshape(1, D))
    asc = np.ascontiguousarray(np.asarray(ascale).reshape(1, 1))
    in_maps = [{"x": s, "twT": twT, "bias": b1, "ascale": asc} for s in shards]
    res = run_bass_kernel_spmd(nc, in_maps, core_ids=list(range(NCORES)))
    return np.concatenate([r["x_new"].reshape(B_PER, S, D)
                           for r in res.results], axis=0)


def _run(nc, in_maps):
    return run_bass_kernel_spmd(nc, in_maps, core_ids=list(range(NCORES)))


def kernel(**inputs):
    cc = np.ascontiguousarray
    g = {n: np.asarray(v) for n, v in inputs.items()}
    xs = np_shard_x(g["L0_attn_x"])

    com = {
        "wq": cc(g["L0_attn_wq"].reshape(D, D)),
        "wk": cc(g["L0_attn_wk"].reshape(D, D)),
        "wv": cc(g["L0_attn_wv"].reshape(D, D)),
        "wo": cc(g["L0_attn_wo"].reshape(D, D)),
        "bqc": cc(g["L0_attn_bq"].reshape(-1).reshape(8, 128).T),
        "bkc": cc(g["L0_attn_bk"].reshape(-1).reshape(8, 128).T),
        "bvc": cc(g["L0_attn_bv"].reshape(-1).reshape(8, 128).T),
        "bor": cc(g["L0_attn_bo"].reshape(1, D)),
    }
    res = _run(build_L0a(), [dict(com, x=s) for s in xs])
    xs = [np.asarray(r["x_new"]) for r in res.results]

    res = _run(build_maxln(), [{"x": s} for s in xs])
    pmax8 = cc(np.concatenate([np.asarray(r["pmax"]).reshape(1, 1)
                               for r in res.results], axis=1))

    com = {
        "pmax8": pmax8,
        "w1": cc(g["L0_ffn_w1"]),
        "b1c": cc(g["L0_ffn_b1"].reshape(32, 128).T),
    }
    res = _run(build_L0f1(), [dict(com, x=s) for s in xs])
    hs = [np.asarray(r["h_out"]) for r in res.results]
    pmaxh8 = cc(np.concatenate([np.asarray(r["pmaxh"]).reshape(1, 1)
                                for r in res.results], axis=1))

    com = {
        "pmaxh8": pmaxh8,
        "w2": cc(g["L0_ffn_w2"]),
        "b2c": cc(g["L0_ffn_b2"].reshape(8, 128).T),
    }
    res = _run(build_L0f2(), [dict(com, x=s, h_in=h) for s, h in zip(xs, hs)])
    xs = [np.asarray(r["x_new"]) for r in res.results]
    del hs

    com = {
        "wqT": cc(g["L1_attn_Wq"].T), "wkT": cc(g["L1_attn_Wk"].T),
        "wvT": cc(g["L1_attn_Wv"].T), "woT": cc(g["L1_attn_Wo"].T),
        "bqc": cc(g["L1_attn_bq"].reshape(8, 128).T),
        "bkc": cc(g["L1_attn_bk"].reshape(8, 128).T),
        "bvr": cc(g["L1_attn_bv"].reshape(1, D)),
        "bor": cc(g["L1_attn_bo"].reshape(1, D)),
    }
    res = _run(build_L1a(), [dict(com, x=s) for s in xs])
    xs = [np.asarray(r["x_new"]) for r in res.results]

    com = {
        "wfcT": cc(g["L1_ffn_W_fc"].T), "wprT": cc(g["L1_ffn_W_proj"].T),
        "afcT": cc(g["L1_ffn_A_fc"].T), "bfcT": cc(g["L1_ffn_B_fc"].T),
        "aprT": cc(g["L1_ffn_A_proj"].T), "bprT": cc(g["L1_ffn_B_proj"].T),
        "bfc_pc": cc(g["L1_ffn_b_fc"].reshape(32, 128).T),
        "bpr": cc(g["L1_ffn_b_proj"].reshape(1, D)),
    }
    res = _run(build_L1f(), [dict(com, x=s) for s in xs])
    xs = [np.asarray(r["x_new"]) for r in res.results]

    wkr = cc(np.repeat(g["L2_attn_Wk"].reshape(D, 8, 64), 2,
                       axis=1).reshape(D, D))
    wvr = cc(np.repeat(g["L2_attn_Wv"].reshape(D, 8, 64), 2,
                       axis=1).reshape(D, D))
    com = {
        "wq2": cc(g["L2_attn_Wq"]), "wkr": wkr, "wvr": wvr,
        "memi": cc(g["L2_attn_memory"]),
        "memn": cc(g["L2_attn_memory_norm"].reshape(4, 128).T),
        "memw": cc(g["L2_attn_memory_weight"].reshape(1, 1)),
    }
    res = _run(build_L2a(), [dict(com, x=s) for s in xs])
    xs = [np.asarray(r["x_new"]) for r in res.results]

    com = {"wupT": cc(g["L2_ffn_w_up"].T), "wdnT": cc(g["L2_ffn_w_down"].T)}
    res = _run(build_L2f(), [dict(com, x=s) for s in xs])
    xs = [np.asarray(r["x_new"]) for r in res.results]

    com = {
        "wqT": cc(g["L3_attn_Wq"].T), "wkT": cc(g["L3_attn_Wk"].T),
        "wvT": cc(g["L3_attn_Wv"].T), "woT": cc(g["L3_attn_Wo"].T),
        "glr": cc(np.concatenate([
            g["L3_attn_W_gamma"], g["L3_attn_b_gamma"],
            g["L3_attn_W_lambda"], g["L3_attn_b_lambda"]]).reshape(1, 64)),
    }
    res = _run(build_L3a(), [dict(com, x=s) for s in xs])
    xs = [np.asarray(r["x_new"]) for r in res.results]

    com = {
        "twT": cc(g["L3_ffn_ternary_weight"].T),
        "bias": cc(g["L3_ffn_bias"].reshape(1, D)),
        "ascale": cc(g["L3_ffn_act_scale"].reshape(1, 1).astype(np.float32)),
    }
    res = _run(build_L3f(), [dict(com, x=s) for s in xs])
    xs = [np.asarray(r["x_new"]) for r in res.results]

    return np.concatenate([s.reshape(B_PER, S, D) for s in xs],
                          axis=0)
